# version 12
# baseline (speedup 1.0000x reference)
import numpy as np
import ml_dtypes
import concourse.bass as bass
import concourse.mybir as mybir
from concourse.bass_utils import run_bass_kernel_spmd

F32 = mybir.dt.float32
BF16 = mybir.dt.bfloat16
AF = mybir.ActivationFunctionType
ALU = mybir.AluOpType
AX = mybir.AxisListType

T = 2048
C = 1024
NCT = 8
H = 16
D = 64
M = 64
CH = 128
NCH = T // CH
NP = 8
E = 16
EH = 256
HS = 2048
NF = 3968
GC1 = 0.7978845608028654
GC2 = 0.044715

STAGE_ALL = 99


import os
SERIAL = bool(int(os.environ.get("BK_SERIAL", "0")))


class Prog:
    CENG = ("tensor", "scalar", "vector", "gpsimd", "sync")
    QUEUES = ("sync", "scalar", "gpsimd")

    def __init__(self, nc):
        self.nc = nc
        self.ops = []
        self.count = {e: 0 for e in self.CENG}
        self.dma_count = {q: 0 for q in self.QUEUES}
        self.last_write = {}
        self.readers = {}
        self.prev_me = None

    def _collect(self, reads, writes):
        deps = set()
        rkeys, wkeys = [], []
        for r in reads:
            for k in (r if isinstance(r, list) else [r]):
                k = k if isinstance(k, tuple) else (id(getattr(k, "tensor", k)), None)
                rkeys.append(k)
                lw = self.last_write.get(k)
                if lw is not None:
                    deps.add(lw)
        for w in writes:
            for k in (w if isinstance(w, list) else [w]):
                k = k if isinstance(k, tuple) else (id(getattr(k, "tensor", k)), None)
                wkeys.append(k)
                lw = self.last_write.get(k)
                if lw is not None:
                    deps.add(lw)
                for rd in self.readers.get(k, ()):
                    deps.add(rd)
        return deps, rkeys, wkeys

    def _book(self, me, rkeys, wkeys):
        for k in rkeys:
            self.readers.setdefault(k, []).append(me)
        for k in wkeys:
            self.last_write[k] = me
            self.readers[k] = []

    def op(self, engine, emit, reads=(), writes=(), tag=""):
        deps, rkeys, wkeys = self._collect(reads, writes)
        me = ("c", engine, self.count[engine])
        self.count[engine] += 1
        if SERIAL and self.prev_me is not None:
            deps = {self.prev_me}
        self.prev_me = me
        self.ops.append(dict(engine=engine, emit=emit, deps=deps, me=me, tag=tag, dma=False))
        self._book(me, rkeys, wkeys)
        return me

    NLANES = 8

    def dma(self, queue, out, in_, reads=(), writes=(), tag="", **kw):
        deps, rkeys, wkeys = self._collect(reads, writes)
        me = ("d", queue, self.dma_count[queue])
        self.dma_count[queue] += 1
        if SERIAL and self.prev_me is not None:
            deps = {self.prev_me}
        self.prev_me = me
        emit = lambda eng, out=out, in_=in_, kw=kw: eng.dma_start(out=out, in_=in_, **kw)
        self.ops.append(dict(engine=queue, emit=emit, deps=deps, me=me, tag=tag, dma=True))
        self._book(me, rkeys, wkeys)
        return me

    def emit_all(self, block, sems, qsems):
        per_engine = {e: [] for e in self.CENG}
        for o in self.ops:
            per_engine[o["engine"]].append(o)
        NL = self.NLANES

        def lane_of(dep):
            q, i = dep[1], dep[2]
            return qsems[q][i % NL], 16 * (i // NL + 1)

        def mk_section(ename, ops_here):
            def fn(eng):
                hw = {}
                ndma = 0
                for o in ops_here:
                    if o["dma"] and ndma >= NL:
                        sem, val = qsems[o["engine"]][ndma % NL], 16 * (ndma // NL)
                        if hw.get(id(sem), -1) < val:
                            hw[id(sem)] = val
                            eng.wait_ge(sem, val)
                    for dep in sorted(o["deps"], key=repr):
                        if dep[0] == "c":
                            dep_eng, dep_seq = dep[1], dep[2]
                            if dep_eng == ename == "tensor" and not SERIAL:
                                continue
                            sem, val = sems[dep_eng], dep_seq + 1
                        else:
                            sem, val = lane_of(dep)
                        if hw.get(id(sem), -1) >= val:
                            continue
                        hw[id(sem)] = val
                        eng.wait_ge(sem, val)
                    ins = o["emit"](eng)
                    if o["dma"]:
                        ins.then_inc(qsems[o["engine"]][ndma % NL], 16)
                        ndma += 1
                    else:
                        ins.then_inc(sems[o["engine"]], 1)
                if ename == "gpsimd":
                    for q in self.QUEUES:
                        n = self.dma_count[q]
                        for l in range(NL):
                            cnt = 16 * ((n - l + NL - 1) // NL)
                            if cnt:
                                eng.wait_ge(qsems[q][l], cnt)
            return fn

        for ename in self.CENG:
            if per_engine[ename] or ename == "gpsimd":
                getattr(block, ename)(mk_section(ename, per_engine[ename]))


def prep_consts():
    c = {}
    c["ident_bf"] = np.eye(128, dtype=np.float32).astype(ml_dtypes.bfloat16)
    c["ident_f32"] = np.eye(128, dtype=np.float32)
    ones = np.ones((128, 128), dtype=np.float32)
    c["ones_f32"] = ones
    c["ones_bf"] = ones.astype(ml_dtypes.bfloat16)
    tri = np.triu(np.ones((128, 128), dtype=np.float32))
    c["trimask_bf"] = np.concatenate([tri, tri], axis=1).astype(ml_dtypes.bfloat16)
    return c


def prep_weights(inp):
    w = {}
    w["WqkvT"] = np.ascontiguousarray(np.asarray(inp["L0_attn_Wqkv"]).T)
    w["bqkv"] = np.ascontiguousarray(np.asarray(inp["L0_attn_bqkv"]))
    w["WoutT"] = np.ascontiguousarray(np.asarray(inp["L0_attn_Wout"]).T)
    w["bout"] = np.ascontiguousarray(np.asarray(inp["L0_attn_bout"]))
    w["proj_pair"] = np.ascontiguousarray(np.asarray(inp["L0_attn_proj"]).reshape(NP, 2 * D, M))
    w["w_router"] = np.ascontiguousarray(np.asarray(inp["L0_ffn_w_router"]))
    w["w_block"] = np.ascontiguousarray(np.asarray(inp["L0_ffn_w_block"]))
    w["Wq"] = np.ascontiguousarray(np.asarray(inp["L1_attn_Wq"]))
    w["Wk"] = np.ascontiguousarray(np.asarray(inp["L1_attn_Wk"]))
    w["Wv"] = np.ascontiguousarray(np.asarray(inp["L1_attn_Wv"]))
    w["bq"] = np.ascontiguousarray(np.asarray(inp["L1_attn_bq"]))
    w["bk"] = np.ascontiguousarray(np.asarray(inp["L1_attn_bk"]))
    w["bv"] = np.ascontiguousarray(np.asarray(inp["L1_attn_bv"]))
    rel = np.asarray(inp["L1_attn_rel_emb"])
    stk = np.empty((H, 128, NF), dtype=np.float32)
    for h in range(H):
        r = rel[h]
        stk[h] = np.lib.stride_tricks.as_strided(
            r[127:], shape=(128, NF), strides=(-r.strides[0], r.strides[0]))
    w["rel_stack"] = stk
    w["router_wT"] = np.ascontiguousarray(np.asarray(inp["L1_ffn_router_w"]).T)
    w["corr_bias"] = np.ascontiguousarray(np.asarray(inp["L1_ffn_correction_bias"]))
    w["gate_w"] = np.ascontiguousarray(np.asarray(inp["L1_ffn_gate_w"]))
    w["up_w"] = np.ascontiguousarray(np.asarray(inp["L1_ffn_up_w"]))
    w["down_w"] = np.ascontiguousarray(np.asarray(inp["L1_ffn_down_w"]))
    w["sgwT"] = np.ascontiguousarray(np.asarray(inp["L1_ffn_shared_gate_w"]).T)
    w["suwT"] = np.ascontiguousarray(np.asarray(inp["L1_ffn_shared_up_w"]).T)
    w["sdwT"] = np.ascontiguousarray(np.asarray(inp["L1_ffn_shared_down_w"]).T)
    return w


def build_program(stage=STAGE_ALL, debug=False):
    nc = bass.Bass("TRN2", target_bir_lowering=False)
    P = Prog(nc)
    Hd = {}

    def din(name, shape, dtype=F32):
        Hd[name] = nc.dram_tensor(name, list(shape), dtype, kind="ExternalInput")
        return Hd[name]

    din("x", (T, C))
    din("WqkvT", (C, 3 * C)); din("bqkv", (3 * C,))
    din("WoutT", (C, C)); din("bout", (C,))
    din("proj_pair", (NP, 128, M))
    din("w_router", (C,)); din("w_block", (C, C))
    din("Wq", (C, C)); din("Wk", (C, C)); din("Wv", (C, C))
    din("bq", (C,)); din("bk", (C,)); din("bv", (C,))
    din("rel_stack", (H, 128, NF))
    din("router_wT", (C, 16)); din("corr_bias", (16,))
    din("gate_w", (E, C, EH)); din("up_w", (E, C, EH)); din("down_w", (E, EH, C))
    din("sgwT", (C, HS)); din("suwT", (C, HS)); din("sdwT", (HS, C))
    din("ident_bf", (128, 128), BF16); din("ident_f32", (128, 128))
    din("ones_f32", (128, 128)); din("ones_bf", (128, 128), BF16)
    din("trimask_bf", (128, 256), BF16)
    y = nc.dram_tensor("y", [T, C], F32, kind="ExternalOutput")

    A = nc.alloc_sbuf_tensor
    xT = A("xT", [128, NCT, T], F32)
    xnT = A("xnT", [128, NCT, T], BF16)
    big0 = A("big0", [128, NCT, T], BF16)
    ident_bf = A("c_idb", [128, 128], BF16)
    ident_f32 = A("c_idf", [128, 128], F32)
    ones_f32 = A("c_onf", [128, 128], F32)
    ones_bf = A("c_onb", [128, 128], BF16)
    trimask = A("c_tri", [128, 256], BF16)
    consts = {"ident_bf": ident_bf, "ident_f32": ident_f32, "ones_f32": ones_f32,
              "ones_bf": ones_bf, "trimask_bf": trimask}

    SCR_W = 16384
    scratch = A("scratch", [128, SCR_W], F32)
    scr_f32 = scratch.ap()
    scr_bf = scratch.ap().bitcast(BF16)
    PGW = 512

    def skeys(w0, w1):
        return [(id(scratch), pg) for pg in range(w0 // PGW, (w1 - 1) // PGW + 1)]

    class SV:
        def __init__(self, off_w, n_w, dtype):
            self.off_w, self.n_w, self.dtype = off_w, n_w, dtype
            self.keys = skeys(off_w, off_w + n_w)

        def ap(self, part=(0, 128)):
            if self.dtype == F32:
                a = scr_f32[:, self.off_w:self.off_w + self.n_w]
            else:
                a = scr_bf[:, 2 * self.off_w:2 * self.off_w + 2 * self.n_w]
            if part != (0, 128):
                a = scr_f32[part[0]:part[1], self.off_w:self.off_w + self.n_w] if self.dtype == F32 \
                    else scr_bf[part[0]:part[1], 2 * self.off_w:2 * self.off_w + 2 * self.n_w]
            return a

    def sv(off_w, n_w, dtype=F32):
        assert off_w + n_w <= SCR_W, (off_w, n_w)
        return SV(off_w, n_w, dtype)

    APm = nc.alloc_psum_tensor
    psA = APm("psA", [128, T], F32)
    psB = APm("psB", [128, T], F32)

    def lp(fn):
        def g(eng):
            with nc.allow_low_precision(reason="bf16 normalizer for broadcast matmul (blessed bf16 operand path)"):
                return fn(eng)
        return g

    def pk(pst, banks):
        return [(id(pst), b) for b in banks]

    def bank_of(col_lo, col_hi):
        return list(range(col_lo // 512, (col_hi - 1) // 512 + 1))

    rows = sv(0, 2 * T)
    m_b = sv(2 * T, T)
    rstd_b = sv(3 * T, T)
    logitsT = sv(4 * T, T)
    stg_a = sv(5 * T, T)
    stg_b = sv(6 * T, T)
    xbf_a = sv(7 * T, T // 2, BF16)
    xbf_b = sv(7 * T + T // 2, T // 2, BF16)

    class Arena:
        FULL = [(0, 4 * T), (5 * T, 8 * T)]
        LATE = [(5 * T, 8 * T)]

        def __init__(self, ranges="full"):
            self.RANGES = self.FULL if ranges == "full" else self.LATE
            self.ri = 0
            self.cur = self.RANGES[0][0]

        def sv(self, n_w, dtype=F32, align=16):
            self.cur = (self.cur + align - 1) // align * align
            while self.cur + n_w > self.RANGES[self.ri][1]:
                self.ri += 1
                assert self.ri < len(self.RANGES), f"arena overflow ({n_w} words)"
                self.cur = self.RANGES[self.ri][0]
            v = sv(self.cur, n_w, dtype)
            self.cur += n_w
            return v

    for nm, hh in consts.items():
        P.dma("sync", out=hh[:], in_=Hd[nm].ap(), reads=[Hd[nm]], writes=[hh], tag=f"ldc{nm}")

    xr = Hd["x"].ap().rearrange("(ti p) c -> ti p c", p=128)
    for ti in range(NCH):
        stgv = stg_a if ti % 2 == 0 else stg_b
        P.dma("sync", out=stgv.ap()[:, 0:C], in_=xr[ti],
              reads=[Hd["x"]], writes=[stgv.keys], tag=f"ldx{ti}")
        pst = psA if ti % 2 == 0 else psB
        for ci in range(NCT):
            bank = ci % 4
            col = bank * 512 + (ci // 4) * 128
            dst = pst[:, col:col + 128]
            P.op("tensor",
                 lambda eng, dst=dst, s=stgv, ci=ci: eng.transpose(
                     dst, s.ap()[:, ci * 128:(ci + 1) * 128], ident_f32[:]),
                 reads=[stgv.keys, ident_f32], writes=[pk(pst, [bank])], tag=f"tx{ti}_{ci}")
            P.op("vector",
                 lambda eng, dst=dst, ti=ti, ci=ci: eng.tensor_copy(
                     xT[:, ci, ti * 128:(ti + 1) * 128], dst),
                 reads=[pk(pst, [bank])], writes=[xT], tag=f"txe{ti}_{ci}")

    def layernorm(layer):
        m_row = rows.ap((0, 1))[:, 0:T]
        rstd_row = rows.ap((0, 1))[:, T:2 * T]
        s1, s2 = psA, psB
        for ci in range(NCT):
            sqv = stg_a if ci % 2 == 0 else stg_b
            P.op("scalar",
                 lambda eng, ci=ci, sqv=sqv: eng.square(sqv.ap(), xT[:, ci, :]),
                 reads=[xT], writes=[sqv.keys], tag=f"ln{layer}sq{ci}")
            for tb in range(4):
                P.op("tensor",
                     lambda eng, ci=ci, tb=tb: eng.matmul(
                         s1[0:1, tb * 512:(tb + 1) * 512], ones_f32[:, 0:1],
                         xT[:, ci, tb * 512:(tb + 1) * 512],
                         start=(ci == 0), stop=(ci == NCT - 1)),
                     reads=[xT, ones_f32], writes=[pk(s1, [tb])], tag=f"ln{layer}s1_{ci}_{tb}")
                P.op("tensor",
                     lambda eng, ci=ci, tb=tb, sqv=sqv: eng.matmul(
                         s2[0:1, tb * 512:(tb + 1) * 512], ones_f32[:, 0:1],
                         sqv.ap()[:, tb * 512:(tb + 1) * 512],
                         start=(ci == 0), stop=(ci == NCT - 1)),
                     reads=[sqv.keys, ones_f32], writes=[pk(s2, [tb])], tag=f"ln{layer}s2_{ci}_{tb}")
        var_row = stg_a.ap((0, 1))[:, 0:T]
        tmp_row = stg_b.ap((0, 1))[:, 0:T]
        P.op("vector", lambda eng: eng.tensor_scalar_mul(m_row, s1[0:1, :], 1.0 / C),
             reads=[pk(s1, range(4))], writes=[rows.keys], tag=f"ln{layer}m")
        P.op("vector", lambda eng: eng.tensor_scalar_mul(var_row, s2[0:1, :], 1.0 / C),
             reads=[pk(s2, range(4))], writes=[stg_a.keys], tag=f"ln{layer}ms")
        P.op("vector", lambda eng: eng.scalar_tensor_tensor(
            tmp_row, m_row, -1.0, m_row, ALU.mult, ALU.mult),
             reads=[rows.keys], writes=[stg_b.keys], tag=f"ln{layer}negm2")
        P.op("vector", lambda eng: eng.tensor_add(var_row, var_row, tmp_row),
             reads=[stg_a.keys, stg_b.keys], writes=[stg_a.keys], tag=f"ln{layer}var")
        P.op("vector", lambda eng: eng.tensor_scalar_add(var_row, var_row, 1e-5),
             reads=[stg_a.keys], writes=[stg_a.keys], tag=f"ln{layer}eps")
        P.op("scalar", lambda eng: eng.activation(rstd_row, var_row, AF.Ln),
             reads=[stg_a.keys], writes=[rows.keys], tag=f"ln{layer}ln")
        P.op("scalar", lambda eng: eng.activation(rstd_row, rstd_row, AF.Exp, scale=-0.5),
             reads=[rows.keys], writes=[rows.keys], tag=f"ln{layer}ex")
        P.op("vector", lambda eng: eng.tensor_mul(tmp_row, rstd_row, rstd_row),
             reads=[rows.keys], writes=[stg_b.keys], tag=f"ln{layer}n1")
        P.op("vector", lambda eng: eng.tensor_mul(tmp_row, tmp_row, var_row),
             reads=[stg_a.keys, stg_b.keys], writes=[stg_b.keys], tag=f"ln{layer}n2")
        P.op("vector", lambda eng: eng.tensor_scalar(
            tmp_row, tmp_row, -0.5, 1.5, ALU.mult, ALU.add),
             reads=[stg_b.keys], writes=[stg_b.keys], tag=f"ln{layer}n3")
        P.op("vector", lambda eng: eng.tensor_mul(rstd_row, rstd_row, tmp_row),
             reads=[rows.keys, stg_b.keys], writes=[rows.keys], tag=f"ln{layer}n4")
        for tb in range(4):
            P.op("tensor",
                 lambda eng, tb=tb: eng.matmul(
                     s1[:, tb * 512:(tb + 1) * 512], ones_f32[0:1, :],
                     m_row[:, tb * 512:(tb + 1) * 512], start=True, stop=True),
                 reads=[rows.keys, ones_f32], writes=[pk(s1, [tb])], tag=f"ln{layer}bm{tb}")
            P.op("tensor",
                 lambda eng, tb=tb: eng.matmul(
                     s2[:, tb * 512:(tb + 1) * 512], ones_f32[0:1, :],
                     rstd_row[:, tb * 512:(tb + 1) * 512], start=True, stop=True),
                 reads=[rows.keys, ones_f32], writes=[pk(s2, [tb])], tag=f"ln{layer}br{tb}")
        P.op("vector", lambda eng: eng.tensor_copy(m_b.ap(), s1[:, :]),
             reads=[pk(s1, range(4))], writes=[m_b.keys], tag=f"ln{layer}me")
        P.op("vector", lambda eng: eng.tensor_copy(rstd_b.ap(), s2[:, :]),
             reads=[pk(s2, range(4))], writes=[rstd_b.keys], tag=f"ln{layer}re")
        for ci in range(NCT):
            P.op("vector",
                 lambda eng, ci=ci: eng.tensor_sub(stg_b.ap(), xT[:, ci, :], m_b.ap()),
                 reads=[xT, m_b.keys], writes=[stg_b.keys], tag=f"ln{layer}sb{ci}")
            P.op("vector",
                 lambda eng, ci=ci: eng.tensor_mul(xnT[:, ci, :], stg_b.ap(), rstd_b.ap()),
                 reads=[stg_b.keys, rstd_b.keys], writes=[xnT], tag=f"ln{layer}xn{ci}")

    def router_l0(ar):
        wr = ar.sv(8)
        for j in range(NCT):
            P.dma("sync", out=wr.ap()[:, j:j + 1],
                  in_=Hd["w_router"].ap()[j * 128:(j + 1) * 128].unsqueeze(1),
                  reads=[Hd["w_router"]], writes=[wr.keys], tag=f"ldwr{j}")
        for ci in range(NCT):
            for tb in range(4):
                P.op("tensor",
                     lambda eng, ci=ci, tb=tb: eng.matmul(
                         psA[0:1, tb * 512:(tb + 1) * 512], wr.ap()[:, ci:ci + 1],
                         xT[:, ci, tb * 512:(tb + 1) * 512],
                         start=(ci == 0), stop=(ci == NCT - 1)),
                     reads=[wr.keys, xT], writes=[pk(psA, [tb])], tag=f"rs0_{ci}_{tb}")
        for ci in range(NCT):
            P.op("tensor",
                 lambda eng, ci=ci: eng.matmul(
                     psB[0:1, 0:1], wr.ap()[:, ci:ci + 1], ones_f32[:, 0:1],
                     start=(ci == 0), stop=(ci == NCT - 1)),
                 reads=[wr.keys, ones_f32], writes=[pk(psB, [0])], tag=f"rsw{ci}")
        sumw_sb = ar.sv(1)
        P.op("vector", lambda eng: eng.tensor_copy(sumw_sb.ap((0, 1)), psB[0:1, 0:1]),
             reads=[pk(psB, [0])], writes=[sumw_sb.keys], tag="rswe")
        sc = logitsT.ap((0, 1))[:, 0:T]
        m_row = rows.ap((0, 1))[:, 0:T]
        rstd_row = rows.ap((0, 1))[:, T:2 * T]
        P.op("vector", lambda eng: eng.tensor_single_scalar(
            sc, m_row, sumw_sb.ap((0, 1)), ALU.mult),
             reads=[rows.keys, sumw_sb.keys], writes=[logitsT.keys], tag="rt1")
        P.op("vector", lambda eng: eng.tensor_sub(sc, psA[0:1, :], sc),
             reads=[pk(psA, range(4)), logitsT.keys], writes=[logitsT.keys], tag="rt2")
        P.op("vector", lambda eng: eng.tensor_mul(sc, sc, rstd_row),
             reads=[logitsT.keys, rows.keys], writes=[logitsT.keys], tag="rt3")

    def l0_ffn(ar):
        sc_row = logitsT.ap((0, 1))[:, 0:T]
        scr_d = nc.dram_tensor("scr_row_d", [T], F32, kind="Internal")
        plane = ar.sv(16)
        cnt = ar.sv(16)
        maskpl = ar.sv(16)
        S_b = ar.sv(T)
        M_b = ar.sv(T)
        fs1 = ar.sv(T)
        fs2 = ar.sv(T)
        fs3 = ar.sv(T)
        wblk = [ar.sv(512, BF16), ar.sv(512, BF16)]
        P.dma("scalar", out=scr_d.ap().unsqueeze(0), in_=sc_row,
              reads=[logitsT.keys], writes=[scr_d], tag="ffn_scout")
        P.dma("scalar", out=plane.ap()[:, 0:16],
              in_=scr_d.ap().rearrange("(p j) -> p j", j=16),
              reads=[scr_d], writes=[plane.keys], tag="ffn_plane")
        for tb in range(4):
            P.op("tensor", lambda eng, tb=tb: eng.matmul(
                psA[:, tb * 512:(tb + 1) * 512], ones_f32[0:1, :],
                sc_row[:, tb * 512:(tb + 1) * 512], start=True, stop=True),
                 reads=[logitsT.keys, ones_f32], writes=[pk(psA, [tb])], tag=f"ffn_sb{tb}")
        P.op("vector", lambda eng: eng.tensor_copy(S_b.ap(), psA[:, :]),
             reads=[pk(psA, range(4))], writes=[S_b.keys], tag="ffn_sbe")
        for j in range(16):
            P.op("vector", lambda eng, j=j: eng.tensor_scalar(
                fs1.ap(), S_b.ap(), plane.ap()[:, j:j + 1], 0.0, ALU.is_gt, ALU.add,
                accum_out=cnt.ap()[:, j:j + 1]),
                 reads=[S_b.keys, plane.keys], writes=[fs1.keys, cnt.keys], tag=f"ffn_cnt{j}")
        KCAP = int(T * 0.75)
        P.op("vector", lambda eng: eng.tensor_single_scalar(
            maskpl.ap()[:, 0:16], cnt.ap()[:, 0:16], float(KCAP), ALU.is_lt),
             reads=[cnt.keys], writes=[maskpl.keys], tag="ffn_mask")
        P.dma("scalar", out=scr_d.ap().rearrange("(p j) -> p j", j=16),
              in_=maskpl.ap()[:, 0:16], reads=[maskpl.keys], writes=[scr_d], tag="ffn_mkout")
        P.dma("scalar", out=fs2.ap((0, 1))[:, 0:T], in_=scr_d.ap().unsqueeze(0),
              reads=[scr_d], writes=[fs2.keys], tag="ffn_mkin")
        for tb in range(4):
            P.op("tensor", lambda eng, tb=tb: eng.matmul(
                psA[:, tb * 512:(tb + 1) * 512], ones_f32[0:1, :],
                fs2.ap((0, 1))[:, tb * 512:(tb + 1) * 512], start=True, stop=True),
                 reads=[fs2.keys, ones_f32], writes=[pk(psA, [tb])], tag=f"ffn_mb{tb}")
        P.op("vector", lambda eng: eng.tensor_copy(M_b.ap(), psA[:, :]),
             reads=[pk(psA, range(4))], writes=[M_b.keys], tag="ffn_mbe")
        wblk_r = Hd["w_block"].ap().rearrange("(ci p) (eo q) -> eo p ci q", p=128, q=128)
        for eo in range(NCT):
            wb = wblk[eo % 2]
            P.dma("gpsimd", out=wb.ap().rearrange("p (ci q) -> p ci q", q=128),
                  in_=wblk_r[eo], reads=[Hd["w_block"]], writes=[wb.keys], tag=f"ffn_w{eo}")
            ps = psA if eo % 2 == 0 else psB
            for ci in range(NCT):
                for tb in range(4):
                    P.op("tensor", lambda eng, ps=ps, ci=ci, tb=tb, wb=wb: eng.matmul(
                        ps[:, tb * 512:(tb + 1) * 512],
                        wb.ap().rearrange("p (ci q) -> p ci q", q=128)[:, ci, :],
                        xnT[:, ci, tb * 512:(tb + 1) * 512],
                        start=(ci == 0), stop=(ci == NCT - 1)),
                         reads=[wb.keys, xnT], writes=[pk(ps, [tb])], tag=f"ffn_p{eo}_{ci}_{tb}")
            P.op("scalar", lambda eng, ps=ps: eng.square(fs1.ap(), ps[:, :]),
                 reads=[pk(ps, range(4))], writes=[fs1.keys], tag=f"ffn_g1_{eo}")
            P.op("vector", lambda eng, ps=ps: eng.scalar_tensor_tensor(
                fs2.ap(), fs1.ap(), GC1 * GC2, ps[:, :], ALU.mult, ALU.mult),
                 reads=[fs1.keys, pk(ps, range(4))], writes=[fs2.keys], tag=f"ffn_g2_{eo}")
            P.op("vector", lambda eng, ps=ps: eng.scalar_tensor_tensor(
                fs1.ap(), ps[:, :], GC1, fs2.ap(), ALU.mult, ALU.add),
                 reads=[fs2.keys, pk(ps, range(4))], writes=[fs1.keys], tag=f"ffn_g3_{eo}")
            P.op("scalar", lambda eng: eng.activation(fs2.ap(), fs1.ap(), AF.Tanh),
                 reads=[fs1.keys], writes=[fs2.keys], tag=f"ffn_g4_{eo}")
            P.op("vector", lambda eng: eng.tensor_scalar(
                fs1.ap(), fs2.ap(), 0.5, 0.5, ALU.mult, ALU.add),
                 reads=[fs2.keys], writes=[fs1.keys], tag=f"ffn_g5_{eo}")
            P.op("vector", lambda eng, ps=ps: eng.tensor_mul(fs2.ap(), fs1.ap(), ps[:, :]),
                 reads=[fs1.keys, pk(ps, range(4))], writes=[fs2.keys], tag=f"ffn_g6_{eo}")
            P.op("vector", lambda eng, eo=eo: eng.tensor_sub(fs3.ap(), fs2.ap(), xnT[:, eo, :]),
                 reads=[fs2.keys, xnT], writes=[fs3.keys], tag=f"ffn_e1_{eo}")
            P.op("vector", lambda eng: eng.tensor_mul(fs3.ap(), fs3.ap(), M_b.ap()),
                 reads=[fs3.keys, M_b.keys], writes=[fs3.keys], tag=f"ffn_e2_{eo}")
            P.op("vector", lambda eng, eo=eo: eng.tensor_add(xT[:, eo, :], xT[:, eo, :], fs3.ap()),
                 reads=[fs3.keys, xT], writes=[xT], tag=f"ffn_e3_{eo}")
            P.op("vector", lambda eng, eo=eo: eng.tensor_add(xT[:, eo, :], xT[:, eo, :], xnT[:, eo, :]),
                 reads=[xnT, xT], writes=[xT], tag=f"ffn_e4_{eo}")

    def l0_attn(ar):
        qTp = ar.sv(T // 2, BF16)
        kTp = ar.sv(T // 2, BF16)
        vstg = ar.sv(T)
        vnat = ar.sv(T // 2, BF16)
        qp_sb = ar.sv(T, BF16)
        kpn_sb = ar.sv(T // 2, BF16)
        kp_sb = ar.sv(T, BF16)
        _wq = ar.sv(512, BF16); _wk = ar.sv(512, BF16); _wv = ar.sv(512, BF16)
        wq = [_wq, _wq]
        wk = [_wk, _wk]
        wv = [_wv, _wv]
        projs = [ar.sv(32, BF16), ar.sv(32, BF16)]
        bq24 = ar.sv(24)
        aw_sb = ar.sv(128, BF16)
        kv32 = ar.sv(128)
        kvbf = ar.sv(64, BF16)
        kst32 = ar.sv(2)
        kstbf = ar.sv(2, BF16)
        nr_sb = ar.sv(512)
        bc_sb = ar.sv(512)
        kst_sb = ar.sv(32)
        _wo = ar.sv(512, BF16)
        wout = [_wo, _wo]
        bo8 = ar.sv(8)
        tout = vstg

        for j in range(24):
            P.dma("sync", out=bq24.ap()[:, j:j + 1],
                  in_=Hd["bqkv"].ap()[j * 128:(j + 1) * 128].unsqueeze(1),
                  reads=[Hd["bqkv"]], writes=[bq24.keys], tag=f"l0a_bq{j}")
        for j in range(8):
            P.dma("sync", out=bo8.ap()[:, j:j + 1],
                  in_=Hd["bout"].ap()[j * 128:(j + 1) * 128].unsqueeze(1),
                  reads=[Hd["bout"]], writes=[bo8.keys], tag=f"l0a_bo{j}")
        wr_all = Hd["WqkvT"].ap().rearrange("(ci p) e -> p ci e", p=128)

        def wtile(sv_):
            return sv_.ap().rearrange("p (ci q) -> p ci q", q=128)

        for hp in range(NP):
            bi = hp % 2
            for mat, wsv, col0 in (("q", wq[bi], hp * 128), ("k", wk[bi], C + hp * 128),
                                   ("v", wv[bi], 2 * C + hp * 128)):
                P.dma("gpsimd", out=wtile(wsv), in_=wr_all[:, :, col0:col0 + 128],
                      reads=[Hd["WqkvT"]], writes=[wsv.keys], tag=f"l0a_w{mat}{hp}")
            P.dma("gpsimd", out=projs[bi].ap(), in_=Hd["proj_pair"].ap()[hp],
                  reads=[Hd["proj_pair"]], writes=[projs[bi].keys], tag=f"l0a_pj{hp}")
            for mat, wsv, ps, outsv, seg in (("q", wq[bi], psA, qTp, hp),
                                             ("k", wk[bi], psB, kTp, 8 + hp),
                                             ("v", wv[bi], psA, vstg, 16 + hp)):
                for ci in range(NCT):
                    for tb in range(4):
                        P.op("tensor", lambda eng, ps=ps, wsv=wsv, ci=ci, tb=tb: eng.matmul(
                            ps[:, tb * 512:(tb + 1) * 512], wtile(wsv)[:, ci, :],
                            xnT[:, ci, tb * 512:(tb + 1) * 512],
                            start=(ci == 0), stop=(ci == NCT - 1)),
                             reads=[wsv.keys, xnT], writes=[pk(ps, [tb])],
                             tag=f"l0a_{mat}{hp}_{ci}_{tb}")
                P.op("scalar", lambda eng, ps=ps, outsv=outsv, seg=seg: eng.activation(
                    outsv.ap(), ps[:, :], AF.Identity, bias=bq24.ap()[:, seg:seg + 1]),
                     reads=[pk(ps, range(4)), bq24.keys], writes=[outsv.keys],
                     tag=f"l0a_{mat}e{hp}")
            for ti in range(16):
                bank = ti % 4
                col = bank * 512 + ((ti // 4) % 4) * 128
                P.op("tensor", lambda eng, ti=ti, col=col: eng.transpose(
                    psB[:, col:col + 128], vstg.ap()[:, ti * 128:(ti + 1) * 128], ident_f32[:]),
                     reads=[vstg.keys, ident_f32], writes=[pk(psB, [bank])], tag=f"l0a_vt{hp}_{ti}")
                P.op("vector", lambda eng, ti=ti, col=col: eng.tensor_copy(
                    vnat.ap().rearrange("p (a b) -> p a b", b=128)[:, ti, :],
                    psB[:, col:col + 128]),
                     reads=[pk(psB, [ti % 4])], writes=[vnat.keys], tag=f"l0a_vte{hp}_{ti}")
            for nm, src, dst in (("qp", qTp, qp_sb), ("kp", kTp, kp_sb)):
                for h2 in range(2):
                    ps = psA if h2 == 0 else psB
                    r0, r1 = h2 * 64, h2 * 64 + 64
                    for tb in range(4):
                        P.op("tensor", lambda eng, ps=ps, src=src, r0=r0, r1=r1, tb=tb, pj=projs[bi]: eng.matmul(
                            ps[0:64, tb * 512:(tb + 1) * 512], pj.ap()[r0:r1, :],
                            src.ap()[r0:r1, tb * 512:(tb + 1) * 512], start=True, stop=True),
                             reads=[projs[bi].keys, src.keys], writes=[pk(ps, [tb])],
                             tag=f"l0a_{nm}{hp}_{h2}_{tb}")
                for h2 in range(2):
                    ps = psA if h2 == 0 else psB
                    P.op("scalar", lambda eng, ps=ps, dst=dst, h2=h2: eng.activation(
                        dst.ap((0, 64))[:, h2 * 2048:(h2 + 1) * 2048], ps[0:64, :], AF.Relu),
                         reads=[pk(ps, range(4))], writes=[dst.keys], tag=f"l0a_{nm}e{hp}_{h2}")
            for h2 in range(2):
                r0, r1 = h2 * 64, h2 * 64 + 64
                for ch in range(NCH):
                    col = h2 * 1024 + ch * 64
                    bank = col // 512
                    P.op("tensor", lambda eng, r0=r0, r1=r1, ch=ch, col=col, pj=projs[bi]: eng.matmul(
                        psA[:, col:col + 64], kTp.ap()[r0:r1, ch * 128:(ch + 1) * 128],
                        pj.ap()[r0:r1, :], start=True, stop=True),
                         reads=[kTp.keys, projs[bi].keys], writes=[pk(psA, [bank])],
                         tag=f"l0a_kpn{hp}_{h2}_{ch}")
            P.op("scalar", lambda eng: eng.activation(kpn_sb.ap(), psA[:, :], AF.Relu),
                 reads=[pk(psA, range(4))], writes=[kpn_sb.keys], tag=f"l0a_kpne{hp}")
            aw_all = sv(qTp.off_w, 2048, BF16)
            kvch_sb = vstg
            qp_r = qp_sb.ap((0, 64))
            kp_r = kp_sb.ap((0, 64))
            kpn_r = kpn_sb.ap()
            vnat_r = vnat.ap().rearrange("p (a b) -> p a b", b=128)
            kvbf_r = kvbf.ap((0, 64))
            kstbf_r = kstbf.ap((0, 64))
            for ch in range(NCH):
                pb = ch % 2
                for h2 in range(2):
                    P.op("tensor", lambda eng, h2=h2, ch=ch, pb=pb: eng.matmul(
                        psB[:, pb * 512 + h2 * 128: pb * 512 + (h2 + 1) * 128],
                        kp_r[:, h2 * 2048 + ch * 128: h2 * 2048 + (ch + 1) * 128],
                        qp_r[:, h2 * 2048 + ch * 128: h2 * 2048 + (ch + 1) * 128],
                        start=True, stop=True),
                         reads=[kp_sb.keys, qp_sb.keys], writes=[pk(psB, [pb])],
                         tag=f"pf_aw{hp}_{ch}_{h2}")
                P.op("vector", lambda eng, ch=ch, pb=pb: eng.scalar_tensor_tensor(
                    aw_all.ap()[:, ch * 256:(ch + 1) * 256], psB[:, pb * 512:pb * 512 + 256],
                    1.0, trimask[:], ALU.mult, ALU.mult),
                     reads=[pk(psB, [pb]), trimask], writes=[aw_all.keys], tag=f"pf_awm{hp}_{ch}")
            for ch in range(NCH):
                bank = ch // 4
                col = bank * 512 + (ch % 4) * 128
                for h2 in range(2):
                    P.op("tensor", lambda eng, h2=h2, ch=ch, col=col: eng.matmul(
                        psB[0:64, col + h2 * 64: col + (h2 + 1) * 64],
                        kpn_r[:, h2 * 1024 + ch * 64: h2 * 1024 + (ch + 1) * 64],
                        vnat_r[:, ch, h2 * 64:(h2 + 1) * 64], start=True, stop=True),
                         reads=[kpn_sb.keys, vnat.keys], writes=[pk(psB, [bank])],
                         tag=f"pf_kv{hp}_{ch}_{h2}")
            P.op("vector", lambda eng: eng.tensor_copy(kvch_sb.ap((0, 64)), psB[0:64, :]),
                 reads=[pk(psB, range(4))], writes=[kvch_sb.keys], tag=f"pf_kve{hp}")
            for ch in range(NCH):
                for h2 in range(2):
                    P.op("tensor", lambda eng, h2=h2, ch=ch: eng.matmul(
                        psA[0:64, 1536 + ch * 2 + h2: 1537 + ch * 2 + h2],
                        kpn_r[:, h2 * 1024 + ch * 64: h2 * 1024 + (ch + 1) * 64],
                        ones_bf[:, 0:1], start=True, stop=True),
                         reads=[kpn_sb.keys, ones_bf], writes=[pk(psA, [3])],
                         tag=f"pf_ks{hp}_{ch}_{h2}")
            P.op("vector", lambda eng: eng.tensor_copy(kst_sb.ap((0, 64)), psA[0:64, 1536:1568]),
                 reads=[pk(psA, [3])], writes=[kst_sb.keys], tag=f"pf_kse{hp}")
            P.op("vector", lambda eng: eng.memset(kv32.ap((0, 64)), 0.0),
                 reads=[], writes=[kv32.keys], tag=f"l0a_kv0{hp}")
            P.op("vector", lambda eng: eng.memset(kst32.ap((0, 64)), 0.0),
                 reads=[], writes=[kst32.keys], tag=f"l0a_ks0{hp}")
            P.op("scalar", lambda eng: eng.copy(kvbf.ap((0, 64)), kv32.ap((0, 64))),
                 reads=[kv32.keys], writes=[kvbf.keys], tag=f"l0a_kvb0{hp}")
            P.op("scalar", lambda eng: eng.copy(kstbf.ap((0, 64))[:, 0:2], kst32.ap((0, 64))[:, 0:2]),
                 reads=[kst32.keys], writes=[kstbf.keys], tag=f"l0a_ksb0{hp}")
            for ch in range(NCH):
                pb = ch % 2
                cb0 = pb * 512
                nb0 = 1024 + pb * 512
                bb0 = pb * 512
                for h2 in range(2):
                    if ch > 0:
                        P.op("tensor", lambda eng, h2=h2, ch=ch, cb0=cb0: eng.matmul(
                            psA[0:64, cb0 + h2 * 128: cb0 + (h2 + 1) * 128],
                            kvbf_r[:, h2 * 64:(h2 + 1) * 64],
                            qp_r[:, h2 * 2048 + ch * 128: h2 * 2048 + (ch + 1) * 128],
                            start=True, stop=False),
                             reads=[kvbf.keys, qp_sb.keys], writes=[pk(psA, [cb0 // 512])],
                             tag=f"pf_cxh{hp}_{ch}_{h2}")
                    P.op("tensor", lambda eng, h2=h2, ch=ch, cb0=cb0: eng.matmul(
                        psA[0:64, cb0 + h2 * 128: cb0 + (h2 + 1) * 128],
                        vnat_r[:, ch, h2 * 64:(h2 + 1) * 64],
                        aw_all.ap()[:, ch * 256 + h2 * 128: ch * 256 + (h2 + 1) * 128],
                        start=(ch == 0), stop=True),
                         reads=[vnat.keys, aw_all.keys], writes=[pk(psA, [cb0 // 512])],
                         tag=f"pf_cxi{hp}_{ch}_{h2}")
                    if ch > 0:
                        P.op("tensor", lambda eng, h2=h2, ch=ch, nb0=nb0: eng.matmul(
                            psA[0:1, nb0 + h2 * 128: nb0 + (h2 + 1) * 128],
                            kstbf_r[:, h2:h2 + 1],
                            qp_r[:, h2 * 2048 + ch * 128: h2 * 2048 + (ch + 1) * 128],
                            start=True, stop=False),
                             reads=[kstbf.keys, qp_sb.keys], writes=[pk(psA, [nb0 // 512])],
                             tag=f"pf_nh{hp}_{ch}_{h2}")
                    P.op("tensor", lambda eng, h2=h2, ch=ch, nb0=nb0: eng.matmul(
                        psA[0:1, nb0 + h2 * 128: nb0 + (h2 + 1) * 128], ones_bf[:, 0:1],
                        aw_all.ap()[:, ch * 256 + h2 * 128: ch * 256 + (h2 + 1) * 128],
                        start=(ch == 0), stop=True),
                         reads=[ones_bf, aw_all.keys], writes=[pk(psA, [nb0 // 512])],
                         tag=f"pf_ni{hp}_{ch}_{h2}")
                if ch < NCH - 1:
                    P.op("vector", lambda eng, ch=ch: eng.tensor_add(
                        kv32.ap((0, 64)), kv32.ap((0, 64)),
                        kvch_sb.ap((0, 64))[:, ch * 128:(ch + 1) * 128]),
                         reads=[kvch_sb.keys, kv32.keys], writes=[kv32.keys], tag=f"pf_kva{hp}_{ch}")
                    P.op("vector", lambda eng, ch=ch: eng.tensor_add(
                        kst32.ap((0, 64))[:, 0:2], kst32.ap((0, 64))[:, 0:2],
                        kst_sb.ap((0, 64))[:, ch * 2:(ch + 1) * 2]),
                         reads=[kst_sb.keys, kst32.keys], writes=[kst32.keys], tag=f"pf_ksa{hp}_{ch}")
                    P.op("scalar", lambda eng: eng.copy(kvbf.ap((0, 64)), kv32.ap((0, 64))),
                         reads=[kv32.keys], writes=[kvbf.keys], tag=f"pf_kvb{hp}_{ch}")
                    P.op("scalar", lambda eng: eng.copy(kstbf_r[:, 0:2], kst32.ap((0, 64))[:, 0:2]),
                         reads=[kst32.keys], writes=[kstbf.keys], tag=f"pf_ksb{hp}_{ch}")
                nrv = nr_sb.ap((0, 1))[:, pb * 256:pb * 256 + 256]
                nrb = scr_bf[0:1, 2 * nr_sb.off_w + pb * 512: 2 * nr_sb.off_w + pb * 512 + 256]
                P.op("vector", lambda eng, nb0=nb0, nrv=nrv: eng.tensor_scalar_add(
                    nrv, psA[0:1, nb0:nb0 + 256], 1e-6),
                     reads=[pk(psA, [nb0 // 512])], writes=[nr_sb.keys], tag=f"pf_ne{hp}_{ch}")
                P.op("vector", lp(lambda eng, nrv=nrv, nrb=nrb: eng.reciprocal(nrb, nrv)),
                     reads=[nr_sb.keys], writes=[nr_sb.keys], tag=f"pf_nr{hp}_{ch}")
                P.op("tensor", lambda eng, bb0=bb0, nrb=nrb: eng.matmul(
                    psB[0:64, bb0 + 256:bb0 + 512], ones_bf[0:1, 0:64], nrb,
                    start=True, stop=True),
                     reads=[nr_sb.keys, ones_bf], writes=[pk(psB, [bb0 // 512])], tag=f"pf_nb{hp}_{ch}")
                P.op("vector", lambda eng, bb0=bb0, pb=pb: eng.tensor_copy(
                    bc_sb.ap((0, 64))[:, pb * 256:pb * 256 + 256], psB[0:64, bb0 + 256:bb0 + 512]),
                     reads=[pk(psB, [bb0 // 512])], writes=[bc_sb.keys], tag=f"pf_nbe{hp}_{ch}")
                P.op("vector", lambda eng, hp=hp, ch=ch, cb0=cb0, pb=pb: eng.tensor_mul(
                    big0[0:64, hp, ch * 128:(ch + 1) * 128], psA[0:64, cb0:cb0 + 128],
                    bc_sb.ap((0, 64))[:, pb * 256:pb * 256 + 128]),
                     reads=[pk(psA, [cb0 // 512]), bc_sb.keys], writes=[big0], tag=f"pf_o0{hp}_{ch}")
                P.op("vector", lambda eng, hp=hp, ch=ch, cb0=cb0, pb=pb: eng.tensor_mul(
                    big0[64:128, hp, ch * 128:(ch + 1) * 128], psA[0:64, cb0 + 128:cb0 + 256],
                    bc_sb.ap((0, 64))[:, pb * 256 + 128:pb * 256 + 256]),
                     reads=[pk(psA, [cb0 // 512]), bc_sb.keys], writes=[big0], tag=f"pf_o1{hp}_{ch}")
        wout_r = Hd["WoutT"].ap().rearrange("(ci p) (co q) -> co p ci q", p=128, q=128)
        for co in range(NCT):
            wb = wout[co % 2]
            P.dma("gpsimd", out=wtile(wb), in_=wout_r[co],
                  reads=[Hd["WoutT"]], writes=[wb.keys], tag=f"l0a_wo{co}")
            ps = psA if co % 2 == 0 else psB
            for ci in range(NCT):
                for tb in range(4):
                    P.op("tensor", lambda eng, ps=ps, wb=wb, ci=ci, tb=tb, co=co: eng.matmul(
                        ps[:, tb * 512:(tb + 1) * 512], wtile(wb)[:, ci, :],
                        big0[:, ci, tb * 512:(tb + 1) * 512],
                        start=(ci == 0), stop=(ci == NCT - 1)),
                         reads=[wb.keys, big0], writes=[pk(ps, [tb])], tag=f"l0a_o{co}_{ci}_{tb}")
            P.op("scalar", lambda eng, ps=ps, co=co: eng.activation(
                tout.ap(), ps[:, :], AF.Identity, bias=bo8.ap()[:, co:co + 1]),
                 reads=[pk(ps, range(4)), bo8.keys], writes=[tout.keys], tag=f"l0a_oe{co}")
            P.op("vector", lambda eng, co=co: eng.tensor_add(
                xT[:, co, :], xT[:, co, :], tout.ap()),
                 reads=[tout.keys, xT], writes=[xT], tag=f"l0a_oa{co}")

    def router_l1(ar):
        rw = ar.sv(128)
        sumw = ar.sv(16)
        t16 = ar.sv(T)
        P.dma("sync", out=rw.ap().rearrange("p (ci e) -> p ci e", e=16),
              in_=Hd["router_wT"].ap().rearrange("(ci p) e -> p ci e", p=128),
              reads=[Hd["router_wT"]], writes=[rw.keys], tag="r1_w")
        rw_r = rw.ap().rearrange("p (ci e) -> p ci e", e=16)
        for ci in range(NCT):
            for tb in range(4):
                P.op("tensor", lambda eng, ci=ci, tb=tb: eng.matmul(
                    psA[0:16, tb * 512:(tb + 1) * 512], rw_r[:, ci, :],
                    xT[:, ci, tb * 512:(tb + 1) * 512],
                    start=(ci == 0), stop=(ci == NCT - 1)),
                     reads=[rw.keys, xT], writes=[pk(psA, [tb])], tag=f"r1_s0_{ci}_{tb}")
            P.op("tensor", lambda eng, ci=ci: eng.matmul(
                psB[0:16, 0:1], rw_r[:, ci, :], ones_f32[:, 0:1],
                start=(ci == 0), stop=(ci == NCT - 1)),
                 reads=[rw.keys, ones_f32], writes=[pk(psB, [0])], tag=f"r1_sw{ci}")
        P.op("vector", lambda eng: eng.tensor_copy(sumw.ap((0, 16))[:, 0:1], psB[0:16, 0:1]),
             reads=[pk(psB, [0])], writes=[sumw.keys], tag="r1_swe")
        lg = logitsT.ap((0, 16))[:, 0:T]
        P.op("vector", lambda eng: eng.tensor_single_scalar(
            t16.ap((0, 16)), m_b.ap((0, 16)), sumw.ap((0, 16))[:, 0:1], ALU.mult),
             reads=[m_b.keys, sumw.keys], writes=[t16.keys], tag="r1_t1")
        P.op("vector", lambda eng: eng.tensor_sub(lg, psA[0:16, :], t16.ap((0, 16))),
             reads=[pk(psA, range(4)), t16.keys], writes=[logitsT.keys], tag="r1_t2")
        P.op("vector", lambda eng: eng.tensor_mul(lg, lg, rstd_b.ap((0, 16))),
             reads=[logitsT.keys, rstd_b.keys], writes=[logitsT.keys], tag="r1_t3")

    def l1_attn(ar):
        expstk = [ar.sv(1984, BF16), ar.sv(1984, BF16)]
        kTp = [ar.sv(T // 2, BF16), ar.sv(T // 2, BF16)]
        vnat0 = ar.sv(1088, BF16)
        vstg = ar.sv(512)
        ab0 = ar.sv(512, BF16)
        vnat = [vnat0, ar.sv(1088, BF16)]
        attn_bf = [ab0, ar.sv(512, BF16)]
        t64 = ar.sv(1024)
        bc_sb = ar.sv(1024)
        wq_t = ar.sv(512, BF16)
        wk = [ar.sv(512, BF16), wq_t]
        wv = [ar.sv(512, BF16), ar.sv(512, BF16)]
        bseg = ar.sv(24)
        bqs = ar.sv(8)

        for nm, j0 in (("bq", 0), ("bk", 8), ("bv", 16)):
            for j in range(8):
                P.dma("sync", out=bseg.ap()[:, j0 + j:j0 + j + 1],
                      in_=Hd[nm].ap()[j * 128:(j + 1) * 128].unsqueeze(1),
                      reads=[Hd[nm]], writes=[bseg.keys], tag=f"l1a_{nm}{j}")
        P.op("vector", lambda eng: eng.tensor_scalar_mul(bqs.ap(), bseg.ap()[:, 0:8], 0.125),
             reads=[bseg.keys], writes=[bqs.keys], tag="l1a_bqs")

        def wtile(sv_):
            return sv_.ap().rearrange("p (ci q) -> p ci q", q=128)

        wq_r = Hd["Wq"].ap().rearrange("(ci p) (hp q) -> hp p ci q", p=128, q=128)
        wk_r = Hd["Wk"].ap().rearrange("(ci p) (hp q) -> hp p ci q", p=128, q=128)
        wv_r = Hd["Wv"].ap().rearrange("(ci p) (hp q) -> hp p ci q", p=128, q=128)
        for hp in range(NP):
            P.dma("gpsimd", out=wtile(wq_t), in_=wq_r[hp],
                  reads=[Hd["Wq"]], writes=[wq_t.keys], tag=f"l1a_wq{hp}")
            ps = psA if hp % 2 == 0 else psB
            for ci in range(NCT):
                for tb in range(4):
                    P.op("tensor", lambda eng, ps=ps, ci=ci, tb=tb: eng.matmul(
                        ps[:, tb * 512:(tb + 1) * 512], wtile(wq_t)[:, ci, :],
                        xnT[:, ci, tb * 512:(tb + 1) * 512],
                        start=(ci == 0), stop=(ci == NCT - 1)),
                         reads=[wq_t.keys, xnT], writes=[pk(ps, [tb])],
                         tag=f"l1a_q{hp}_{ci}_{tb}")
            P.op("scalar", lambda eng, ps=ps, hp=hp: eng.activation(
                big0[:, hp, :], ps[:, :], AF.Identity,
                bias=bqs.ap()[:, hp:hp + 1], scale=0.125),
                 reads=[pk(ps, range(4)), bqs.keys], writes=[big0], tag=f"l1a_qe{hp}")

        def prep_gen(hp):
            pi = hp % 2
            P.dma("gpsimd", out=wtile(wk[pi]), in_=wk_r[hp],
                  reads=[Hd["Wk"]], writes=[wk[pi].keys], tag=f"l1a_wk{hp}")
            yield
            P.dma("gpsimd", out=wtile(wv[pi]), in_=wv_r[hp],
                  reads=[Hd["Wv"]], writes=[wv[pi].keys], tag=f"l1a_wv{hp}")
            yield
            vnat_r = vnat[pi].ap().rearrange("p (a b) -> p a b", b=136)
            for half in range(4):
                c0 = half * 512
                for ci in range(NCT):
                    P.op("tensor", lambda eng, pi=pi, ci=ci, c0=c0: eng.matmul(
                        psB[:, 1536:2048], wtile(wk[pi])[:, ci, :],
                        xnT[:, ci, c0:c0 + 512], start=(ci == 0), stop=(ci == NCT - 1)),
                         reads=[wk[pi].keys, xnT], writes=[pk(psB, [3])],
                         tag=f"l1a_k{hp}_{half}_{ci}")
                    yield
                P.op("scalar", lambda eng, pi=pi, hp=hp, c0=c0: eng.activation(
                    kTp[pi].ap()[:, c0:c0 + 512], psB[:, 1536:2048], AF.Identity,
                    bias=bseg.ap()[:, 8 + hp:9 + hp]),
                     reads=[pk(psB, [3]), bseg.keys], writes=[kTp[pi].keys],
                     tag=f"l1a_ke{hp}_{half}")
                yield
            for half in range(4):
                c0 = half * 512
                for ci in range(NCT):
                    P.op("tensor", lambda eng, pi=pi, ci=ci, c0=c0: eng.matmul(
                        psB[:, 1536:2048], wtile(wv[pi])[:, ci, :],
                        xnT[:, ci, c0:c0 + 512], start=(ci == 0), stop=(ci == NCT - 1)),
                         reads=[wv[pi].keys, xnT], writes=[pk(psB, [3])],
                         tag=f"l1a_v{hp}_{half}_{ci}")
                    yield
                P.op("scalar", lambda eng, hp=hp: eng.activation(
                    vstg.ap(), psB[:, 1536:2048], AF.Identity,
                    bias=bseg.ap()[:, 16 + hp:17 + hp]),
                     reads=[pk(psB, [3]), bseg.keys], writes=[vstg.keys],
                     tag=f"l1a_ve{hp}_{half}")
                yield
                for tj in range(4):
                    ti = half * 4 + tj
                    P.op("tensor", lambda eng, tj=tj: eng.transpose(
                        psB[:, 1536:1664], vstg.ap()[:, tj * 128:(tj + 1) * 128], ident_f32[:]),
                         reads=[vstg.keys, ident_f32], writes=[pk(psB, [3])],
                         tag=f"l1a_vt{hp}_{ti}")
                    yield
                    P.op("vector", lambda eng, ti=ti, vnat_r=vnat_r: eng.tensor_copy(
                        vnat_r[:, ti, 0:64], psB[:, 1536:1600]),
                         reads=[pk(psB, [3])], writes=[vnat[pi].keys], tag=f"l1a_va{hp}_{ti}")
                    P.op("vector", lambda eng, ti=ti, vnat_r=vnat_r: eng.tensor_copy(
                        vnat_r[:, ti, 68:132], psB[:, 1600:1664]),
                         reads=[pk(psB, [3])], writes=[vnat[pi].keys], tag=f"l1a_vb{hp}_{ti}")
                    yield
            P.op("vector", lambda eng, vnat_r=vnat_r: eng.memset(vnat_r[:, :, 64:68], 1.0),
                 reads=[], writes=[vnat[pi].keys], tag=f"l1a_on1{hp}")
            P.op("vector", lambda eng, vnat_r=vnat_r: eng.memset(vnat_r[:, :, 132:136], 1.0),
                 reads=[], writes=[vnat[pi].keys], tag=f"l1a_on2{hp}")
            yield

        def attn_gen(hp):
            pi = hp % 2
            vnat_r = vnat[pi].ap().rearrange("p (a b) -> p a b", b=136)
            for h2 in range(2):
                h = 2 * hp + h2
                est = expstk[h2]
                P.dma("gpsimd", out=est.ap()[:, 0:NF], in_=Hd["rel_stack"].ap()[h],
                      reads=[Hd["rel_stack"]], writes=[est.keys], tag=f"l1a_stk{h}")
                P.op("scalar", lambda eng, est=est: eng.activation(
                    est.ap()[:, 0:NF], est.ap()[:, 0:NF], AF.Exp),
                     reads=[est.keys], writes=[est.keys], tag=f"l1a_estk{h}")
                yield
            for h2 in range(2):
                h = 2 * hp + h2
                r0, r1 = h2 * 64, h2 * 64 + 64
                est = expstk[h2]
                for qh in range(2):
                    q0 = qh * 1024
                    for kc in range(NCH):
                        sb0 = (kc % 2) * 1024
                        ab = attn_bf[kc % 2]
                        for tb in range(2):
                            P.op("tensor", lambda eng, kc=kc, sb0=sb0, tb=tb, r0=r0, r1=r1, q0=q0, hp=hp, pi=pi: eng.matmul(
                                psA[:, sb0 + tb * 512: sb0 + (tb + 1) * 512],
                                kTp[pi].ap()[r0:r1, kc * 128:(kc + 1) * 128],
                                big0[r0:r1, hp, q0 + tb * 512: q0 + (tb + 1) * 512],
                                start=True, stop=True),
                                 reads=[kTp[pi].keys, big0], writes=[pk(psA, [sb0 // 512 + tb])],
                                 tag=f"l1a_s{h}_{qh}_{kc}_{tb}")
                        P.op("scalar", lambda eng, ab=ab, sb0=sb0: eng.activation(
                            ab.ap(), psA[:, sb0:sb0 + 1024], AF.Exp),
                             reads=[pk(psA, [sb0 // 512, sb0 // 512 + 1])],
                             writes=[ab.keys], tag=f"l1a_ex{h}_{qh}_{kc}")
                        f0 = 1920 - kc * 128 + q0
                        P.op("vector", lambda eng, ab=ab, est=est, f0=f0: eng.tensor_mul(
                            ab.ap(), ab.ap(), est.ap()[:, f0:f0 + 1024]),
                             reads=[ab.keys, est.keys], writes=[ab.keys],
                             tag=f"l1a_bm{h}_{qh}_{kc}")
                        for tb in range(2):
                            P.op("tensor", lambda eng, ab=ab, kc=kc, tb=tb, h2=h2, vnat_r=vnat_r, pi=pi: eng.matmul(
                                psB[0:68, tb * 512:(tb + 1) * 512],
                                vnat_r[:, kc, h2 * 68:(h2 + 1) * 68],
                                ab.ap()[:, tb * 512:(tb + 1) * 512],
                                start=(kc == 0), stop=(kc == NCH - 1)),
                                 reads=[vnat[pi].keys, ab.keys], writes=[pk(psB, [tb])],
                                 tag=f"l1a_av{h}_{qh}_{kc}_{tb}")
                        yield
                    nrbf = scr_bf[0:1, 2 * t64.off_w:2 * t64.off_w + 1024]
                    P.op("vector", lp(lambda eng, nrbf=nrbf: eng.reciprocal(
                        nrbf, psB[64:65, 0:1024])),
                         reads=[pk(psB, [0, 1])], writes=[t64.keys], tag=f"l1a_nr{h}_{qh}")
                    for half in range(2):
                        P.op("tensor", lambda eng, nrbf=nrbf, half=half: eng.matmul(
                            psB[0:64, 1024:1536], ones_bf[0:1, 0:64],
                            nrbf[:, half * 512:(half + 1) * 512], start=True, stop=True),
                             reads=[t64.keys, ones_bf], writes=[pk(psB, [2])],
                             tag=f"l1a_b{half}{h}_{qh}")
                        P.op("vector", lambda eng, half=half: eng.tensor_copy(
                            bc_sb.ap((0, 64))[:, half * 512:(half + 1) * 512], psB[0:64, 1024:1536]),
                             reads=[pk(psB, [2])], writes=[bc_sb.keys],
                             tag=f"l1a_bce{half}{h}_{qh}")
                    P.op("vector", lambda eng, r0=r0, r1=r1: eng.tensor_mul(
                        t64.ap((r0, r1)), psB[0:64, 0:1024], bc_sb.ap((0, 64))),
                         reads=[pk(psB, [0, 1]), bc_sb.keys], writes=[t64.keys],
                         tag=f"l1a_ap{h}_{qh}")
                    P.op("vector", lambda eng, r0=r0, r1=r1, hp=hp, q0=q0: eng.tensor_add(
                        xT[r0:r1, hp, q0:q0 + 1024], xT[r0:r1, hp, q0:q0 + 1024],
                        t64.ap((r0, r1))),
                         reads=[t64.keys, xT], writes=[xT], tag=f"l1a_xa{h}_{qh}")
                    yield

        def drive(agen, pgen, ratio=3):
            done_a = agen is None
            done_p = pgen is None
            while not (done_a and done_p):
                for _ in range(ratio):
                    if not done_a:
                        try:
                            next(agen)
                        except StopIteration:
                            done_a = True
                if not done_p:
                    try:
                        next(pgen)
                    except StopIteration:
                        done_p = True

        drive(None, prep_gen(0))
        for hp in range(NP):
            drive(attn_gen(hp), prep_gen(hp + 1) if hp + 1 < NP else None)

    def l1_moe(ar):
        lgdram = nc.dram_tensor("moe_lg_d", [16, T], F32, kind="Internal")
        cmbdram = nc.dram_tensor("moe_cmb_d", [16, T], F32, kind="Internal")
        spl = ar.sv(256)
        sbp = ar.sv(256)
        msb = ar.sv(256)
        msbo = ar.sv(256)
        gs = ar.sv(64)
        gwk = ar.sv(64)
        gmx = ar.sv(16)
        gmask = ar.sv(64)
        selp = ar.sv(256)
        wsum = ar.sv(16)
        cb16 = ar.sv(16)
        t256 = ar.sv(256)
        cmb_bf = ar.sv(T // 2, BF16)
        sg_bf = ar.sv(T // 2, BF16)
        su_bf = ar.sv(T // 2, BF16)
        wg = [ar.sv(1024, BF16), ar.sv(1024, BF16)]
        wu = [ar.sv(1024, BF16), ar.sv(1024, BF16)]
        wdg = [ar.sv(1024, BF16) for _ in range(4)]
        wsg, wsu, wsd = wg[0], wu[0], wdg[0]
        cmbrow = sv(spl.off_w, T)

        lg = logitsT.ap((0, 16))[:, 0:T]
        P.dma("scalar", out=lgdram.ap(), in_=lg, reads=[logitsT.keys], writes=[lgdram], tag="moe_lgd")
        for e in range(E):
            P.dma("scalar", out=spl.ap()[:, e * 16:(e + 1) * 16],
                  in_=lgdram.ap()[e].rearrange("(p j) -> p j", j=16),
                  reads=[lgdram], writes=[spl.keys], tag=f"moe_pl{e}")
        P.op("scalar", lambda eng: eng.activation(spl.ap(), spl.ap(), AF.Sigmoid),
             reads=[spl.keys], writes=[spl.keys], tag="moe_sig")
        P.dma("sync", out=cb16.ap((0, 1))[:, 0:16], in_=Hd["corr_bias"].ap().unsqueeze(0),
              reads=[Hd["corr_bias"]], writes=[cb16.keys], tag="moe_cb")
        P.op("tensor", lambda eng: eng.matmul(
            psA[:, 0:16], ones_f32[0:1, :], cb16.ap((0, 1))[:, 0:16], start=True, stop=True),
             reads=[cb16.keys, ones_f32], writes=[pk(psA, [0])], tag="moe_cbb")
        P.op("vector", lambda eng: eng.tensor_copy(cb16.ap(), psA[:, 0:16]),
             reads=[pk(psA, [0])], writes=[cb16.keys], tag="moe_cbe")
        for e in range(E):
            P.op("vector", lambda eng, e=e: eng.tensor_single_scalar(
                sbp.ap()[:, e * 16:(e + 1) * 16], spl.ap()[:, e * 16:(e + 1) * 16],
                cb16.ap()[:, e:e + 1], ALU.add),
                 reads=[spl.keys, cb16.keys], writes=[sbp.keys], tag=f"moe_sb{e}")
        for g in range(4):
            e0 = g * 4
            blk = lambda i, e0=e0: sbp.ap()[:, (e0 + i) * 16:(e0 + i + 1) * 16]
            P.op("vector", lambda eng, blk=blk: eng.tensor_max(t256.ap()[:, 0:16], blk(0), blk(1)),
                 reads=[sbp.keys], writes=[t256.keys], tag=f"moe_g{g}a")
            P.op("vector", lambda eng, blk=blk: eng.tensor_max(t256.ap()[:, 16:32], blk(2), blk(3)),
                 reads=[sbp.keys], writes=[t256.keys], tag=f"moe_g{g}b")
            P.op("vector", lambda eng: eng.tensor_max(
                gmx.ap(), t256.ap()[:, 0:16], t256.ap()[:, 16:32]),
                 reads=[t256.keys], writes=[gmx.keys], tag=f"moe_g{g}m1")
            for i in range(4):
                P.op("vector", lambda eng, blk=blk, i=i: eng.tensor_tensor(
                    t256.ap()[:, 32 + i * 16:48 + i * 16], blk(i), gmx.ap(), ALU.is_lt),
                     reads=[sbp.keys, gmx.keys], writes=[t256.keys], tag=f"moe_g{g}lt{i}")
                P.op("vector", lambda eng, blk=blk, i=i: eng.tensor_mul(
                    t256.ap()[:, 96 + i * 16:112 + i * 16], blk(i),
                    t256.ap()[:, 32 + i * 16:48 + i * 16]),
                     reads=[sbp.keys, t256.keys], writes=[t256.keys], tag=f"moe_g{g}q{i}")
                P.op("vector", lambda eng, i=i: eng.tensor_scalar(
                    t256.ap()[:, 32 + i * 16:48 + i * 16],
                    t256.ap()[:, 32 + i * 16:48 + i * 16], 1e30, -1e30, ALU.mult, ALU.add),
                     reads=[t256.keys], writes=[t256.keys], tag=f"moe_g{g}p{i}")
                P.op("vector", lambda eng, i=i: eng.tensor_add(
                    t256.ap()[:, 96 + i * 16:112 + i * 16],
                    t256.ap()[:, 96 + i * 16:112 + i * 16],
                    t256.ap()[:, 32 + i * 16:48 + i * 16]),
                     reads=[t256.keys], writes=[t256.keys], tag=f"moe_g{g}r{i}")
            P.op("vector", lambda eng: eng.tensor_max(
                t256.ap()[:, 0:16], t256.ap()[:, 96:112], t256.ap()[:, 112:128]),
                 reads=[t256.keys], writes=[t256.keys], tag=f"moe_g{g}c")
            P.op("vector", lambda eng: eng.tensor_max(
                t256.ap()[:, 16:32], t256.ap()[:, 128:144], t256.ap()[:, 144:160]),
                 reads=[t256.keys], writes=[t256.keys], tag=f"moe_g{g}d")
            P.op("vector", lambda eng: eng.tensor_max(
                t256.ap()[:, 0:16], t256.ap()[:, 0:16], t256.ap()[:, 16:32]),
                 reads=[t256.keys], writes=[t256.keys], tag=f"moe_g{g}m2")
            P.op("vector", lambda eng, g=g: eng.tensor_add(
                gs.ap()[:, g * 16:(g + 1) * 16], gmx.ap(), t256.ap()[:, 0:16]),
                 reads=[gmx.keys, t256.keys], writes=[gs.keys], tag=f"moe_g{g}s")
        for g in range(4):
            first = True
            for g2 in range(4):
                if g2 == g:
                    continue
                P.op("vector", lambda eng, g=g, g2=g2: eng.tensor_tensor(
                    t256.ap()[:, 0:16], gs.ap()[:, g2 * 16:(g2 + 1) * 16],
                    gs.ap()[:, g * 16:(g + 1) * 16], ALU.is_gt),
                     reads=[gs.keys], writes=[t256.keys], tag=f"moe_c{g}{g2}")
                if first:
                    P.op("vector", lambda eng, g=g: eng.tensor_copy(
                        gwk.ap()[:, g * 16:(g + 1) * 16], t256.ap()[:, 0:16]),
                         reads=[t256.keys], writes=[gwk.keys], tag=f"moe_ci{g}{g2}")
                    first = False
                else:
                    P.op("vector", lambda eng, g=g: eng.tensor_add(
                        gwk.ap()[:, g * 16:(g + 1) * 16], gwk.ap()[:, g * 16:(g + 1) * 16],
                        t256.ap()[:, 0:16]),
                         reads=[t256.keys, gwk.keys], writes=[gwk.keys], tag=f"moe_ca{g}{g2}")
            P.op("vector", lambda eng, g=g: eng.tensor_single_scalar(
                gmask.ap()[:, g * 16:(g + 1) * 16], gwk.ap()[:, g * 16:(g + 1) * 16],
                2.0, ALU.is_lt),
                 reads=[gwk.keys], writes=[gmask.keys], tag=f"moe_gm{g}")
        P.op("vector", lambda eng: eng.tensor_scalar(
            gwk.ap(), gmask.ap(), 1e30, -1e30, ALU.mult, ALU.add),
             reads=[gmask.keys], writes=[gwk.keys], tag="moe_gterm")
        for e in range(E):
            g = e // 4
            P.op("vector", lambda eng, e=e, g=g: eng.tensor_mul(
                msb.ap()[:, e * 16:(e + 1) * 16], sbp.ap()[:, e * 16:(e + 1) * 16],
                gmask.ap()[:, g * 16:(g + 1) * 16]),
                 reads=[sbp.keys, gmask.keys], writes=[msb.keys], tag=f"moe_m2{e}")
            P.op("vector", lambda eng, e=e, g=g: eng.tensor_add(
                msb.ap()[:, e * 16:(e + 1) * 16], msb.ap()[:, e * 16:(e + 1) * 16],
                gwk.ap()[:, g * 16:(g + 1) * 16]),
                 reads=[msb.keys, gwk.keys], writes=[msb.keys], tag=f"moe_m3{e}")
        P.op("vector", lambda eng: eng.tensor_copy(msbo.ap(), msb.ap()),
             reads=[msb.keys], writes=[msbo.keys], tag="moe_mo")
        def max_over_e(dst_keys, dst_ap, src_ap, src_keys, tagp):
            P.op("vector", lambda eng: eng.tensor_max(t256.ap()[:, 0:128],
                 src_ap[:, 0:128], src_ap[:, 128:256]),
                 reads=[src_keys], writes=[t256.keys], tag=tagp + "a")
            P.op("vector", lambda eng: eng.tensor_max(t256.ap()[:, 0:64],
                 t256.ap()[:, 0:64], t256.ap()[:, 64:128]),
                 reads=[t256.keys], writes=[t256.keys], tag=tagp + "b")
            P.op("vector", lambda eng: eng.tensor_max(t256.ap()[:, 0:32],
                 t256.ap()[:, 0:32], t256.ap()[:, 32:64]),
                 reads=[t256.keys], writes=[t256.keys], tag=tagp + "c")
            P.op("vector", lambda eng: eng.tensor_max(dst_ap,
                 t256.ap()[:, 0:16], t256.ap()[:, 16:32]),
                 reads=[t256.keys], writes=[dst_keys], tag=tagp + "d")
        for it in range(4):
            max_over_e(gmx.keys, gmx.ap(), msb.ap(), msb.keys, f"moe_t{it}")
            if it < 3:
                for e in range(E):
                    P.op("vector", lambda eng, e=e: eng.tensor_tensor(
                        t256.ap()[:, e * 16:(e + 1) * 16], msb.ap()[:, e * 16:(e + 1) * 16],
                        gmx.ap(), ALU.is_ge),
                         reads=[msb.keys, gmx.keys], writes=[t256.keys], tag=f"moe_x{it}_{e}")
                P.op("vector", lambda eng: eng.scalar_tensor_tensor(
                    msb.ap(), t256.ap(), -2e30, msb.ap(), ALU.mult, ALU.add),
                     reads=[t256.keys, msb.keys], writes=[msb.keys], tag=f"moe_y{it}")
        for e in range(E):
            P.op("vector", lambda eng, e=e: eng.tensor_tensor(
                selp.ap()[:, e * 16:(e + 1) * 16], msbo.ap()[:, e * 16:(e + 1) * 16],
                gmx.ap(), ALU.is_ge),
                 reads=[msbo.keys, gmx.keys], writes=[selp.keys], tag=f"moe_s{e}")
        P.op("vector", lambda eng: eng.tensor_mul(selp.ap(), selp.ap(), spl.ap()),
             reads=[selp.keys, spl.keys], writes=[selp.keys], tag="moe_w")
        P.op("vector", lambda eng: eng.tensor_add(t256.ap()[:, 0:128],
             selp.ap()[:, 0:128], selp.ap()[:, 128:256]),
             reads=[selp.keys], writes=[t256.keys], tag="moe_wsa")
        P.op("vector", lambda eng: eng.tensor_add(t256.ap()[:, 0:64],
             t256.ap()[:, 0:64], t256.ap()[:, 64:128]),
             reads=[t256.keys], writes=[t256.keys], tag="moe_wsb")
        P.op("vector", lambda eng: eng.tensor_add(t256.ap()[:, 0:32],
             t256.ap()[:, 0:32], t256.ap()[:, 32:64]),
             reads=[t256.keys], writes=[t256.keys], tag="moe_wsc")
        P.op("vector", lambda eng: eng.tensor_add(wsum.ap(),
             t256.ap()[:, 0:16], t256.ap()[:, 16:32]),
             reads=[t256.keys], writes=[wsum.keys], tag="moe_wsd")
        P.op("vector", lambda eng: eng.tensor_scalar_add(wsum.ap(), wsum.ap(), 1e-20),
             reads=[wsum.keys], writes=[wsum.keys], tag="moe_wse")
        P.op("vector", lambda eng: eng.reciprocal(wsum.ap(), wsum.ap()),
             reads=[wsum.keys], writes=[wsum.keys], tag="moe_wsi")
        for e in range(E):
            P.op("vector", lambda eng, e=e: eng.tensor_mul(
                selp.ap()[:, e * 16:(e + 1) * 16], selp.ap()[:, e * 16:(e + 1) * 16],
                wsum.ap()),
                 reads=[selp.keys, wsum.keys], writes=[selp.keys], tag=f"moe_cw{e}")
            P.dma("scalar", out=cmbdram.ap()[e].rearrange("(p j) -> p j", j=16),
                  in_=selp.ap()[:, e * 16:(e + 1) * 16],
                  reads=[selp.keys], writes=[cmbdram], tag=f"moe_cd{e}")
        if debug:
            for nm, svv, w_ in (("selp", selp, 256), ("gmask", gmask, 64), ("sbp", sbp, 256),
                                ("spl", spl, 256), ("gs", gs, 64), ("msbo", msbo, 256),
                                ("gmx", gmx, 16)):
                hh = nc.dram_tensor("dbg_" + nm, [128, w_], F32, kind="ExternalOutput")
                P.dma("sync", out=hh.ap(), in_=svv.ap()[:, 0:w_], reads=[svv.keys], writes=[], tag="dbg" + nm)
        sgw_r = Hd["sgwT"].ap().rearrange("(ci p) (hs q) -> hs p ci q", p=128, q=128)
        suw_r = Hd["suwT"].ap().rearrange("(ci p) (hs q) -> hs p ci q", p=128, q=128)
        sdw_r = Hd["sdwT"].ap().rearrange("(hs p) (ct q) -> ct p hs q", p=128, q=128)
        for half in range(2):
            for hs8 in range(8):
                hs = half * 8 + hs8
                P.dma("gpsimd", out=wsg.ap()[:, 0:1024].rearrange("p (ci q) -> p ci q", q=128),
                      in_=sgw_r[hs], reads=[Hd["sgwT"]], writes=[wsg.keys], tag=f"moe_sg{hs}")
                P.dma("gpsimd", out=wsu.ap()[:, 0:1024].rearrange("p (ci q) -> p ci q", q=128),
                      in_=suw_r[hs], reads=[Hd["suwT"]], writes=[wsu.keys], tag=f"moe_su{hs}")
                for ci in range(NCT):
                    for tb in range(4):
                        P.op("tensor", lambda eng, ci=ci, tb=tb: eng.matmul(
                            psA[:, tb * 512:(tb + 1) * 512],
                            wsg.ap()[:, 0:1024].rearrange("p (ci q) -> p ci q", q=128)[:, ci, :],
                            xnT[:, ci, tb * 512:(tb + 1) * 512],
                            start=(ci == 0), stop=(ci == NCT - 1)),
                             reads=[wsg.keys, xnT], writes=[pk(psA, [tb])], tag=f"moe_g{hs}_{ci}_{tb}")
                        P.op("tensor", lambda eng, ci=ci, tb=tb: eng.matmul(
                            psB[:, tb * 512:(tb + 1) * 512],
                            wsu.ap()[:, 0:1024].rearrange("p (ci q) -> p ci q", q=128)[:, ci, :],
                            xnT[:, ci, tb * 512:(tb + 1) * 512],
                            start=(ci == 0), stop=(ci == NCT - 1)),
                             reads=[wsu.keys, xnT], writes=[pk(psB, [tb])], tag=f"moe_u{hs}_{ci}_{tb}")
                P.op("scalar", lambda eng: eng.activation(sg_bf.ap(), psA[:, :], AF.Silu),
                     reads=[pk(psA, range(4))], writes=[sg_bf.keys], tag=f"moe_sge{hs}")
                P.op("scalar", lambda eng: eng.copy(su_bf.ap(), psB[:, :]),
                     reads=[pk(psB, range(4))], writes=[su_bf.keys], tag=f"moe_sue{hs}")
                P.op("vector", lambda eng, hs8=hs8: eng.tensor_mul(
                    big0[:, hs8, :], sg_bf.ap(), su_bf.ap()),
                     reads=[sg_bf.keys, su_bf.keys], writes=[big0], tag=f"moe_h{hs}")
            for ct in range(NCT):
                P.dma("gpsimd", out=wsd.ap()[:, 0:1024].rearrange("p (hs q) -> p hs q", q=128),
                      in_=sdw_r[ct][:, half * 8:(half + 1) * 8, :],
                      reads=[Hd["sdwT"]], writes=[wsd.keys], tag=f"moe_sd{half}_{ct}")
                ps = psA if ct % 2 == 0 else psB
                for hs8 in range(8):
                    for tb in range(4):
                        P.op("tensor", lambda eng, ps=ps, hs8=hs8, tb=tb: eng.matmul(
                            ps[:, tb * 512:(tb + 1) * 512],
                            wsd.ap()[:, 0:1024].rearrange("p (hs q) -> p hs q", q=128)[:, hs8, :],
                            big0[:, hs8, tb * 512:(tb + 1) * 512],
                            start=(hs8 == 0), stop=(hs8 == 7)),
                             reads=[wsd.keys, big0], writes=[pk(ps, [tb])],
                             tag=f"moe_d{half}_{ct}_{hs8}_{tb}")
                P.op("vector", lambda eng, ps=ps, ct=ct: eng.tensor_add(
                    xT[:, ct, :], xT[:, ct, :], ps[:, :]),
                     reads=[pk(ps, range(4)), xT], writes=[xT], tag=f"moe_da{half}_{ct}")
        if debug:
            for ci in range(NCT):
                hh = nc.dram_tensor(f"dbg_x17_{ci}", [128, T], F32, kind="ExternalOutput")
                P.dma("sync", out=hh.ap(), in_=xT[:, ci, :], reads=[xT], writes=[], tag=f"dbgx17_{ci}")
        gw_r = Hd["gate_w"].ap().rearrange("e (ci p) hh -> e p ci hh", p=128)
        uw_r = Hd["up_w"].ap().rearrange("e (ci p) hh -> e p ci hh", p=128)
        dw_r = Hd["down_w"].ap().rearrange("e (hc p) (ct q) -> e p hc ct q", p=128, q=128)
        for grp in range(4):
            for ei in range(4):
                e = grp * 4 + ei
                crbf = scr_bf[0:1, 2 * cmbrow.off_w:2 * cmbrow.off_w + T]
                P.dma("gpsimd", out=crbf, in_=cmbdram.ap()[e].unsqueeze(0),
                      reads=[cmbdram], writes=[cmbrow.keys], tag=f"moe_cr{e}")
                for tb in range(4):
                    P.op("tensor", lambda eng, tb=tb, crbf=crbf: eng.matmul(
                        psA[:, tb * 512:(tb + 1) * 512], ones_bf[0:1, :],
                        crbf[:, tb * 512:(tb + 1) * 512], start=True, stop=True),
                         reads=[cmbrow.keys, ones_bf], writes=[pk(psA, [tb])], tag=f"moe_cb{e}_{tb}")
                P.op("scalar", lambda eng: eng.copy(cmb_bf.ap(), psA[:, :]),
                     reads=[pk(psA, range(4))], writes=[cmb_bf.keys], tag=f"moe_cbe{e}")
                wge, wue = wg[e % 2], wu[e % 2]
                P.dma("gpsimd", out=wge.ap().rearrange("p (ci hh) -> p ci hh", hh=256),
                      in_=gw_r[e], reads=[Hd["gate_w"]], writes=[wge.keys], tag=f"moe_wg{e}")
                P.dma("gpsimd", out=wue.ap().rearrange("p (ci hh) -> p ci hh", hh=256),
                      in_=uw_r[e], reads=[Hd["up_w"]], writes=[wue.keys], tag=f"moe_wu{e}")
                for hc in range(2):
                    for ci in range(NCT):
                        for tb in range(4):
                            P.op("tensor", lambda eng, hc=hc, ci=ci, tb=tb, wge=wge: eng.matmul(
                                psA[:, tb * 512:(tb + 1) * 512],
                                wge.ap().rearrange("p (ci hh) -> p ci hh", hh=256)[:, ci, hc * 128:(hc + 1) * 128],
                                xnT[:, ci, tb * 512:(tb + 1) * 512],
                                start=(ci == 0), stop=(ci == NCT - 1)),
                                 reads=[wge.keys, xnT], writes=[pk(psA, [tb])],
                                 tag=f"moe_eg{e}_{hc}_{ci}_{tb}")
                            P.op("tensor", lambda eng, hc=hc, ci=ci, tb=tb, wue=wue: eng.matmul(
                                psB[:, tb * 512:(tb + 1) * 512],
                                wue.ap().rearrange("p (ci hh) -> p ci hh", hh=256)[:, ci, hc * 128:(hc + 1) * 128],
                                xnT[:, ci, tb * 512:(tb + 1) * 512],
                                start=(ci == 0), stop=(ci == NCT - 1)),
                                 reads=[wue.keys, xnT], writes=[pk(psB, [tb])],
                                 tag=f"moe_eu{e}_{hc}_{ci}_{tb}")
                    P.op("scalar", lambda eng: eng.activation(sg_bf.ap(), psA[:, :], AF.Silu),
                         reads=[pk(psA, range(4))], writes=[sg_bf.keys], tag=f"moe_ege{e}_{hc}")
                    P.op("scalar", lambda eng: eng.copy(su_bf.ap(), psB[:, :]),
                         reads=[pk(psB, range(4))], writes=[su_bf.keys], tag=f"moe_eue{e}_{hc}")
                    P.op("vector", lambda eng: eng.tensor_mul(sg_bf.ap(), sg_bf.ap(), su_bf.ap()),
                         reads=[sg_bf.keys, su_bf.keys], writes=[sg_bf.keys], tag=f"moe_eh{e}_{hc}")
                    P.op("vector", lambda eng, ei=ei, hc=hc: eng.tensor_mul(
                        big0[:, ei * 2 + hc, :], sg_bf.ap(), cmb_bf.ap()),
                         reads=[sg_bf.keys, cmb_bf.keys], writes=[big0], tag=f"moe_ehw{e}_{hc}")
            for ei in range(4):
                e = grp * 4 + ei
                P.dma("gpsimd", out=wdg[ei].ap().rearrange("p (hc ct q) -> p hc ct q", hc=2, q=128),
                      in_=dw_r[e], reads=[Hd["down_w"]], writes=[wdg[ei].keys],
                      tag=f"moe_wd{e}")
            for ct in range(NCT):
                ps = psA if ct % 2 == 0 else psB
                for ei in range(4):
                    e = grp * 4 + ei
                    for hc in range(2):
                        for tb in range(4):
                            P.op("tensor", lambda eng, ps=ps, ei=ei, e=e, hc=hc, ct=ct, tb=tb: eng.matmul(
                                ps[:, tb * 512:(tb + 1) * 512],
                                wdg[ei].ap().rearrange("p (hc ct q) -> p hc ct q", hc=2, q=128)[:, hc, ct, :],
                                big0[:, ei * 2 + hc, tb * 512:(tb + 1) * 512],
                                start=(ei == 0 and hc == 0), stop=(ei == 3 and hc == 1)),
                                 reads=[wdg[ei].keys, big0], writes=[pk(ps, [tb])],
                                 tag=f"moe_ed{e}_{ct}_{hc}_{tb}")
                P.op("vector", lambda eng, ps=ps, ct=ct: eng.tensor_add(
                    xT[:, ct, :], xT[:, ct, :], ps[:, :]),
                     reads=[pk(ps, range(4)), xT], writes=[xT], tag=f"moe_eda{grp}_{ct}")

    layernorm(0)
    if stage >= 1:
        ar0 = Arena("late")
        router_l0(ar0)
    if stage >= 2:
        ar_f = Arena()
        l0_ffn(ar_f)
    if stage >= 3:
        ar_a = Arena()
        l0_attn(ar_a)
    if stage >= 4 and debug:
        for ci in range(NCT):
            hh = nc.dram_tensor(f"dbg_x1_{ci}", [128, T], F32, kind="ExternalOutput")
            P.dma("sync", out=hh.ap(), in_=xT[:, ci, :], reads=[xT], writes=[], tag=f"dbgx1_{ci}")
    if stage >= 4:
        layernorm(1)
        ar1 = Arena("late")
        router_l1(ar1)
    if stage >= 5:
        ar_l1 = Arena()
        l1_attn(ar_l1)
    if stage >= 6 and debug:
        for ci in range(NCT):
            hh = nc.dram_tensor(f"dbg_x15_{ci}", [128, T], F32, kind="ExternalOutput")
            P.dma("sync", out=hh.ap(), in_=xT[:, ci, :], reads=[xT], writes=[], tag=f"dbgx15_{ci}")
    if stage >= 6:
        ar_m = Arena()
        l1_moe(ar_m)

    for ti in range(NCH):
        pst = psA if ti % 2 == 0 else psB
        stgv = stg_a if ti % 2 == 0 else stg_b
        for ci in range(NCT):
            bank = ci % 4
            col = bank * 512 + (ci // 4) * 128
            dst = pst[:, col:col + 128]
            P.op("tensor",
                 lambda eng, dst=dst, ti=ti, ci=ci: eng.transpose(
                     dst, xT[:, ci, ti * 128:(ti + 1) * 128], ident_f32[:]),
                 reads=[xT, ident_f32], writes=[pk(pst, [bank])], tag=f"oy{ti}_{ci}")
            P.op("vector",
                 lambda eng, dst=dst, ci=ci, stgv=stgv: eng.tensor_copy(
                     stgv.ap()[:, ci * 128:(ci + 1) * 128], dst),
                 reads=[pk(pst, [bank])], writes=[stgv.keys], tag=f"oye{ti}_{ci}")
        P.dma("sync", out=y.ap().rearrange("(ti p) c -> ti p c", p=128)[ti],
              in_=stgv.ap()[:, 0:C], reads=[stgv.keys], writes=[], tag=f"sty{ti}")

    dbg = {}
    if debug:
        def dump(name, src_ap, shape, dtype, readkeys):
            hh = nc.dram_tensor("dbg_" + name, list(shape), dtype, kind="ExternalOutput")
            P.dma("sync", out=hh.ap(), in_=src_ap, reads=[readkeys], writes=[], tag="dbg" + name)
            dbg[name] = hh
        for ci in range(NCT):
            hh = nc.dram_tensor(f"dbg_xnT{ci}", [128, T], BF16, kind="ExternalOutput")
            P.dma("sync", out=hh.ap(), in_=xnT[:, ci, :], reads=[xnT], writes=[], tag=f"dbgxn{ci}")
        dump("mrow", rows.ap((0, 1))[:, 0:T], (1, T), F32, rows.keys)
        dump("rstd", rows.ap((0, 1))[:, T:2 * T], (1, T), F32, rows.keys)
        if stage >= 1:
            dump("scoreL0", logitsT.ap((0, 1))[:, 0:T], (1, T), F32, logitsT.keys)
        if stage >= 4:
            dump("logits", logitsT.ap((0, 16))[:, 0:T], (16, T), F32, logitsT.keys)

    from contextlib import ExitStack
    stack = ExitStack()
    sems = {e: stack.enter_context(nc.semaphore(f"s_{e}")) for e in Prog.CENG}
    qsems = {q: [stack.enter_context(nc.semaphore(f"q_{q}{l}")) for l in range(Prog.NLANES)]
             for q in Prog.QUEUES}
    with nc.Block() as block:
        P.emit_all(block, sems, qsems)
    stack.close()
    return nc


def make_in_maps(inputs, n_cores=8):
    x = np.asarray(inputs["L0_attn_x"])
    base = prep_weights(inputs)
    base.update(prep_consts())
    in_maps = []
    for b in range(n_cores):
        m = dict(base)
        m["x"] = np.ascontiguousarray(x[b])
        in_maps.append(m)
    return in_maps


def kernel(**inputs):
    nc = build_program()
    in_maps = make_in_maps(inputs)
    res = run_bass_kernel_spmd(nc, in_maps, core_ids=list(range(8)))
    return np.stack([res.results[b]["y"] for b in range(8)], axis=0)


# version 13
# speedup vs baseline: 1.0225x; 1.0225x over previous
import numpy as np
import ml_dtypes
import concourse.bass as bass
import concourse.mybir as mybir
from concourse.bass_utils import run_bass_kernel_spmd

F32 = mybir.dt.float32
BF16 = mybir.dt.bfloat16
AF = mybir.ActivationFunctionType
ALU = mybir.AluOpType
AX = mybir.AxisListType

T = 2048
C = 1024
NCT = 8
H = 16
D = 64
M = 64
CH = 128
NCH = T // CH
NP = 8
E = 16
EH = 256
HS = 2048
NF = 3968
GC1 = 0.7978845608028654
GC2 = 0.044715

STAGE_ALL = 99


import os
SERIAL = bool(int(os.environ.get("BK_SERIAL", "0")))


class Prog:
    CENG = ("tensor", "scalar", "vector", "gpsimd", "sync")
    QUEUES = ("sync", "scalar", "gpsimd")

    def __init__(self, nc):
        self.nc = nc
        self.ops = []
        self.count = {e: 0 for e in self.CENG}
        self.dma_count = {q: 0 for q in self.QUEUES}
        self.last_write = {}
        self.readers = {}
        self.prev_me = None

    def _collect(self, reads, writes):
        deps = set()
        rkeys, wkeys = [], []
        for r in reads:
            for k in (r if isinstance(r, list) else [r]):
                k = k if isinstance(k, tuple) else (id(getattr(k, "tensor", k)), None)
                rkeys.append(k)
                lw = self.last_write.get(k)
                if lw is not None:
                    deps.add(lw)
        for w in writes:
            for k in (w if isinstance(w, list) else [w]):
                k = k if isinstance(k, tuple) else (id(getattr(k, "tensor", k)), None)
                wkeys.append(k)
                lw = self.last_write.get(k)
                if lw is not None:
                    deps.add(lw)
                for rd in self.readers.get(k, ()):
                    deps.add(rd)
        return deps, rkeys, wkeys

    def _book(self, me, rkeys, wkeys):
        for k in rkeys:
            self.readers.setdefault(k, []).append(me)
        for k in wkeys:
            self.last_write[k] = me
            self.readers[k] = []

    def op(self, engine, emit, reads=(), writes=(), tag=""):
        deps, rkeys, wkeys = self._collect(reads, writes)
        me = ("c", engine, self.count[engine])
        self.count[engine] += 1
        if SERIAL and self.prev_me is not None:
            deps = {self.prev_me}
        self.prev_me = me
        self.ops.append(dict(engine=engine, emit=emit, deps=deps, me=me, tag=tag, dma=False))
        self._book(me, rkeys, wkeys)
        return me

    NLANES = 8

    def dma(self, queue, out, in_, reads=(), writes=(), tag="", **kw):
        deps, rkeys, wkeys = self._collect(reads, writes)
        me = ("d", queue, self.dma_count[queue])
        self.dma_count[queue] += 1
        if SERIAL and self.prev_me is not None:
            deps = {self.prev_me}
        self.prev_me = me
        emit = lambda eng, out=out, in_=in_, kw=kw: eng.dma_start(out=out, in_=in_, **kw)
        self.ops.append(dict(engine=queue, emit=emit, deps=deps, me=me, tag=tag, dma=True))
        self._book(me, rkeys, wkeys)
        return me

    def emit_all(self, block, sems, qsems):
        per_engine = {e: [] for e in self.CENG}
        for o in self.ops:
            per_engine[o["engine"]].append(o)
        NL = self.NLANES

        def lane_of(dep):
            q, i = dep[1], dep[2]
            return qsems[q][i % NL], 16 * (i // NL + 1)

        def mk_section(ename, ops_here):
            def fn(eng):
                hw = {}
                ndma = 0
                for o in ops_here:
                    if o["dma"] and ndma >= NL:
                        sem, val = qsems[o["engine"]][ndma % NL], 16 * (ndma // NL)
                        if hw.get(id(sem), -1) < val:
                            hw[id(sem)] = val
                            eng.wait_ge(sem, val)
                    for dep in sorted(o["deps"], key=repr):
                        if dep[0] == "c":
                            dep_eng, dep_seq = dep[1], dep[2]
                            if dep_eng == ename == "tensor" and not SERIAL:
                                continue
                            sem, val = sems[dep_eng], dep_seq + 1
                        else:
                            sem, val = lane_of(dep)
                        if hw.get(id(sem), -1) >= val:
                            continue
                        hw[id(sem)] = val
                        eng.wait_ge(sem, val)
                    ins = o["emit"](eng)
                    if o["dma"]:
                        ins.then_inc(qsems[o["engine"]][ndma % NL], 16)
                        ndma += 1
                    else:
                        ins.then_inc(sems[o["engine"]], 1)
                if ename == "gpsimd":
                    for q in self.QUEUES:
                        n = self.dma_count[q]
                        for l in range(NL):
                            cnt = 16 * ((n - l + NL - 1) // NL)
                            if cnt:
                                eng.wait_ge(qsems[q][l], cnt)
            return fn

        for ename in self.CENG:
            if per_engine[ename] or ename == "gpsimd":
                getattr(block, ename)(mk_section(ename, per_engine[ename]))


def prep_consts():
    c = {}
    c["ident_bf"] = np.eye(128, dtype=np.float32).astype(ml_dtypes.bfloat16)
    c["ident_f32"] = np.eye(128, dtype=np.float32)
    ones = np.ones((128, 128), dtype=np.float32)
    c["ones_f32"] = ones
    c["ones_bf"] = ones.astype(ml_dtypes.bfloat16)
    tri = np.triu(np.ones((128, 128), dtype=np.float32))
    c["trimask_bf"] = np.concatenate([tri, tri], axis=1).astype(ml_dtypes.bfloat16)
    return c


def prep_weights(inp):
    w = {}
    w["WqkvT"] = np.ascontiguousarray(np.asarray(inp["L0_attn_Wqkv"]).T)
    w["bqkv"] = np.ascontiguousarray(np.asarray(inp["L0_attn_bqkv"]))
    w["WoutT"] = np.ascontiguousarray(np.asarray(inp["L0_attn_Wout"]).T)
    w["bout"] = np.ascontiguousarray(np.asarray(inp["L0_attn_bout"]))
    w["proj_pair"] = np.ascontiguousarray(np.asarray(inp["L0_attn_proj"]).reshape(NP, 2 * D, M))
    w["w_router"] = np.ascontiguousarray(np.asarray(inp["L0_ffn_w_router"]))
    w["w_block"] = np.ascontiguousarray(np.asarray(inp["L0_ffn_w_block"]))
    w["Wq"] = np.ascontiguousarray(np.asarray(inp["L1_attn_Wq"]))
    w["Wk"] = np.ascontiguousarray(np.asarray(inp["L1_attn_Wk"]))
    w["Wv"] = np.ascontiguousarray(np.asarray(inp["L1_attn_Wv"]))
    w["bq"] = np.ascontiguousarray(np.asarray(inp["L1_attn_bq"]))
    w["bk"] = np.ascontiguousarray(np.asarray(inp["L1_attn_bk"]))
    w["bv"] = np.ascontiguousarray(np.asarray(inp["L1_attn_bv"]))
    rel = np.asarray(inp["L1_attn_rel_emb"])
    stk = np.empty((H, 128, NF), dtype=np.float32)
    for h in range(H):
        r = rel[h]
        stk[h] = np.lib.stride_tricks.as_strided(
            r[127:], shape=(128, NF), strides=(-r.strides[0], r.strides[0]))
    w["rel_stack"] = stk
    w["router_wT"] = np.ascontiguousarray(np.asarray(inp["L1_ffn_router_w"]).T)
    w["corr_bias"] = np.ascontiguousarray(np.asarray(inp["L1_ffn_correction_bias"]))
    w["gate_w"] = np.ascontiguousarray(np.asarray(inp["L1_ffn_gate_w"]))
    w["up_w"] = np.ascontiguousarray(np.asarray(inp["L1_ffn_up_w"]))
    w["down_w"] = np.ascontiguousarray(np.asarray(inp["L1_ffn_down_w"]))
    w["sgwT"] = np.ascontiguousarray(np.asarray(inp["L1_ffn_shared_gate_w"]).T)
    w["suwT"] = np.ascontiguousarray(np.asarray(inp["L1_ffn_shared_up_w"]).T)
    w["sdwT"] = np.ascontiguousarray(np.asarray(inp["L1_ffn_shared_down_w"]).T)
    return w


def build_program(stage=STAGE_ALL, debug=False):
    nc = bass.Bass("TRN2", target_bir_lowering=False)
    P = Prog(nc)
    Hd = {}

    def din(name, shape, dtype=F32):
        Hd[name] = nc.dram_tensor(name, list(shape), dtype, kind="ExternalInput")
        return Hd[name]

    din("x", (T, C))
    din("WqkvT", (C, 3 * C)); din("bqkv", (3 * C,))
    din("WoutT", (C, C)); din("bout", (C,))
    din("proj_pair", (NP, 128, M))
    din("w_router", (C,)); din("w_block", (C, C))
    din("Wq", (C, C)); din("Wk", (C, C)); din("Wv", (C, C))
    din("bq", (C,)); din("bk", (C,)); din("bv", (C,))
    din("rel_stack", (H, 128, NF))
    din("router_wT", (C, 16)); din("corr_bias", (16,))
    din("gate_w", (E, C, EH)); din("up_w", (E, C, EH)); din("down_w", (E, EH, C))
    din("sgwT", (C, HS)); din("suwT", (C, HS)); din("sdwT", (HS, C))
    din("ident_bf", (128, 128), BF16); din("ident_f32", (128, 128))
    din("ones_f32", (128, 128)); din("ones_bf", (128, 128), BF16)
    din("trimask_bf", (128, 256), BF16)
    y = nc.dram_tensor("y", [T, C], F32, kind="ExternalOutput")

    A = nc.alloc_sbuf_tensor
    xT = A("xT", [128, NCT, T], F32)
    xnT = A("xnT", [128, NCT, T], BF16)
    big0 = A("big0", [128, NCT, T], BF16)
    ident_bf = A("c_idb", [128, 128], BF16)
    ident_f32 = A("c_idf", [128, 128], F32)
    ones_f32 = A("c_onf", [128, 128], F32)
    ones_bf = A("c_onb", [128, 128], BF16)
    trimask = A("c_tri", [128, 256], BF16)
    consts = {"ident_bf": ident_bf, "ident_f32": ident_f32, "ones_f32": ones_f32,
              "ones_bf": ones_bf, "trimask_bf": trimask}

    SCR_W = 16384
    scratch = A("scratch", [128, SCR_W], F32)
    scr_f32 = scratch.ap()
    scr_bf = scratch.ap().bitcast(BF16)
    PGW = 512

    def skeys(w0, w1):
        return [(id(scratch), pg) for pg in range(w0 // PGW, (w1 - 1) // PGW + 1)]

    class SV:
        def __init__(self, off_w, n_w, dtype):
            self.off_w, self.n_w, self.dtype = off_w, n_w, dtype
            self.keys = skeys(off_w, off_w + n_w)

        def ap(self, part=(0, 128)):
            if self.dtype == F32:
                a = scr_f32[:, self.off_w:self.off_w + self.n_w]
            else:
                a = scr_bf[:, 2 * self.off_w:2 * self.off_w + 2 * self.n_w]
            if part != (0, 128):
                a = scr_f32[part[0]:part[1], self.off_w:self.off_w + self.n_w] if self.dtype == F32 \
                    else scr_bf[part[0]:part[1], 2 * self.off_w:2 * self.off_w + 2 * self.n_w]
            return a

    def sv(off_w, n_w, dtype=F32):
        assert off_w + n_w <= SCR_W, (off_w, n_w)
        return SV(off_w, n_w, dtype)

    APm = nc.alloc_psum_tensor
    psA = APm("psA", [128, T], F32)
    psB = APm("psB", [128, T], F32)

    def lp(fn):
        def g(eng):
            with nc.allow_low_precision(reason="bf16 normalizer for broadcast matmul (blessed bf16 operand path)"):
                return fn(eng)
        return g

    def pk(pst, banks):
        return [(id(pst), b) for b in banks]

    def bank_of(col_lo, col_hi):
        return list(range(col_lo // 512, (col_hi - 1) // 512 + 1))

    rows = sv(0, 2 * T)
    m_b = sv(2 * T, T)
    rstd_b = sv(3 * T, T)
    logitsT = sv(4 * T, T)
    stg_a = sv(5 * T, T)
    stg_b = sv(6 * T, T)
    xbf_a = sv(7 * T, T // 2, BF16)
    xbf_b = sv(7 * T + T // 2, T // 2, BF16)

    class Arena:
        FULL = [(0, 4 * T), (5 * T, 8 * T)]
        LATE = [(5 * T, 8 * T)]

        def __init__(self, ranges="full"):
            self.RANGES = self.FULL if ranges == "full" else self.LATE
            self.ri = 0
            self.cur = self.RANGES[0][0]

        def sv(self, n_w, dtype=F32, align=16):
            self.cur = (self.cur + align - 1) // align * align
            while self.cur + n_w > self.RANGES[self.ri][1]:
                self.ri += 1
                assert self.ri < len(self.RANGES), f"arena overflow ({n_w} words)"
                self.cur = self.RANGES[self.ri][0]
            v = sv(self.cur, n_w, dtype)
            self.cur += n_w
            return v

    for nm, hh in consts.items():
        P.dma("sync", out=hh[:], in_=Hd[nm].ap(), reads=[Hd[nm]], writes=[hh], tag=f"ldc{nm}")

    xr = Hd["x"].ap().rearrange("(ti p) c -> ti p c", p=128)
    for ti in range(NCH):
        stgv = stg_a if ti % 2 == 0 else stg_b
        P.dma("sync", out=stgv.ap()[:, 0:C], in_=xr[ti],
              reads=[Hd["x"]], writes=[stgv.keys], tag=f"ldx{ti}")
        pst = psA if ti % 2 == 0 else psB
        for ci in range(NCT):
            bank = ci % 4
            col = bank * 512 + (ci // 4) * 128
            dst = pst[:, col:col + 128]
            P.op("tensor",
                 lambda eng, dst=dst, s=stgv, ci=ci: eng.transpose(
                     dst, s.ap()[:, ci * 128:(ci + 1) * 128], ident_f32[:]),
                 reads=[stgv.keys, ident_f32], writes=[pk(pst, [bank])], tag=f"tx{ti}_{ci}")
            P.op("vector",
                 lambda eng, dst=dst, ti=ti, ci=ci: eng.tensor_copy(
                     xT[:, ci, ti * 128:(ti + 1) * 128], dst),
                 reads=[pk(pst, [bank])], writes=[xT], tag=f"txe{ti}_{ci}")

    def layernorm(layer):
        m_row = rows.ap((0, 1))[:, 0:T]
        rstd_row = rows.ap((0, 1))[:, T:2 * T]
        s1, s2 = psA, psB
        for ci in range(NCT):
            sqv = stg_a if ci % 2 == 0 else stg_b
            P.op("scalar",
                 lambda eng, ci=ci, sqv=sqv: eng.square(sqv.ap(), xT[:, ci, :]),
                 reads=[xT], writes=[sqv.keys], tag=f"ln{layer}sq{ci}")
            for tb in range(4):
                P.op("tensor",
                     lambda eng, ci=ci, tb=tb: eng.matmul(
                         s1[0:1, tb * 512:(tb + 1) * 512], ones_f32[:, 0:1],
                         xT[:, ci, tb * 512:(tb + 1) * 512],
                         start=(ci == 0), stop=(ci == NCT - 1)),
                     reads=[xT, ones_f32], writes=[pk(s1, [tb])], tag=f"ln{layer}s1_{ci}_{tb}")
                P.op("tensor",
                     lambda eng, ci=ci, tb=tb, sqv=sqv: eng.matmul(
                         s2[0:1, tb * 512:(tb + 1) * 512], ones_f32[:, 0:1],
                         sqv.ap()[:, tb * 512:(tb + 1) * 512],
                         start=(ci == 0), stop=(ci == NCT - 1)),
                     reads=[sqv.keys, ones_f32], writes=[pk(s2, [tb])], tag=f"ln{layer}s2_{ci}_{tb}")
        var_row = stg_a.ap((0, 1))[:, 0:T]
        tmp_row = stg_b.ap((0, 1))[:, 0:T]
        P.op("vector", lambda eng: eng.tensor_scalar_mul(m_row, s1[0:1, :], 1.0 / C),
             reads=[pk(s1, range(4))], writes=[rows.keys], tag=f"ln{layer}m")
        P.op("vector", lambda eng: eng.tensor_scalar_mul(var_row, s2[0:1, :], 1.0 / C),
             reads=[pk(s2, range(4))], writes=[stg_a.keys], tag=f"ln{layer}ms")
        P.op("vector", lambda eng: eng.scalar_tensor_tensor(
            tmp_row, m_row, -1.0, m_row, ALU.mult, ALU.mult),
             reads=[rows.keys], writes=[stg_b.keys], tag=f"ln{layer}negm2")
        P.op("vector", lambda eng: eng.tensor_add(var_row, var_row, tmp_row),
             reads=[stg_a.keys, stg_b.keys], writes=[stg_a.keys], tag=f"ln{layer}var")
        P.op("vector", lambda eng: eng.tensor_scalar_add(var_row, var_row, 1e-5),
             reads=[stg_a.keys], writes=[stg_a.keys], tag=f"ln{layer}eps")
        P.op("scalar", lambda eng: eng.activation(rstd_row, var_row, AF.Ln),
             reads=[stg_a.keys], writes=[rows.keys], tag=f"ln{layer}ln")
        P.op("scalar", lambda eng: eng.activation(rstd_row, rstd_row, AF.Exp, scale=-0.5),
             reads=[rows.keys], writes=[rows.keys], tag=f"ln{layer}ex")
        P.op("vector", lambda eng: eng.tensor_mul(tmp_row, rstd_row, rstd_row),
             reads=[rows.keys], writes=[stg_b.keys], tag=f"ln{layer}n1")
        P.op("vector", lambda eng: eng.tensor_mul(tmp_row, tmp_row, var_row),
             reads=[stg_a.keys, stg_b.keys], writes=[stg_b.keys], tag=f"ln{layer}n2")
        P.op("vector", lambda eng: eng.tensor_scalar(
            tmp_row, tmp_row, -0.5, 1.5, ALU.mult, ALU.add),
             reads=[stg_b.keys], writes=[stg_b.keys], tag=f"ln{layer}n3")
        P.op("vector", lambda eng: eng.tensor_mul(rstd_row, rstd_row, tmp_row),
             reads=[rows.keys, stg_b.keys], writes=[rows.keys], tag=f"ln{layer}n4")
        for tb in range(4):
            P.op("tensor",
                 lambda eng, tb=tb: eng.matmul(
                     s1[:, tb * 512:(tb + 1) * 512], ones_f32[0:1, :],
                     m_row[:, tb * 512:(tb + 1) * 512], start=True, stop=True),
                 reads=[rows.keys, ones_f32], writes=[pk(s1, [tb])], tag=f"ln{layer}bm{tb}")
            P.op("tensor",
                 lambda eng, tb=tb: eng.matmul(
                     s2[:, tb * 512:(tb + 1) * 512], ones_f32[0:1, :],
                     rstd_row[:, tb * 512:(tb + 1) * 512], start=True, stop=True),
                 reads=[rows.keys, ones_f32], writes=[pk(s2, [tb])], tag=f"ln{layer}br{tb}")
        P.op("vector", lambda eng: eng.tensor_copy(m_b.ap(), s1[:, :]),
             reads=[pk(s1, range(4))], writes=[m_b.keys], tag=f"ln{layer}me")
        P.op("vector", lambda eng: eng.tensor_copy(rstd_b.ap(), s2[:, :]),
             reads=[pk(s2, range(4))], writes=[rstd_b.keys], tag=f"ln{layer}re")
        for ci in range(NCT):
            P.op("vector",
                 lambda eng, ci=ci: eng.tensor_sub(stg_b.ap(), xT[:, ci, :], m_b.ap()),
                 reads=[xT, m_b.keys], writes=[stg_b.keys], tag=f"ln{layer}sb{ci}")
            P.op("vector",
                 lambda eng, ci=ci: eng.tensor_mul(xnT[:, ci, :], stg_b.ap(), rstd_b.ap()),
                 reads=[stg_b.keys, rstd_b.keys], writes=[xnT], tag=f"ln{layer}xn{ci}")

    def router_l0(ar):
        wr = ar.sv(8)
        for j in range(NCT):
            P.dma("sync", out=wr.ap()[:, j:j + 1],
                  in_=Hd["w_router"].ap()[j * 128:(j + 1) * 128].unsqueeze(1),
                  reads=[Hd["w_router"]], writes=[wr.keys], tag=f"ldwr{j}")
        for ci in range(NCT):
            for tb in range(4):
                P.op("tensor",
                     lambda eng, ci=ci, tb=tb: eng.matmul(
                         psA[0:1, tb * 512:(tb + 1) * 512], wr.ap()[:, ci:ci + 1],
                         xT[:, ci, tb * 512:(tb + 1) * 512],
                         start=(ci == 0), stop=(ci == NCT - 1)),
                     reads=[wr.keys, xT], writes=[pk(psA, [tb])], tag=f"rs0_{ci}_{tb}")
        for ci in range(NCT):
            P.op("tensor",
                 lambda eng, ci=ci: eng.matmul(
                     psB[0:1, 0:1], wr.ap()[:, ci:ci + 1], ones_f32[:, 0:1],
                     start=(ci == 0), stop=(ci == NCT - 1)),
                 reads=[wr.keys, ones_f32], writes=[pk(psB, [0])], tag=f"rsw{ci}")
        sumw_sb = ar.sv(1)
        P.op("vector", lambda eng: eng.tensor_copy(sumw_sb.ap((0, 1)), psB[0:1, 0:1]),
             reads=[pk(psB, [0])], writes=[sumw_sb.keys], tag="rswe")
        sc = logitsT.ap((0, 1))[:, 0:T]
        m_row = rows.ap((0, 1))[:, 0:T]
        rstd_row = rows.ap((0, 1))[:, T:2 * T]
        P.op("vector", lambda eng: eng.tensor_single_scalar(
            sc, m_row, sumw_sb.ap((0, 1)), ALU.mult),
             reads=[rows.keys, sumw_sb.keys], writes=[logitsT.keys], tag="rt1")
        P.op("vector", lambda eng: eng.tensor_sub(sc, psA[0:1, :], sc),
             reads=[pk(psA, range(4)), logitsT.keys], writes=[logitsT.keys], tag="rt2")
        P.op("vector", lambda eng: eng.tensor_mul(sc, sc, rstd_row),
             reads=[logitsT.keys, rows.keys], writes=[logitsT.keys], tag="rt3")

    def l0_ffn(ar):
        sc_row = logitsT.ap((0, 1))[:, 0:T]
        scr_d = nc.dram_tensor("scr_row_d", [T], F32, kind="Internal")
        plane = ar.sv(16)
        cnt = ar.sv(16)
        maskpl = ar.sv(16)
        S_b = ar.sv(T)
        M_b = ar.sv(T)
        fs1 = ar.sv(T)
        fs2 = ar.sv(T)
        fs3 = ar.sv(T)
        wblk = [ar.sv(512, BF16), ar.sv(512, BF16)]
        P.dma("scalar", out=scr_d.ap().unsqueeze(0), in_=sc_row,
              reads=[logitsT.keys], writes=[scr_d], tag="ffn_scout")
        P.dma("scalar", out=plane.ap()[:, 0:16],
              in_=scr_d.ap().rearrange("(p j) -> p j", j=16),
              reads=[scr_d], writes=[plane.keys], tag="ffn_plane")
        for tb in range(4):
            P.op("tensor", lambda eng, tb=tb: eng.matmul(
                psA[:, tb * 512:(tb + 1) * 512], ones_f32[0:1, :],
                sc_row[:, tb * 512:(tb + 1) * 512], start=True, stop=True),
                 reads=[logitsT.keys, ones_f32], writes=[pk(psA, [tb])], tag=f"ffn_sb{tb}")
        P.op("vector", lambda eng: eng.tensor_copy(S_b.ap(), psA[:, :]),
             reads=[pk(psA, range(4))], writes=[S_b.keys], tag="ffn_sbe")
        for j in range(16):
            P.op("vector", lambda eng, j=j: eng.tensor_scalar(
                fs1.ap(), S_b.ap(), plane.ap()[:, j:j + 1], 0.0, ALU.is_gt, ALU.add,
                accum_out=cnt.ap()[:, j:j + 1]),
                 reads=[S_b.keys, plane.keys], writes=[fs1.keys, cnt.keys], tag=f"ffn_cnt{j}")
        KCAP = int(T * 0.75)
        P.op("vector", lambda eng: eng.tensor_single_scalar(
            maskpl.ap()[:, 0:16], cnt.ap()[:, 0:16], float(KCAP), ALU.is_lt),
             reads=[cnt.keys], writes=[maskpl.keys], tag="ffn_mask")
        P.dma("scalar", out=scr_d.ap().rearrange("(p j) -> p j", j=16),
              in_=maskpl.ap()[:, 0:16], reads=[maskpl.keys], writes=[scr_d], tag="ffn_mkout")
        P.dma("scalar", out=fs2.ap((0, 1))[:, 0:T], in_=scr_d.ap().unsqueeze(0),
              reads=[scr_d], writes=[fs2.keys], tag="ffn_mkin")
        for tb in range(4):
            P.op("tensor", lambda eng, tb=tb: eng.matmul(
                psA[:, tb * 512:(tb + 1) * 512], ones_f32[0:1, :],
                fs2.ap((0, 1))[:, tb * 512:(tb + 1) * 512], start=True, stop=True),
                 reads=[fs2.keys, ones_f32], writes=[pk(psA, [tb])], tag=f"ffn_mb{tb}")
        P.op("vector", lambda eng: eng.tensor_copy(M_b.ap(), psA[:, :]),
             reads=[pk(psA, range(4))], writes=[M_b.keys], tag="ffn_mbe")
        wblk_r = Hd["w_block"].ap().rearrange("(ci p) (eo q) -> eo p ci q", p=128, q=128)
        for eo in range(NCT):
            wb = wblk[eo % 2]
            P.dma("gpsimd", out=wb.ap().rearrange("p (ci q) -> p ci q", q=128),
                  in_=wblk_r[eo], reads=[Hd["w_block"]], writes=[wb.keys], tag=f"ffn_w{eo}")
            ps = psA if eo % 2 == 0 else psB
            for ci in range(NCT):
                for tb in range(4):
                    P.op("tensor", lambda eng, ps=ps, ci=ci, tb=tb, wb=wb: eng.matmul(
                        ps[:, tb * 512:(tb + 1) * 512],
                        wb.ap().rearrange("p (ci q) -> p ci q", q=128)[:, ci, :],
                        xnT[:, ci, tb * 512:(tb + 1) * 512],
                        start=(ci == 0), stop=(ci == NCT - 1)),
                         reads=[wb.keys, xnT], writes=[pk(ps, [tb])], tag=f"ffn_p{eo}_{ci}_{tb}")
            P.op("scalar", lambda eng, ps=ps: eng.square(fs1.ap(), ps[:, :]),
                 reads=[pk(ps, range(4))], writes=[fs1.keys], tag=f"ffn_g1_{eo}")
            P.op("vector", lambda eng, ps=ps: eng.scalar_tensor_tensor(
                fs2.ap(), fs1.ap(), GC1 * GC2, ps[:, :], ALU.mult, ALU.mult),
                 reads=[fs1.keys, pk(ps, range(4))], writes=[fs2.keys], tag=f"ffn_g2_{eo}")
            P.op("vector", lambda eng, ps=ps: eng.scalar_tensor_tensor(
                fs1.ap(), ps[:, :], GC1, fs2.ap(), ALU.mult, ALU.add),
                 reads=[fs2.keys, pk(ps, range(4))], writes=[fs1.keys], tag=f"ffn_g3_{eo}")
            P.op("scalar", lambda eng: eng.activation(fs2.ap(), fs1.ap(), AF.Tanh),
                 reads=[fs1.keys], writes=[fs2.keys], tag=f"ffn_g4_{eo}")
            P.op("vector", lambda eng: eng.tensor_scalar(
                fs1.ap(), fs2.ap(), 0.5, 0.5, ALU.mult, ALU.add),
                 reads=[fs2.keys], writes=[fs1.keys], tag=f"ffn_g5_{eo}")
            P.op("vector", lambda eng, ps=ps: eng.tensor_mul(fs2.ap(), fs1.ap(), ps[:, :]),
                 reads=[fs1.keys, pk(ps, range(4))], writes=[fs2.keys], tag=f"ffn_g6_{eo}")
            P.op("vector", lambda eng, eo=eo: eng.tensor_sub(fs3.ap(), fs2.ap(), xnT[:, eo, :]),
                 reads=[fs2.keys, xnT], writes=[fs3.keys], tag=f"ffn_e1_{eo}")
            P.op("vector", lambda eng: eng.tensor_mul(fs3.ap(), fs3.ap(), M_b.ap()),
                 reads=[fs3.keys, M_b.keys], writes=[fs3.keys], tag=f"ffn_e2_{eo}")
            P.op("vector", lambda eng, eo=eo: eng.tensor_add(xT[:, eo, :], xT[:, eo, :], fs3.ap()),
                 reads=[fs3.keys, xT], writes=[xT], tag=f"ffn_e3_{eo}")
            P.op("vector", lambda eng, eo=eo: eng.tensor_add(xT[:, eo, :], xT[:, eo, :], xnT[:, eo, :]),
                 reads=[xnT, xT], writes=[xT], tag=f"ffn_e4_{eo}")

    def l0_attn(ar):
        qTp = ar.sv(T // 2, BF16)
        kTp = ar.sv(T // 2, BF16)
        vstg = ar.sv(T)
        vnat = ar.sv(T // 2, BF16)
        qp_sb = ar.sv(T, BF16)
        kpn_sb = ar.sv(T // 2, BF16)
        kp_sb = ar.sv(T, BF16)
        _wq = ar.sv(512, BF16); _wk = ar.sv(512, BF16); _wv = ar.sv(512, BF16)
        wq = [_wq, _wq]
        wk = [_wk, _wk]
        wv = [_wv, _wv]
        projs = [ar.sv(32, BF16), ar.sv(32, BF16)]
        bq24 = ar.sv(24)
        aw_sb = ar.sv(128, BF16)
        kv32 = ar.sv(128)
        kvbf = ar.sv(64, BF16)
        kst32 = ar.sv(2)
        kstbf = ar.sv(2, BF16)
        nr_sb = ar.sv(512)
        bc_sb = ar.sv(512)
        kst_sb = ar.sv(32)
        _wo = ar.sv(512, BF16)
        wout = [_wo, _wo]
        bo8 = ar.sv(8)
        tout = vstg

        for j in range(24):
            P.dma("sync", out=bq24.ap()[:, j:j + 1],
                  in_=Hd["bqkv"].ap()[j * 128:(j + 1) * 128].unsqueeze(1),
                  reads=[Hd["bqkv"]], writes=[bq24.keys], tag=f"l0a_bq{j}")
        for j in range(8):
            P.dma("sync", out=bo8.ap()[:, j:j + 1],
                  in_=Hd["bout"].ap()[j * 128:(j + 1) * 128].unsqueeze(1),
                  reads=[Hd["bout"]], writes=[bo8.keys], tag=f"l0a_bo{j}")
        wr_all = Hd["WqkvT"].ap().rearrange("(ci p) e -> p ci e", p=128)

        def wtile(sv_):
            return sv_.ap().rearrange("p (ci q) -> p ci q", q=128)

        for hp in range(NP):
            bi = hp % 2
            for mat, wsv, col0 in (("q", wq[bi], hp * 128), ("k", wk[bi], C + hp * 128),
                                   ("v", wv[bi], 2 * C + hp * 128)):
                P.dma("gpsimd", out=wtile(wsv), in_=wr_all[:, :, col0:col0 + 128],
                      reads=[Hd["WqkvT"]], writes=[wsv.keys], tag=f"l0a_w{mat}{hp}")
            P.dma("gpsimd", out=projs[bi].ap(), in_=Hd["proj_pair"].ap()[hp],
                  reads=[Hd["proj_pair"]], writes=[projs[bi].keys], tag=f"l0a_pj{hp}")
            for mat, wsv, ps, outsv, seg in (("q", wq[bi], psA, qTp, hp),
                                             ("k", wk[bi], psB, kTp, 8 + hp),
                                             ("v", wv[bi], psA, vstg, 16 + hp)):
                for ci in range(NCT):
                    for tb in range(4):
                        P.op("tensor", lambda eng, ps=ps, wsv=wsv, ci=ci, tb=tb: eng.matmul(
                            ps[:, tb * 512:(tb + 1) * 512], wtile(wsv)[:, ci, :],
                            xnT[:, ci, tb * 512:(tb + 1) * 512],
                            start=(ci == 0), stop=(ci == NCT - 1)),
                             reads=[wsv.keys, xnT], writes=[pk(ps, [tb])],
                             tag=f"l0a_{mat}{hp}_{ci}_{tb}")
                P.op("scalar", lambda eng, ps=ps, outsv=outsv, seg=seg: eng.activation(
                    outsv.ap(), ps[:, :], AF.Identity, bias=bq24.ap()[:, seg:seg + 1]),
                     reads=[pk(ps, range(4)), bq24.keys], writes=[outsv.keys],
                     tag=f"l0a_{mat}e{hp}")
            for ti in range(16):
                bank = ti % 4
                col = bank * 512 + ((ti // 4) % 4) * 128
                P.op("tensor", lambda eng, ti=ti, col=col: eng.transpose(
                    psB[:, col:col + 128], vstg.ap()[:, ti * 128:(ti + 1) * 128], ident_f32[:]),
                     reads=[vstg.keys, ident_f32], writes=[pk(psB, [bank])], tag=f"l0a_vt{hp}_{ti}")
                P.op("vector", lambda eng, ti=ti, col=col: eng.tensor_copy(
                    vnat.ap().rearrange("p (a b) -> p a b", b=128)[:, ti, :],
                    psB[:, col:col + 128]),
                     reads=[pk(psB, [ti % 4])], writes=[vnat.keys], tag=f"l0a_vte{hp}_{ti}")
            for nm, src, dst in (("qp", qTp, qp_sb), ("kp", kTp, kp_sb)):
                for h2 in range(2):
                    ps = psA if h2 == 0 else psB
                    r0, r1 = h2 * 64, h2 * 64 + 64
                    for tb in range(4):
                        P.op("tensor", lambda eng, ps=ps, src=src, r0=r0, r1=r1, tb=tb, pj=projs[bi]: eng.matmul(
                            ps[0:64, tb * 512:(tb + 1) * 512], pj.ap()[r0:r1, :],
                            src.ap()[r0:r1, tb * 512:(tb + 1) * 512], start=True, stop=True),
                             reads=[projs[bi].keys, src.keys], writes=[pk(ps, [tb])],
                             tag=f"l0a_{nm}{hp}_{h2}_{tb}")
                for h2 in range(2):
                    ps = psA if h2 == 0 else psB
                    P.op("scalar", lambda eng, ps=ps, dst=dst, h2=h2: eng.activation(
                        dst.ap((0, 64))[:, h2 * 2048:(h2 + 1) * 2048], ps[0:64, :], AF.Relu),
                         reads=[pk(ps, range(4))], writes=[dst.keys], tag=f"l0a_{nm}e{hp}_{h2}")
            for h2 in range(2):
                r0, r1 = h2 * 64, h2 * 64 + 64
                for ch in range(NCH):
                    col = h2 * 1024 + ch * 64
                    bank = col // 512
                    P.op("tensor", lambda eng, r0=r0, r1=r1, ch=ch, col=col, pj=projs[bi]: eng.matmul(
                        psA[:, col:col + 64], kTp.ap()[r0:r1, ch * 128:(ch + 1) * 128],
                        pj.ap()[r0:r1, :], start=True, stop=True),
                         reads=[kTp.keys, projs[bi].keys], writes=[pk(psA, [bank])],
                         tag=f"l0a_kpn{hp}_{h2}_{ch}")
            P.op("scalar", lambda eng: eng.activation(kpn_sb.ap(), psA[:, :], AF.Relu),
                 reads=[pk(psA, range(4))], writes=[kpn_sb.keys], tag=f"l0a_kpne{hp}")
            aw_all = sv(qTp.off_w, 2048, BF16)
            kvch_sb = vstg
            qp_r = qp_sb.ap((0, 64))
            kp_r = kp_sb.ap((0, 64))
            kpn_r = kpn_sb.ap()
            vnat_r = vnat.ap().rearrange("p (a b) -> p a b", b=128)
            kvbf_r = kvbf.ap((0, 64))
            kstbf_r = kstbf.ap((0, 64))
            for ch in range(NCH):
                pb = ch % 2
                for h2 in range(2):
                    P.op("tensor", lambda eng, h2=h2, ch=ch, pb=pb: eng.matmul(
                        psB[:, pb * 512 + h2 * 128: pb * 512 + (h2 + 1) * 128],
                        kp_r[:, h2 * 2048 + ch * 128: h2 * 2048 + (ch + 1) * 128],
                        qp_r[:, h2 * 2048 + ch * 128: h2 * 2048 + (ch + 1) * 128],
                        start=True, stop=True),
                         reads=[kp_sb.keys, qp_sb.keys], writes=[pk(psB, [pb])],
                         tag=f"pf_aw{hp}_{ch}_{h2}")
                P.op("vector", lambda eng, ch=ch, pb=pb: eng.scalar_tensor_tensor(
                    aw_all.ap()[:, ch * 256:(ch + 1) * 256], psB[:, pb * 512:pb * 512 + 256],
                    1.0, trimask[:], ALU.mult, ALU.mult),
                     reads=[pk(psB, [pb]), trimask], writes=[aw_all.keys], tag=f"pf_awm{hp}_{ch}")
            for ch in range(NCH):
                bank = ch // 4
                col = bank * 512 + (ch % 4) * 128
                for h2 in range(2):
                    P.op("tensor", lambda eng, h2=h2, ch=ch, col=col: eng.matmul(
                        psB[0:64, col + h2 * 64: col + (h2 + 1) * 64],
                        kpn_r[:, h2 * 1024 + ch * 64: h2 * 1024 + (ch + 1) * 64],
                        vnat_r[:, ch, h2 * 64:(h2 + 1) * 64], start=True, stop=True),
                         reads=[kpn_sb.keys, vnat.keys], writes=[pk(psB, [bank])],
                         tag=f"pf_kv{hp}_{ch}_{h2}")
            P.op("vector", lambda eng: eng.tensor_copy(kvch_sb.ap((0, 64)), psB[0:64, :]),
                 reads=[pk(psB, range(4))], writes=[kvch_sb.keys], tag=f"pf_kve{hp}")
            for ch in range(NCH):
                for h2 in range(2):
                    P.op("tensor", lambda eng, h2=h2, ch=ch: eng.matmul(
                        psA[0:64, 1536 + ch * 2 + h2: 1537 + ch * 2 + h2],
                        kpn_r[:, h2 * 1024 + ch * 64: h2 * 1024 + (ch + 1) * 64],
                        ones_bf[:, 0:1], start=True, stop=True),
                         reads=[kpn_sb.keys, ones_bf], writes=[pk(psA, [3])],
                         tag=f"pf_ks{hp}_{ch}_{h2}")
            P.op("vector", lambda eng: eng.tensor_copy(kst_sb.ap((0, 64)), psA[0:64, 1536:1568]),
                 reads=[pk(psA, [3])], writes=[kst_sb.keys], tag=f"pf_kse{hp}")
            P.op("vector", lambda eng: eng.memset(kv32.ap((0, 64)), 0.0),
                 reads=[], writes=[kv32.keys], tag=f"l0a_kv0{hp}")
            P.op("vector", lambda eng: eng.memset(kst32.ap((0, 64)), 0.0),
                 reads=[], writes=[kst32.keys], tag=f"l0a_ks0{hp}")
            P.op("scalar", lambda eng: eng.copy(kvbf.ap((0, 64)), kv32.ap((0, 64))),
                 reads=[kv32.keys], writes=[kvbf.keys], tag=f"l0a_kvb0{hp}")
            P.op("scalar", lambda eng: eng.copy(kstbf.ap((0, 64))[:, 0:2], kst32.ap((0, 64))[:, 0:2]),
                 reads=[kst32.keys], writes=[kstbf.keys], tag=f"l0a_ksb0{hp}")
            for ch in range(NCH):
                pb = ch % 2
                cb0 = pb * 512
                nb0 = 1024 + pb * 512
                bb0 = pb * 512
                for h2 in range(2):
                    if ch > 0:
                        P.op("tensor", lambda eng, h2=h2, ch=ch, cb0=cb0: eng.matmul(
                            psA[0:64, cb0 + h2 * 128: cb0 + (h2 + 1) * 128],
                            kvbf_r[:, h2 * 64:(h2 + 1) * 64],
                            qp_r[:, h2 * 2048 + ch * 128: h2 * 2048 + (ch + 1) * 128],
                            start=True, stop=False),
                             reads=[kvbf.keys, qp_sb.keys], writes=[pk(psA, [cb0 // 512])],
                             tag=f"pf_cxh{hp}_{ch}_{h2}")
                    P.op("tensor", lambda eng, h2=h2, ch=ch, cb0=cb0: eng.matmul(
                        psA[0:64, cb0 + h2 * 128: cb0 + (h2 + 1) * 128],
                        vnat_r[:, ch, h2 * 64:(h2 + 1) * 64],
                        aw_all.ap()[:, ch * 256 + h2 * 128: ch * 256 + (h2 + 1) * 128],
                        start=(ch == 0), stop=True),
                         reads=[vnat.keys, aw_all.keys], writes=[pk(psA, [cb0 // 512])],
                         tag=f"pf_cxi{hp}_{ch}_{h2}")
                    if ch > 0:
                        P.op("tensor", lambda eng, h2=h2, ch=ch, nb0=nb0: eng.matmul(
                            psA[0:1, nb0 + h2 * 128: nb0 + (h2 + 1) * 128],
                            kstbf_r[:, h2:h2 + 1],
                            qp_r[:, h2 * 2048 + ch * 128: h2 * 2048 + (ch + 1) * 128],
                            start=True, stop=False),
                             reads=[kstbf.keys, qp_sb.keys], writes=[pk(psA, [nb0 // 512])],
                             tag=f"pf_nh{hp}_{ch}_{h2}")
                    P.op("tensor", lambda eng, h2=h2, ch=ch, nb0=nb0: eng.matmul(
                        psA[0:1, nb0 + h2 * 128: nb0 + (h2 + 1) * 128], ones_bf[:, 0:1],
                        aw_all.ap()[:, ch * 256 + h2 * 128: ch * 256 + (h2 + 1) * 128],
                        start=(ch == 0), stop=True),
                         reads=[ones_bf, aw_all.keys], writes=[pk(psA, [nb0 // 512])],
                         tag=f"pf_ni{hp}_{ch}_{h2}")
                if ch < NCH - 1:
                    P.op("vector", lambda eng, ch=ch: eng.tensor_add(
                        kv32.ap((0, 64)), kv32.ap((0, 64)),
                        kvch_sb.ap((0, 64))[:, ch * 128:(ch + 1) * 128]),
                         reads=[kvch_sb.keys, kv32.keys], writes=[kv32.keys], tag=f"pf_kva{hp}_{ch}")
                    P.op("vector", lambda eng, ch=ch: eng.tensor_add(
                        kst32.ap((0, 64))[:, 0:2], kst32.ap((0, 64))[:, 0:2],
                        kst_sb.ap((0, 64))[:, ch * 2:(ch + 1) * 2]),
                         reads=[kst_sb.keys, kst32.keys], writes=[kst32.keys], tag=f"pf_ksa{hp}_{ch}")
                    P.op("scalar", lambda eng: eng.copy(kvbf.ap((0, 64)), kv32.ap((0, 64))),
                         reads=[kv32.keys], writes=[kvbf.keys], tag=f"pf_kvb{hp}_{ch}")
                    P.op("scalar", lambda eng: eng.copy(kstbf_r[:, 0:2], kst32.ap((0, 64))[:, 0:2]),
                         reads=[kst32.keys], writes=[kstbf.keys], tag=f"pf_ksb{hp}_{ch}")
                nrv = nr_sb.ap((0, 1))[:, pb * 256:pb * 256 + 256]
                nrb = scr_bf[0:1, 2 * nr_sb.off_w + pb * 512: 2 * nr_sb.off_w + pb * 512 + 256]
                P.op("vector", lambda eng, nb0=nb0, nrv=nrv: eng.tensor_scalar_add(
                    nrv, psA[0:1, nb0:nb0 + 256], 1e-6),
                     reads=[pk(psA, [nb0 // 512])], writes=[nr_sb.keys], tag=f"pf_ne{hp}_{ch}")
                P.op("vector", lp(lambda eng, nrv=nrv, nrb=nrb: eng.reciprocal(nrb, nrv)),
                     reads=[nr_sb.keys], writes=[nr_sb.keys], tag=f"pf_nr{hp}_{ch}")
                P.op("tensor", lambda eng, bb0=bb0, nrb=nrb: eng.matmul(
                    psB[0:64, bb0 + 256:bb0 + 512], ones_bf[0:1, 0:64], nrb,
                    start=True, stop=True),
                     reads=[nr_sb.keys, ones_bf], writes=[pk(psB, [bb0 // 512])], tag=f"pf_nb{hp}_{ch}")
                P.op("vector", lambda eng, bb0=bb0, pb=pb: eng.tensor_copy(
                    bc_sb.ap((0, 64))[:, pb * 256:pb * 256 + 256], psB[0:64, bb0 + 256:bb0 + 512]),
                     reads=[pk(psB, [bb0 // 512])], writes=[bc_sb.keys], tag=f"pf_nbe{hp}_{ch}")
                P.op("vector", lambda eng, hp=hp, ch=ch, cb0=cb0, pb=pb: eng.tensor_mul(
                    big0[0:64, hp, ch * 128:(ch + 1) * 128], psA[0:64, cb0:cb0 + 128],
                    bc_sb.ap((0, 64))[:, pb * 256:pb * 256 + 128]),
                     reads=[pk(psA, [cb0 // 512]), bc_sb.keys], writes=[big0], tag=f"pf_o0{hp}_{ch}")
                P.op("vector", lambda eng, hp=hp, ch=ch, cb0=cb0, pb=pb: eng.tensor_mul(
                    big0[64:128, hp, ch * 128:(ch + 1) * 128], psA[0:64, cb0 + 128:cb0 + 256],
                    bc_sb.ap((0, 64))[:, pb * 256 + 128:pb * 256 + 256]),
                     reads=[pk(psA, [cb0 // 512]), bc_sb.keys], writes=[big0], tag=f"pf_o1{hp}_{ch}")
        wout_r = Hd["WoutT"].ap().rearrange("(ci p) (co q) -> co p ci q", p=128, q=128)
        for co in range(NCT):
            wb = wout[co % 2]
            P.dma("gpsimd", out=wtile(wb), in_=wout_r[co],
                  reads=[Hd["WoutT"]], writes=[wb.keys], tag=f"l0a_wo{co}")
            ps = psA if co % 2 == 0 else psB
            for ci in range(NCT):
                for tb in range(4):
                    P.op("tensor", lambda eng, ps=ps, wb=wb, ci=ci, tb=tb, co=co: eng.matmul(
                        ps[:, tb * 512:(tb + 1) * 512], wtile(wb)[:, ci, :],
                        big0[:, ci, tb * 512:(tb + 1) * 512],
                        start=(ci == 0), stop=(ci == NCT - 1)),
                         reads=[wb.keys, big0], writes=[pk(ps, [tb])], tag=f"l0a_o{co}_{ci}_{tb}")
            P.op("scalar", lambda eng, ps=ps, co=co: eng.activation(
                tout.ap(), ps[:, :], AF.Identity, bias=bo8.ap()[:, co:co + 1]),
                 reads=[pk(ps, range(4)), bo8.keys], writes=[tout.keys], tag=f"l0a_oe{co}")
            P.op("vector", lambda eng, co=co: eng.tensor_add(
                xT[:, co, :], xT[:, co, :], tout.ap()),
                 reads=[tout.keys, xT], writes=[xT], tag=f"l0a_oa{co}")

    def router_l1(ar):
        rw = ar.sv(128)
        sumw = ar.sv(16)
        t16 = ar.sv(T)
        P.dma("sync", out=rw.ap().rearrange("p (ci e) -> p ci e", e=16),
              in_=Hd["router_wT"].ap().rearrange("(ci p) e -> p ci e", p=128),
              reads=[Hd["router_wT"]], writes=[rw.keys], tag="r1_w")
        rw_r = rw.ap().rearrange("p (ci e) -> p ci e", e=16)
        for ci in range(NCT):
            for tb in range(4):
                P.op("tensor", lambda eng, ci=ci, tb=tb: eng.matmul(
                    psA[0:16, tb * 512:(tb + 1) * 512], rw_r[:, ci, :],
                    xT[:, ci, tb * 512:(tb + 1) * 512],
                    start=(ci == 0), stop=(ci == NCT - 1)),
                     reads=[rw.keys, xT], writes=[pk(psA, [tb])], tag=f"r1_s0_{ci}_{tb}")
            P.op("tensor", lambda eng, ci=ci: eng.matmul(
                psB[0:16, 0:1], rw_r[:, ci, :], ones_f32[:, 0:1],
                start=(ci == 0), stop=(ci == NCT - 1)),
                 reads=[rw.keys, ones_f32], writes=[pk(psB, [0])], tag=f"r1_sw{ci}")
        P.op("vector", lambda eng: eng.tensor_copy(sumw.ap((0, 16))[:, 0:1], psB[0:16, 0:1]),
             reads=[pk(psB, [0])], writes=[sumw.keys], tag="r1_swe")
        lg = logitsT.ap((0, 16))[:, 0:T]
        P.op("vector", lambda eng: eng.tensor_single_scalar(
            t16.ap((0, 16)), m_b.ap((0, 16)), sumw.ap((0, 16))[:, 0:1], ALU.mult),
             reads=[m_b.keys, sumw.keys], writes=[t16.keys], tag="r1_t1")
        P.op("vector", lambda eng: eng.tensor_sub(lg, psA[0:16, :], t16.ap((0, 16))),
             reads=[pk(psA, range(4)), t16.keys], writes=[logitsT.keys], tag="r1_t2")
        P.op("vector", lambda eng: eng.tensor_mul(lg, lg, rstd_b.ap((0, 16))),
             reads=[logitsT.keys, rstd_b.keys], writes=[logitsT.keys], tag="r1_t3")

    def l1_attn(ar):
        expstk = [ar.sv(1984, BF16), ar.sv(1984, BF16)]
        kTp = [ar.sv(T // 2, BF16), ar.sv(T // 2, BF16)]
        vnat0 = ar.sv(1088, BF16)
        vstg = ar.sv(512)
        ab0 = ar.sv(512, BF16)
        vnat = [vnat0, ar.sv(1088, BF16)]
        attn_bf = [ab0, ar.sv(512, BF16)]
        t64 = ar.sv(1024)
        bc_sb = ar.sv(1024)
        wq_t = ar.sv(512, BF16)
        wk = [ar.sv(512, BF16), wq_t]
        wv = [ar.sv(512, BF16), ar.sv(512, BF16)]
        bseg = ar.sv(24)
        bqs = ar.sv(8)

        for nm, j0 in (("bq", 0), ("bk", 8), ("bv", 16)):
            for j in range(8):
                P.dma("sync", out=bseg.ap()[:, j0 + j:j0 + j + 1],
                      in_=Hd[nm].ap()[j * 128:(j + 1) * 128].unsqueeze(1),
                      reads=[Hd[nm]], writes=[bseg.keys], tag=f"l1a_{nm}{j}")
        P.op("vector", lambda eng: eng.tensor_scalar_mul(bqs.ap(), bseg.ap()[:, 0:8], 0.125),
             reads=[bseg.keys], writes=[bqs.keys], tag="l1a_bqs")

        def wtile(sv_):
            return sv_.ap().rearrange("p (ci q) -> p ci q", q=128)

        wq_r = Hd["Wq"].ap().rearrange("(ci p) (hp q) -> hp p ci q", p=128, q=128)
        wk_r = Hd["Wk"].ap().rearrange("(ci p) (hp q) -> hp p ci q", p=128, q=128)
        wv_r = Hd["Wv"].ap().rearrange("(ci p) (hp q) -> hp p ci q", p=128, q=128)
        for hp in range(NP):
            P.dma("gpsimd", out=wtile(wq_t), in_=wq_r[hp],
                  reads=[Hd["Wq"]], writes=[wq_t.keys], tag=f"l1a_wq{hp}")
            ps = psA if hp % 2 == 0 else psB
            for ci in range(NCT):
                for tb in range(4):
                    P.op("tensor", lambda eng, ps=ps, ci=ci, tb=tb: eng.matmul(
                        ps[:, tb * 512:(tb + 1) * 512], wtile(wq_t)[:, ci, :],
                        xnT[:, ci, tb * 512:(tb + 1) * 512],
                        start=(ci == 0), stop=(ci == NCT - 1)),
                         reads=[wq_t.keys, xnT], writes=[pk(ps, [tb])],
                         tag=f"l1a_q{hp}_{ci}_{tb}")
            P.op("scalar", lambda eng, ps=ps, hp=hp: eng.activation(
                big0[:, hp, :], ps[:, :], AF.Identity,
                bias=bqs.ap()[:, hp:hp + 1], scale=0.125),
                 reads=[pk(ps, range(4)), bqs.keys], writes=[big0], tag=f"l1a_qe{hp}")

        def prep_gen(hp):
            pi = hp % 2
            P.dma("gpsimd", out=wtile(wk[pi]), in_=wk_r[hp],
                  reads=[Hd["Wk"]], writes=[wk[pi].keys], tag=f"l1a_wk{hp}")
            yield
            P.dma("gpsimd", out=wtile(wv[pi]), in_=wv_r[hp],
                  reads=[Hd["Wv"]], writes=[wv[pi].keys], tag=f"l1a_wv{hp}")
            yield
            vnat_r = vnat[pi].ap().rearrange("p (a b) -> p a b", b=136)
            gi = 0
            for half in range(4):
                c0 = half * 512
                pb = 1024 + (gi % 2) * 512
                gi += 1
                for ci in range(NCT):
                    P.op("tensor", lambda eng, pi=pi, ci=ci, c0=c0, pb=pb: eng.matmul(
                        psB[:, pb:pb + 512], wtile(wk[pi])[:, ci, :],
                        xnT[:, ci, c0:c0 + 512], start=(ci == 0), stop=(ci == NCT - 1)),
                         reads=[wk[pi].keys, xnT], writes=[pk(psB, [pb // 512])],
                         tag=f"l1a_k{hp}_{half}_{ci}")
                    yield
                P.op("vector", lambda eng, pi=pi, hp=hp, c0=c0, pb=pb: eng.tensor_single_scalar(
                    kTp[pi].ap()[:, c0:c0 + 512], psB[:, pb:pb + 512],
                    bseg.ap()[:, 8 + hp:9 + hp], ALU.add),
                     reads=[pk(psB, [pb // 512]), bseg.keys], writes=[kTp[pi].keys],
                     tag=f"l1a_ke{hp}_{half}")
                yield
            for half in range(4):
                c0 = half * 512
                pb = 1024 + (gi % 2) * 512
                gi += 1
                for ci in range(NCT):
                    P.op("tensor", lambda eng, pi=pi, ci=ci, c0=c0, pb=pb: eng.matmul(
                        psB[:, pb:pb + 512], wtile(wv[pi])[:, ci, :],
                        xnT[:, ci, c0:c0 + 512], start=(ci == 0), stop=(ci == NCT - 1)),
                         reads=[wv[pi].keys, xnT], writes=[pk(psB, [pb // 512])],
                         tag=f"l1a_v{hp}_{half}_{ci}")
                    yield
                P.op("vector", lambda eng, hp=hp, pb=pb: eng.tensor_single_scalar(
                    vstg.ap(), psB[:, pb:pb + 512],
                    bseg.ap()[:, 16 + hp:17 + hp], ALU.add),
                     reads=[pk(psB, [pb // 512]), bseg.keys], writes=[vstg.keys],
                     tag=f"l1a_ve{hp}_{half}")
                yield
                for tj in range(4):
                    ti = half * 4 + tj
                    pb = 1024 + (gi % 2) * 512
                    gi += 1
                    P.op("tensor", lambda eng, tj=tj, pb=pb: eng.transpose(
                        psB[:, pb:pb + 128], vstg.ap()[:, tj * 128:(tj + 1) * 128], ident_f32[:]),
                         reads=[vstg.keys, ident_f32], writes=[pk(psB, [pb // 512])],
                         tag=f"l1a_vt{hp}_{ti}")
                    yield
                    P.op("vector", lambda eng, ti=ti, vnat_r=vnat_r, pb=pb: eng.tensor_copy(
                        vnat_r[:, ti, 0:64], psB[:, pb:pb + 64]),
                         reads=[pk(psB, [pb // 512])], writes=[vnat[pi].keys], tag=f"l1a_va{hp}_{ti}")
                    P.op("vector", lambda eng, ti=ti, vnat_r=vnat_r, pb=pb: eng.tensor_copy(
                        vnat_r[:, ti, 68:132], psB[:, pb + 64:pb + 128]),
                         reads=[pk(psB, [pb // 512])], writes=[vnat[pi].keys], tag=f"l1a_vb{hp}_{ti}")
                    yield
            P.op("vector", lambda eng, vnat_r=vnat_r: eng.memset(vnat_r[:, :, 64:68], 1.0),
                 reads=[], writes=[vnat[pi].keys], tag=f"l1a_on1{hp}")
            P.op("vector", lambda eng, vnat_r=vnat_r: eng.memset(vnat_r[:, :, 132:136], 1.0),
                 reads=[], writes=[vnat[pi].keys], tag=f"l1a_on2{hp}")
            yield

        def attn_gen(hp):
            pi = hp % 2
            vnat_r = vnat[pi].ap().rearrange("p (a b) -> p a b", b=136)
            for h2 in range(2):
                h = 2 * hp + h2
                est = expstk[h2]
                P.dma("gpsimd", out=est.ap()[:, 0:NF], in_=Hd["rel_stack"].ap()[h],
                      reads=[Hd["rel_stack"]], writes=[est.keys], tag=f"l1a_stk{h}")
                P.op("scalar", lambda eng, est=est: eng.activation(
                    est.ap()[:, 0:NF], est.ap()[:, 0:NF], AF.Exp),
                     reads=[est.keys], writes=[est.keys], tag=f"l1a_estk{h}")
                yield
            for h2 in range(2):
                h = 2 * hp + h2
                r0, r1 = h2 * 64, h2 * 64 + 64
                est = expstk[h2]
                for qh in range(2):
                    q0 = qh * 1024
                    for kc in range(NCH):
                        sb0 = (kc % 2) * 1024
                        ab = attn_bf[kc % 2]
                        for tb in range(2):
                            P.op("tensor", lambda eng, kc=kc, sb0=sb0, tb=tb, r0=r0, r1=r1, q0=q0, hp=hp, pi=pi: eng.matmul(
                                psA[:, sb0 + tb * 512: sb0 + (tb + 1) * 512],
                                kTp[pi].ap()[r0:r1, kc * 128:(kc + 1) * 128],
                                big0[r0:r1, hp, q0 + tb * 512: q0 + (tb + 1) * 512],
                                start=True, stop=True),
                                 reads=[kTp[pi].keys, big0], writes=[pk(psA, [sb0 // 512 + tb])],
                                 tag=f"l1a_s{h}_{qh}_{kc}_{tb}")
                        P.op("scalar", lambda eng, ab=ab, sb0=sb0: eng.activation(
                            ab.ap(), psA[:, sb0:sb0 + 1024], AF.Exp),
                             reads=[pk(psA, [sb0 // 512, sb0 // 512 + 1])],
                             writes=[ab.keys], tag=f"l1a_ex{h}_{qh}_{kc}")
                        f0 = 1920 - kc * 128 + q0
                        P.op("vector", lambda eng, ab=ab, est=est, f0=f0: eng.tensor_mul(
                            ab.ap(), ab.ap(), est.ap()[:, f0:f0 + 1024]),
                             reads=[ab.keys, est.keys], writes=[ab.keys],
                             tag=f"l1a_bm{h}_{qh}_{kc}")
                        for tb in range(2):
                            P.op("tensor", lambda eng, ab=ab, kc=kc, tb=tb, h2=h2, vnat_r=vnat_r, pi=pi: eng.matmul(
                                psB[0:68, tb * 512:(tb + 1) * 512],
                                vnat_r[:, kc, h2 * 68:(h2 + 1) * 68],
                                ab.ap()[:, tb * 512:(tb + 1) * 512],
                                start=(kc == 0), stop=(kc == NCH - 1)),
                                 reads=[vnat[pi].keys, ab.keys], writes=[pk(psB, [tb])],
                                 tag=f"l1a_av{h}_{qh}_{kc}_{tb}")
                        yield
                    nrbf = scr_bf[0:1, 2 * t64.off_w:2 * t64.off_w + 1024]
                    P.op("vector", lp(lambda eng, nrbf=nrbf: eng.reciprocal(
                        nrbf, psB[64:65, 0:1024])),
                         reads=[pk(psB, [0, 1])], writes=[t64.keys], tag=f"l1a_nr{h}_{qh}")
                    for half in range(2):
                        P.op("tensor", lambda eng, nrbf=nrbf, half=half: eng.matmul(
                            psA[0:64, 0:512], ones_bf[0:1, 0:64],
                            nrbf[:, half * 512:(half + 1) * 512], start=True, stop=True),
                             reads=[t64.keys, ones_bf], writes=[pk(psA, [0])],
                             tag=f"l1a_b{half}{h}_{qh}")
                        P.op("vector", lambda eng, half=half: eng.tensor_copy(
                            bc_sb.ap((0, 64))[:, half * 512:(half + 1) * 512], psA[0:64, 0:512]),
                             reads=[pk(psA, [0])], writes=[bc_sb.keys],
                             tag=f"l1a_bce{half}{h}_{qh}")
                    P.op("vector", lambda eng, r0=r0, r1=r1: eng.tensor_mul(
                        t64.ap((r0, r1)), psB[0:64, 0:1024], bc_sb.ap((0, 64))),
                         reads=[pk(psB, [0, 1]), bc_sb.keys], writes=[t64.keys],
                         tag=f"l1a_ap{h}_{qh}")
                    P.op("vector", lambda eng, r0=r0, r1=r1, hp=hp, q0=q0: eng.tensor_add(
                        xT[r0:r1, hp, q0:q0 + 1024], xT[r0:r1, hp, q0:q0 + 1024],
                        t64.ap((r0, r1))),
                         reads=[t64.keys, xT], writes=[xT], tag=f"l1a_xa{h}_{qh}")
                    yield

        def drive(agen, pgen, ratio=3):
            done_a = agen is None
            done_p = pgen is None
            while not (done_a and done_p):
                for _ in range(ratio):
                    if not done_a:
                        try:
                            next(agen)
                        except StopIteration:
                            done_a = True
                if not done_p:
                    try:
                        next(pgen)
                    except StopIteration:
                        done_p = True

        drive(None, prep_gen(0))
        for hp in range(NP):
            drive(attn_gen(hp), prep_gen(hp + 1) if hp + 1 < NP else None)

    def l1_moe(ar):
        lgdram = nc.dram_tensor("moe_lg_d", [16, T], F32, kind="Internal")
        cmbdram = nc.dram_tensor("moe_cmb_d", [16, T], F32, kind="Internal")
        spl = ar.sv(256)
        sbp = ar.sv(256)
        msb = ar.sv(256)
        msbo = ar.sv(256)
        gs = ar.sv(64)
        gwk = ar.sv(64)
        gmx = ar.sv(16)
        gmask = ar.sv(64)
        selp = ar.sv(256)
        wsum = ar.sv(16)
        cb16 = ar.sv(16)
        t256 = ar.sv(256)
        cmb_bf = ar.sv(T // 2, BF16)
        sg_bf = ar.sv(T // 2, BF16)
        su_bf = ar.sv(T // 2, BF16)
        wg = [ar.sv(1024, BF16), ar.sv(1024, BF16)]
        wu = [ar.sv(1024, BF16), ar.sv(1024, BF16)]
        wdg = [ar.sv(1024, BF16) for _ in range(4)]
        wsg, wsu, wsd = wg[0], wu[0], wdg[0]
        cmbrow = sv(spl.off_w, T)

        lg = logitsT.ap((0, 16))[:, 0:T]
        P.dma("scalar", out=lgdram.ap(), in_=lg, reads=[logitsT.keys], writes=[lgdram], tag="moe_lgd")
        for e in range(E):
            P.dma("scalar", out=spl.ap()[:, e * 16:(e + 1) * 16],
                  in_=lgdram.ap()[e].rearrange("(p j) -> p j", j=16),
                  reads=[lgdram], writes=[spl.keys], tag=f"moe_pl{e}")
        P.op("scalar", lambda eng: eng.activation(spl.ap(), spl.ap(), AF.Sigmoid),
             reads=[spl.keys], writes=[spl.keys], tag="moe_sig")
        P.dma("sync", out=cb16.ap((0, 1))[:, 0:16], in_=Hd["corr_bias"].ap().unsqueeze(0),
              reads=[Hd["corr_bias"]], writes=[cb16.keys], tag="moe_cb")
        P.op("tensor", lambda eng: eng.matmul(
            psA[:, 0:16], ones_f32[0:1, :], cb16.ap((0, 1))[:, 0:16], start=True, stop=True),
             reads=[cb16.keys, ones_f32], writes=[pk(psA, [0])], tag="moe_cbb")
        P.op("vector", lambda eng: eng.tensor_copy(cb16.ap(), psA[:, 0:16]),
             reads=[pk(psA, [0])], writes=[cb16.keys], tag="moe_cbe")
        for e in range(E):
            P.op("vector", lambda eng, e=e: eng.tensor_single_scalar(
                sbp.ap()[:, e * 16:(e + 1) * 16], spl.ap()[:, e * 16:(e + 1) * 16],
                cb16.ap()[:, e:e + 1], ALU.add),
                 reads=[spl.keys, cb16.keys], writes=[sbp.keys], tag=f"moe_sb{e}")
        for g in range(4):
            e0 = g * 4
            blk = lambda i, e0=e0: sbp.ap()[:, (e0 + i) * 16:(e0 + i + 1) * 16]
            P.op("vector", lambda eng, blk=blk: eng.tensor_max(t256.ap()[:, 0:16], blk(0), blk(1)),
                 reads=[sbp.keys], writes=[t256.keys], tag=f"moe_g{g}a")
            P.op("vector", lambda eng, blk=blk: eng.tensor_max(t256.ap()[:, 16:32], blk(2), blk(3)),
                 reads=[sbp.keys], writes=[t256.keys], tag=f"moe_g{g}b")
            P.op("vector", lambda eng: eng.tensor_max(
                gmx.ap(), t256.ap()[:, 0:16], t256.ap()[:, 16:32]),
                 reads=[t256.keys], writes=[gmx.keys], tag=f"moe_g{g}m1")
            for i in range(4):
                P.op("vector", lambda eng, blk=blk, i=i: eng.tensor_tensor(
                    t256.ap()[:, 32 + i * 16:48 + i * 16], blk(i), gmx.ap(), ALU.is_lt),
                     reads=[sbp.keys, gmx.keys], writes=[t256.keys], tag=f"moe_g{g}lt{i}")
                P.op("vector", lambda eng, blk=blk, i=i: eng.tensor_mul(
                    t256.ap()[:, 96 + i * 16:112 + i * 16], blk(i),
                    t256.ap()[:, 32 + i * 16:48 + i * 16]),
                     reads=[sbp.keys, t256.keys], writes=[t256.keys], tag=f"moe_g{g}q{i}")
                P.op("vector", lambda eng, i=i: eng.tensor_scalar(
                    t256.ap()[:, 32 + i * 16:48 + i * 16],
                    t256.ap()[:, 32 + i * 16:48 + i * 16], 1e30, -1e30, ALU.mult, ALU.add),
                     reads=[t256.keys], writes=[t256.keys], tag=f"moe_g{g}p{i}")
                P.op("vector", lambda eng, i=i: eng.tensor_add(
                    t256.ap()[:, 96 + i * 16:112 + i * 16],
                    t256.ap()[:, 96 + i * 16:112 + i * 16],
                    t256.ap()[:, 32 + i * 16:48 + i * 16]),
                     reads=[t256.keys], writes=[t256.keys], tag=f"moe_g{g}r{i}")
            P.op("vector", lambda eng: eng.tensor_max(
                t256.ap()[:, 0:16], t256.ap()[:, 96:112], t256.ap()[:, 112:128]),
                 reads=[t256.keys], writes=[t256.keys], tag=f"moe_g{g}c")
            P.op("vector", lambda eng: eng.tensor_max(
                t256.ap()[:, 16:32], t256.ap()[:, 128:144], t256.ap()[:, 144:160]),
                 reads=[t256.keys], writes=[t256.keys], tag=f"moe_g{g}d")
            P.op("vector", lambda eng: eng.tensor_max(
                t256.ap()[:, 0:16], t256.ap()[:, 0:16], t256.ap()[:, 16:32]),
                 reads=[t256.keys], writes=[t256.keys], tag=f"moe_g{g}m2")
            P.op("vector", lambda eng, g=g: eng.tensor_add(
                gs.ap()[:, g * 16:(g + 1) * 16], gmx.ap(), t256.ap()[:, 0:16]),
                 reads=[gmx.keys, t256.keys], writes=[gs.keys], tag=f"moe_g{g}s")
        for g in range(4):
            first = True
            for g2 in range(4):
                if g2 == g:
                    continue
                P.op("vector", lambda eng, g=g, g2=g2: eng.tensor_tensor(
                    t256.ap()[:, 0:16], gs.ap()[:, g2 * 16:(g2 + 1) * 16],
                    gs.ap()[:, g * 16:(g + 1) * 16], ALU.is_gt),
                     reads=[gs.keys], writes=[t256.keys], tag=f"moe_c{g}{g2}")
                if first:
                    P.op("vector", lambda eng, g=g: eng.tensor_copy(
                        gwk.ap()[:, g * 16:(g + 1) * 16], t256.ap()[:, 0:16]),
                         reads=[t256.keys], writes=[gwk.keys], tag=f"moe_ci{g}{g2}")
                    first = False
                else:
                    P.op("vector", lambda eng, g=g: eng.tensor_add(
                        gwk.ap()[:, g * 16:(g + 1) * 16], gwk.ap()[:, g * 16:(g + 1) * 16],
                        t256.ap()[:, 0:16]),
                         reads=[t256.keys, gwk.keys], writes=[gwk.keys], tag=f"moe_ca{g}{g2}")
            P.op("vector", lambda eng, g=g: eng.tensor_single_scalar(
                gmask.ap()[:, g * 16:(g + 1) * 16], gwk.ap()[:, g * 16:(g + 1) * 16],
                2.0, ALU.is_lt),
                 reads=[gwk.keys], writes=[gmask.keys], tag=f"moe_gm{g}")
        P.op("vector", lambda eng: eng.tensor_scalar(
            gwk.ap(), gmask.ap(), 1e30, -1e30, ALU.mult, ALU.add),
             reads=[gmask.keys], writes=[gwk.keys], tag="moe_gterm")
        for e in range(E):
            g = e // 4
            P.op("vector", lambda eng, e=e, g=g: eng.tensor_mul(
                msb.ap()[:, e * 16:(e + 1) * 16], sbp.ap()[:, e * 16:(e + 1) * 16],
                gmask.ap()[:, g * 16:(g + 1) * 16]),
                 reads=[sbp.keys, gmask.keys], writes=[msb.keys], tag=f"moe_m2{e}")
            P.op("vector", lambda eng, e=e, g=g: eng.tensor_add(
                msb.ap()[:, e * 16:(e + 1) * 16], msb.ap()[:, e * 16:(e + 1) * 16],
                gwk.ap()[:, g * 16:(g + 1) * 16]),
                 reads=[msb.keys, gwk.keys], writes=[msb.keys], tag=f"moe_m3{e}")
        P.op("vector", lambda eng: eng.tensor_copy(msbo.ap(), msb.ap()),
             reads=[msb.keys], writes=[msbo.keys], tag="moe_mo")
        def max_over_e(dst_keys, dst_ap, src_ap, src_keys, tagp):
            P.op("vector", lambda eng: eng.tensor_max(t256.ap()[:, 0:128],
                 src_ap[:, 0:128], src_ap[:, 128:256]),
                 reads=[src_keys], writes=[t256.keys], tag=tagp + "a")
            P.op("vector", lambda eng: eng.tensor_max(t256.ap()[:, 0:64],
                 t256.ap()[:, 0:64], t256.ap()[:, 64:128]),
                 reads=[t256.keys], writes=[t256.keys], tag=tagp + "b")
            P.op("vector", lambda eng: eng.tensor_max(t256.ap()[:, 0:32],
                 t256.ap()[:, 0:32], t256.ap()[:, 32:64]),
                 reads=[t256.keys], writes=[t256.keys], tag=tagp + "c")
            P.op("vector", lambda eng: eng.tensor_max(dst_ap,
                 t256.ap()[:, 0:16], t256.ap()[:, 16:32]),
                 reads=[t256.keys], writes=[dst_keys], tag=tagp + "d")
        for it in range(4):
            max_over_e(gmx.keys, gmx.ap(), msb.ap(), msb.keys, f"moe_t{it}")
            if it < 3:
                for e in range(E):
                    P.op("vector", lambda eng, e=e: eng.tensor_tensor(
                        t256.ap()[:, e * 16:(e + 1) * 16], msb.ap()[:, e * 16:(e + 1) * 16],
                        gmx.ap(), ALU.is_ge),
                         reads=[msb.keys, gmx.keys], writes=[t256.keys], tag=f"moe_x{it}_{e}")
                P.op("vector", lambda eng: eng.scalar_tensor_tensor(
                    msb.ap(), t256.ap(), -2e30, msb.ap(), ALU.mult, ALU.add),
                     reads=[t256.keys, msb.keys], writes=[msb.keys], tag=f"moe_y{it}")
        for e in range(E):
            P.op("vector", lambda eng, e=e: eng.tensor_tensor(
                selp.ap()[:, e * 16:(e + 1) * 16], msbo.ap()[:, e * 16:(e + 1) * 16],
                gmx.ap(), ALU.is_ge),
                 reads=[msbo.keys, gmx.keys], writes=[selp.keys], tag=f"moe_s{e}")
        P.op("vector", lambda eng: eng.tensor_mul(selp.ap(), selp.ap(), spl.ap()),
             reads=[selp.keys, spl.keys], writes=[selp.keys], tag="moe_w")
        P.op("vector", lambda eng: eng.tensor_add(t256.ap()[:, 0:128],
             selp.ap()[:, 0:128], selp.ap()[:, 128:256]),
             reads=[selp.keys], writes=[t256.keys], tag="moe_wsa")
        P.op("vector", lambda eng: eng.tensor_add(t256.ap()[:, 0:64],
             t256.ap()[:, 0:64], t256.ap()[:, 64:128]),
             reads=[t256.keys], writes=[t256.keys], tag="moe_wsb")
        P.op("vector", lambda eng: eng.tensor_add(t256.ap()[:, 0:32],
             t256.ap()[:, 0:32], t256.ap()[:, 32:64]),
             reads=[t256.keys], writes=[t256.keys], tag="moe_wsc")
        P.op("vector", lambda eng: eng.tensor_add(wsum.ap(),
             t256.ap()[:, 0:16], t256.ap()[:, 16:32]),
             reads=[t256.keys], writes=[wsum.keys], tag="moe_wsd")
        P.op("vector", lambda eng: eng.tensor_scalar_add(wsum.ap(), wsum.ap(), 1e-20),
             reads=[wsum.keys], writes=[wsum.keys], tag="moe_wse")
        P.op("vector", lambda eng: eng.reciprocal(wsum.ap(), wsum.ap()),
             reads=[wsum.keys], writes=[wsum.keys], tag="moe_wsi")
        for e in range(E):
            P.op("vector", lambda eng, e=e: eng.tensor_mul(
                selp.ap()[:, e * 16:(e + 1) * 16], selp.ap()[:, e * 16:(e + 1) * 16],
                wsum.ap()),
                 reads=[selp.keys, wsum.keys], writes=[selp.keys], tag=f"moe_cw{e}")
            P.dma("scalar", out=cmbdram.ap()[e].rearrange("(p j) -> p j", j=16),
                  in_=selp.ap()[:, e * 16:(e + 1) * 16],
                  reads=[selp.keys], writes=[cmbdram], tag=f"moe_cd{e}")
        if debug:
            for nm, svv, w_ in (("selp", selp, 256), ("gmask", gmask, 64), ("sbp", sbp, 256),
                                ("spl", spl, 256), ("gs", gs, 64), ("msbo", msbo, 256),
                                ("gmx", gmx, 16)):
                hh = nc.dram_tensor("dbg_" + nm, [128, w_], F32, kind="ExternalOutput")
                P.dma("sync", out=hh.ap(), in_=svv.ap()[:, 0:w_], reads=[svv.keys], writes=[], tag="dbg" + nm)
        sgw_r = Hd["sgwT"].ap().rearrange("(ci p) (hs q) -> hs p ci q", p=128, q=128)
        suw_r = Hd["suwT"].ap().rearrange("(ci p) (hs q) -> hs p ci q", p=128, q=128)
        sdw_r = Hd["sdwT"].ap().rearrange("(hs p) (ct q) -> ct p hs q", p=128, q=128)
        for half in range(2):
            for hs8 in range(8):
                hs = half * 8 + hs8
                P.dma("gpsimd", out=wsg.ap()[:, 0:1024].rearrange("p (ci q) -> p ci q", q=128),
                      in_=sgw_r[hs], reads=[Hd["sgwT"]], writes=[wsg.keys], tag=f"moe_sg{hs}")
                P.dma("gpsimd", out=wsu.ap()[:, 0:1024].rearrange("p (ci q) -> p ci q", q=128),
                      in_=suw_r[hs], reads=[Hd["suwT"]], writes=[wsu.keys], tag=f"moe_su{hs}")
                for ci in range(NCT):
                    for tb in range(4):
                        P.op("tensor", lambda eng, ci=ci, tb=tb: eng.matmul(
                            psA[:, tb * 512:(tb + 1) * 512],
                            wsg.ap()[:, 0:1024].rearrange("p (ci q) -> p ci q", q=128)[:, ci, :],
                            xnT[:, ci, tb * 512:(tb + 1) * 512],
                            start=(ci == 0), stop=(ci == NCT - 1)),
                             reads=[wsg.keys, xnT], writes=[pk(psA, [tb])], tag=f"moe_g{hs}_{ci}_{tb}")
                        P.op("tensor", lambda eng, ci=ci, tb=tb: eng.matmul(
                            psB[:, tb * 512:(tb + 1) * 512],
                            wsu.ap()[:, 0:1024].rearrange("p (ci q) -> p ci q", q=128)[:, ci, :],
                            xnT[:, ci, tb * 512:(tb + 1) * 512],
                            start=(ci == 0), stop=(ci == NCT - 1)),
                             reads=[wsu.keys, xnT], writes=[pk(psB, [tb])], tag=f"moe_u{hs}_{ci}_{tb}")
                P.op("scalar", lambda eng: eng.activation(sg_bf.ap(), psA[:, :], AF.Silu),
                     reads=[pk(psA, range(4))], writes=[sg_bf.keys], tag=f"moe_sge{hs}")
                P.op("scalar", lambda eng: eng.copy(su_bf.ap(), psB[:, :]),
                     reads=[pk(psB, range(4))], writes=[su_bf.keys], tag=f"moe_sue{hs}")
                P.op("vector", lambda eng, hs8=hs8: eng.tensor_mul(
                    big0[:, hs8, :], sg_bf.ap(), su_bf.ap()),
                     reads=[sg_bf.keys, su_bf.keys], writes=[big0], tag=f"moe_h{hs}")
            for ct in range(NCT):
                P.dma("gpsimd", out=wsd.ap()[:, 0:1024].rearrange("p (hs q) -> p hs q", q=128),
                      in_=sdw_r[ct][:, half * 8:(half + 1) * 8, :],
                      reads=[Hd["sdwT"]], writes=[wsd.keys], tag=f"moe_sd{half}_{ct}")
                ps = psA if ct % 2 == 0 else psB
                for hs8 in range(8):
                    for tb in range(4):
                        P.op("tensor", lambda eng, ps=ps, hs8=hs8, tb=tb: eng.matmul(
                            ps[:, tb * 512:(tb + 1) * 512],
                            wsd.ap()[:, 0:1024].rearrange("p (hs q) -> p hs q", q=128)[:, hs8, :],
                            big0[:, hs8, tb * 512:(tb + 1) * 512],
                            start=(hs8 == 0), stop=(hs8 == 7)),
                             reads=[wsd.keys, big0], writes=[pk(ps, [tb])],
                             tag=f"moe_d{half}_{ct}_{hs8}_{tb}")
                P.op("vector", lambda eng, ps=ps, ct=ct: eng.tensor_add(
                    xT[:, ct, :], xT[:, ct, :], ps[:, :]),
                     reads=[pk(ps, range(4)), xT], writes=[xT], tag=f"moe_da{half}_{ct}")
        if debug:
            for ci in range(NCT):
                hh = nc.dram_tensor(f"dbg_x17_{ci}", [128, T], F32, kind="ExternalOutput")
                P.dma("sync", out=hh.ap(), in_=xT[:, ci, :], reads=[xT], writes=[], tag=f"dbgx17_{ci}")
        gw_r = Hd["gate_w"].ap().rearrange("e (ci p) hh -> e p ci hh", p=128)
        uw_r = Hd["up_w"].ap().rearrange("e (ci p) hh -> e p ci hh", p=128)
        dw_r = Hd["down_w"].ap().rearrange("e (hc p) (ct q) -> e p hc ct q", p=128, q=128)
        for grp in range(4):
            for ei in range(4):
                e = grp * 4 + ei
                crbf = scr_bf[0:1, 2 * cmbrow.off_w:2 * cmbrow.off_w + T]
                P.dma("gpsimd", out=crbf, in_=cmbdram.ap()[e].unsqueeze(0),
                      reads=[cmbdram], writes=[cmbrow.keys], tag=f"moe_cr{e}")
                for tb in range(4):
                    P.op("tensor", lambda eng, tb=tb, crbf=crbf: eng.matmul(
                        psA[:, tb * 512:(tb + 1) * 512], ones_bf[0:1, :],
                        crbf[:, tb * 512:(tb + 1) * 512], start=True, stop=True),
                         reads=[cmbrow.keys, ones_bf], writes=[pk(psA, [tb])], tag=f"moe_cb{e}_{tb}")
                P.op("scalar", lambda eng: eng.copy(cmb_bf.ap(), psA[:, :]),
                     reads=[pk(psA, range(4))], writes=[cmb_bf.keys], tag=f"moe_cbe{e}")
                wge, wue = wg[e % 2], wu[e % 2]
                P.dma("gpsimd", out=wge.ap().rearrange("p (ci hh) -> p ci hh", hh=256),
                      in_=gw_r[e], reads=[Hd["gate_w"]], writes=[wge.keys], tag=f"moe_wg{e}")
                P.dma("gpsimd", out=wue.ap().rearrange("p (ci hh) -> p ci hh", hh=256),
                      in_=uw_r[e], reads=[Hd["up_w"]], writes=[wue.keys], tag=f"moe_wu{e}")
                for hc in range(2):
                    for ci in range(NCT):
                        for tb in range(4):
                            P.op("tensor", lambda eng, hc=hc, ci=ci, tb=tb, wge=wge: eng.matmul(
                                psA[:, tb * 512:(tb + 1) * 512],
                                wge.ap().rearrange("p (ci hh) -> p ci hh", hh=256)[:, ci, hc * 128:(hc + 1) * 128],
                                xnT[:, ci, tb * 512:(tb + 1) * 512],
                                start=(ci == 0), stop=(ci == NCT - 1)),
                                 reads=[wge.keys, xnT], writes=[pk(psA, [tb])],
                                 tag=f"moe_eg{e}_{hc}_{ci}_{tb}")
                            P.op("tensor", lambda eng, hc=hc, ci=ci, tb=tb, wue=wue: eng.matmul(
                                psB[:, tb * 512:(tb + 1) * 512],
                                wue.ap().rearrange("p (ci hh) -> p ci hh", hh=256)[:, ci, hc * 128:(hc + 1) * 128],
                                xnT[:, ci, tb * 512:(tb + 1) * 512],
                                start=(ci == 0), stop=(ci == NCT - 1)),
                                 reads=[wue.keys, xnT], writes=[pk(psB, [tb])],
                                 tag=f"moe_eu{e}_{hc}_{ci}_{tb}")
                    P.op("scalar", lambda eng: eng.activation(sg_bf.ap(), psA[:, :], AF.Silu),
                         reads=[pk(psA, range(4))], writes=[sg_bf.keys], tag=f"moe_ege{e}_{hc}")
                    P.op("scalar", lambda eng: eng.copy(su_bf.ap(), psB[:, :]),
                         reads=[pk(psB, range(4))], writes=[su_bf.keys], tag=f"moe_eue{e}_{hc}")
                    P.op("vector", lambda eng: eng.tensor_mul(sg_bf.ap(), sg_bf.ap(), su_bf.ap()),
                         reads=[sg_bf.keys, su_bf.keys], writes=[sg_bf.keys], tag=f"moe_eh{e}_{hc}")
                    P.op("vector", lambda eng, ei=ei, hc=hc: eng.tensor_mul(
                        big0[:, ei * 2 + hc, :], sg_bf.ap(), cmb_bf.ap()),
                         reads=[sg_bf.keys, cmb_bf.keys], writes=[big0], tag=f"moe_ehw{e}_{hc}")
            for ei in range(4):
                e = grp * 4 + ei
                P.dma("gpsimd", out=wdg[ei].ap().rearrange("p (hc ct q) -> p hc ct q", hc=2, q=128),
                      in_=dw_r[e], reads=[Hd["down_w"]], writes=[wdg[ei].keys],
                      tag=f"moe_wd{e}")
            for ct in range(NCT):
                ps = psA if ct % 2 == 0 else psB
                for ei in range(4):
                    e = grp * 4 + ei
                    for hc in range(2):
                        for tb in range(4):
                            P.op("tensor", lambda eng, ps=ps, ei=ei, e=e, hc=hc, ct=ct, tb=tb: eng.matmul(
                                ps[:, tb * 512:(tb + 1) * 512],
                                wdg[ei].ap().rearrange("p (hc ct q) -> p hc ct q", hc=2, q=128)[:, hc, ct, :],
                                big0[:, ei * 2 + hc, tb * 512:(tb + 1) * 512],
                                start=(ei == 0 and hc == 0), stop=(ei == 3 and hc == 1)),
                                 reads=[wdg[ei].keys, big0], writes=[pk(ps, [tb])],
                                 tag=f"moe_ed{e}_{ct}_{hc}_{tb}")
                P.op("vector", lambda eng, ps=ps, ct=ct: eng.tensor_add(
                    xT[:, ct, :], xT[:, ct, :], ps[:, :]),
                     reads=[pk(ps, range(4)), xT], writes=[xT], tag=f"moe_eda{grp}_{ct}")

    layernorm(0)
    if stage >= 1:
        ar0 = Arena("late")
        router_l0(ar0)
    if stage >= 2:
        ar_f = Arena()
        l0_ffn(ar_f)
    if stage >= 3:
        ar_a = Arena()
        l0_attn(ar_a)
    if stage >= 4 and debug:
        for ci in range(NCT):
            hh = nc.dram_tensor(f"dbg_x1_{ci}", [128, T], F32, kind="ExternalOutput")
            P.dma("sync", out=hh.ap(), in_=xT[:, ci, :], reads=[xT], writes=[], tag=f"dbgx1_{ci}")
    if stage >= 4:
        layernorm(1)
        ar1 = Arena("late")
        router_l1(ar1)
    if stage >= 5:
        ar_l1 = Arena()
        l1_attn(ar_l1)
    if stage >= 6 and debug:
        for ci in range(NCT):
            hh = nc.dram_tensor(f"dbg_x15_{ci}", [128, T], F32, kind="ExternalOutput")
            P.dma("sync", out=hh.ap(), in_=xT[:, ci, :], reads=[xT], writes=[], tag=f"dbgx15_{ci}")
    if stage >= 6:
        ar_m = Arena()
        l1_moe(ar_m)

    for ti in range(NCH):
        pst = psA if ti % 2 == 0 else psB
        stgv = stg_a if ti % 2 == 0 else stg_b
        for ci in range(NCT):
            bank = ci % 4
            col = bank * 512 + (ci // 4) * 128
            dst = pst[:, col:col + 128]
            P.op("tensor",
                 lambda eng, dst=dst, ti=ti, ci=ci: eng.transpose(
                     dst, xT[:, ci, ti * 128:(ti + 1) * 128], ident_f32[:]),
                 reads=[xT, ident_f32], writes=[pk(pst, [bank])], tag=f"oy{ti}_{ci}")
            P.op("vector",
                 lambda eng, dst=dst, ci=ci, stgv=stgv: eng.tensor_copy(
                     stgv.ap()[:, ci * 128:(ci + 1) * 128], dst),
                 reads=[pk(pst, [bank])], writes=[stgv.keys], tag=f"oye{ti}_{ci}")
        P.dma("sync", out=y.ap().rearrange("(ti p) c -> ti p c", p=128)[ti],
              in_=stgv.ap()[:, 0:C], reads=[stgv.keys], writes=[], tag=f"sty{ti}")

    dbg = {}
    if debug:
        def dump(name, src_ap, shape, dtype, readkeys):
            hh = nc.dram_tensor("dbg_" + name, list(shape), dtype, kind="ExternalOutput")
            P.dma("sync", out=hh.ap(), in_=src_ap, reads=[readkeys], writes=[], tag="dbg" + name)
            dbg[name] = hh
        for ci in range(NCT):
            hh = nc.dram_tensor(f"dbg_xnT{ci}", [128, T], BF16, kind="ExternalOutput")
            P.dma("sync", out=hh.ap(), in_=xnT[:, ci, :], reads=[xnT], writes=[], tag=f"dbgxn{ci}")
        dump("mrow", rows.ap((0, 1))[:, 0:T], (1, T), F32, rows.keys)
        dump("rstd", rows.ap((0, 1))[:, T:2 * T], (1, T), F32, rows.keys)
        if stage >= 1:
            dump("scoreL0", logitsT.ap((0, 1))[:, 0:T], (1, T), F32, logitsT.keys)
        if stage >= 4:
            dump("logits", logitsT.ap((0, 16))[:, 0:T], (16, T), F32, logitsT.keys)

    from contextlib import ExitStack
    stack = ExitStack()
    sems = {e: stack.enter_context(nc.semaphore(f"s_{e}")) for e in Prog.CENG}
    qsems = {q: [stack.enter_context(nc.semaphore(f"q_{q}{l}")) for l in range(Prog.NLANES)]
             for q in Prog.QUEUES}
    with nc.Block() as block:
        P.emit_all(block, sems, qsems)
    stack.close()
    return nc


def make_in_maps(inputs, n_cores=8):
    x = np.asarray(inputs["L0_attn_x"])
    base = prep_weights(inputs)
    base.update(prep_consts())
    in_maps = []
    for b in range(n_cores):
        m = dict(base)
        m["x"] = np.ascontiguousarray(x[b])
        in_maps.append(m)
    return in_maps


def kernel(**inputs):
    nc = build_program()
    in_maps = make_in_maps(inputs)
    res = run_bass_kernel_spmd(nc, in_maps, core_ids=list(range(8)))
    return np.stack([res.results[b]["y"] for b in range(8)], axis=0)


# version 14 (final)
# speedup vs baseline: 1.1991x; 1.1727x over previous
import numpy as np
import ml_dtypes
import concourse.bass as bass
import concourse.mybir as mybir
from concourse.bass_utils import run_bass_kernel_spmd

F32 = mybir.dt.float32
BF16 = mybir.dt.bfloat16
AF = mybir.ActivationFunctionType
ALU = mybir.AluOpType
AX = mybir.AxisListType

T = 2048
C = 1024
NCT = 8
H = 16
D = 64
M = 64
CH = 128
NCH = T // CH
NP = 8
E = 16
EH = 256
HS = 2048
NF = 3968
GC1 = 0.7978845608028654
GC2 = 0.044715

STAGE_ALL = 99


import os
SERIAL = bool(int(os.environ.get("BK_SERIAL", "0")))


class Prog:
    CENG = ("tensor", "scalar", "vector", "gpsimd", "sync")
    QUEUES = ("sync", "scalar", "gpsimd")

    def __init__(self, nc):
        self.nc = nc
        self.ops = []
        self.count = {e: 0 for e in self.CENG}
        self.dma_count = {q: 0 for q in self.QUEUES}
        self.last_write = {}
        self.readers = {}
        self.prev_me = None

    def _collect(self, reads, writes):
        deps = set()
        rkeys, wkeys = [], []
        for r in reads:
            for k in (r if isinstance(r, list) else [r]):
                k = k if isinstance(k, tuple) else (id(getattr(k, "tensor", k)), None)
                rkeys.append(k)
                lw = self.last_write.get(k)
                if lw is not None:
                    deps.add(lw)
        for w in writes:
            for k in (w if isinstance(w, list) else [w]):
                k = k if isinstance(k, tuple) else (id(getattr(k, "tensor", k)), None)
                wkeys.append(k)
                lw = self.last_write.get(k)
                if lw is not None:
                    deps.add(lw)
                for rd in self.readers.get(k, ()):
                    deps.add(rd)
        return deps, rkeys, wkeys

    def _book(self, me, rkeys, wkeys):
        for k in rkeys:
            self.readers.setdefault(k, []).append(me)
        for k in wkeys:
            self.last_write[k] = me
            self.readers[k] = []

    def op(self, engine, emit, reads=(), writes=(), tag=""):
        deps, rkeys, wkeys = self._collect(reads, writes)
        me = ("c", engine, self.count[engine])
        self.count[engine] += 1
        if SERIAL and self.prev_me is not None:
            deps = {self.prev_me}
        self.prev_me = me
        self.ops.append(dict(engine=engine, emit=emit, deps=deps, me=me, tag=tag, dma=False))
        self._book(me, rkeys, wkeys)
        return me

    NLANES = 8

    def dma(self, queue, out, in_, reads=(), writes=(), tag="", **kw):
        deps, rkeys, wkeys = self._collect(reads, writes)
        me = ("d", queue, self.dma_count[queue])
        self.dma_count[queue] += 1
        if SERIAL and self.prev_me is not None:
            deps = {self.prev_me}
        self.prev_me = me
        emit = lambda eng, out=out, in_=in_, kw=kw: eng.dma_start(out=out, in_=in_, **kw)
        self.ops.append(dict(engine=queue, emit=emit, deps=deps, me=me, tag=tag, dma=True))
        self._book(me, rkeys, wkeys)
        return me

    def emit_all(self, block, sems, qsems):
        per_engine = {e: [] for e in self.CENG}
        for o in self.ops:
            per_engine[o["engine"]].append(o)
        NL = self.NLANES

        def lane_of(dep):
            q, i = dep[1], dep[2]
            return qsems[q][i % NL], 16 * (i // NL + 1)

        def mk_section(ename, ops_here):
            def fn(eng):
                hw = {}
                ndma = 0
                for o in ops_here:
                    if o["dma"] and ndma >= NL:
                        sem, val = qsems[o["engine"]][ndma % NL], 16 * (ndma // NL)
                        if hw.get(id(sem), -1) < val:
                            hw[id(sem)] = val
                            eng.wait_ge(sem, val)
                    for dep in sorted(o["deps"], key=repr):
                        if dep[0] == "c":
                            dep_eng, dep_seq = dep[1], dep[2]
                            if dep_eng == ename == "tensor" and not SERIAL:
                                continue
                            sem, val = sems[dep_eng], dep_seq + 1
                        else:
                            sem, val = lane_of(dep)
                        if hw.get(id(sem), -1) >= val:
                            continue
                        hw[id(sem)] = val
                        eng.wait_ge(sem, val)
                    ins = o["emit"](eng)
                    if o["dma"]:
                        ins.then_inc(qsems[o["engine"]][ndma % NL], 16)
                        ndma += 1
                    else:
                        ins.then_inc(sems[o["engine"]], 1)
                if ename == "gpsimd":
                    for q in self.QUEUES:
                        n = self.dma_count[q]
                        for l in range(NL):
                            cnt = 16 * ((n - l + NL - 1) // NL)
                            if cnt:
                                eng.wait_ge(qsems[q][l], cnt)
            return fn

        for ename in self.CENG:
            if per_engine[ename] or ename == "gpsimd":
                getattr(block, ename)(mk_section(ename, per_engine[ename]))


def prep_consts():
    c = {}
    c["ident_bf"] = np.eye(128, dtype=np.float32).astype(ml_dtypes.bfloat16)
    c["ident_f32"] = np.eye(128, dtype=np.float32)
    ones = np.ones((128, 128), dtype=np.float32)
    c["ones_f32"] = ones
    c["ones_bf"] = ones.astype(ml_dtypes.bfloat16)
    tri = np.triu(np.ones((128, 128), dtype=np.float32))
    c["trimask_bf"] = np.concatenate([tri, tri], axis=1).astype(ml_dtypes.bfloat16)
    return c


def prep_weights(inp):
    w = {}
    w["WqkvT"] = np.ascontiguousarray(np.asarray(inp["L0_attn_Wqkv"]).T)
    w["bqkv"] = np.ascontiguousarray(np.asarray(inp["L0_attn_bqkv"]))
    w["WoutT"] = np.ascontiguousarray(np.asarray(inp["L0_attn_Wout"]).T)
    w["bout"] = np.ascontiguousarray(np.asarray(inp["L0_attn_bout"]))
    w["proj_pair"] = np.ascontiguousarray(np.asarray(inp["L0_attn_proj"]).reshape(NP, 2 * D, M))
    w["w_router"] = np.ascontiguousarray(np.asarray(inp["L0_ffn_w_router"]))
    w["w_block"] = np.ascontiguousarray(np.asarray(inp["L0_ffn_w_block"]))
    w["Wq"] = np.ascontiguousarray(np.asarray(inp["L1_attn_Wq"]))
    w["Wk"] = np.ascontiguousarray(np.asarray(inp["L1_attn_Wk"]))
    w["Wv"] = np.ascontiguousarray(np.asarray(inp["L1_attn_Wv"]))
    w["bq"] = np.ascontiguousarray(np.asarray(inp["L1_attn_bq"]))
    w["bk"] = np.ascontiguousarray(np.asarray(inp["L1_attn_bk"]))
    w["bv"] = np.ascontiguousarray(np.asarray(inp["L1_attn_bv"]))
    rel = np.asarray(inp["L1_attn_rel_emb"])
    stk = np.empty((H, 128, NF), dtype=np.float32)
    for h in range(H):
        r = rel[h]
        stk[h] = np.lib.stride_tricks.as_strided(
            r[127:], shape=(128, NF), strides=(-r.strides[0], r.strides[0]))
    w["rel_stack"] = stk
    w["router_wT"] = np.ascontiguousarray(np.asarray(inp["L1_ffn_router_w"]).T)
    w["corr_bias"] = np.ascontiguousarray(np.asarray(inp["L1_ffn_correction_bias"]))
    w["gate_w"] = np.ascontiguousarray(np.asarray(inp["L1_ffn_gate_w"]))
    w["up_w"] = np.ascontiguousarray(np.asarray(inp["L1_ffn_up_w"]))
    w["down_w"] = np.ascontiguousarray(np.asarray(inp["L1_ffn_down_w"]))
    w["sgwT"] = np.ascontiguousarray(np.asarray(inp["L1_ffn_shared_gate_w"]).T)
    w["suwT"] = np.ascontiguousarray(np.asarray(inp["L1_ffn_shared_up_w"]).T)
    w["sdwT"] = np.ascontiguousarray(np.asarray(inp["L1_ffn_shared_down_w"]).T)
    return w


def build_program(stage=STAGE_ALL, debug=False):
    nc = bass.Bass("TRN2", target_bir_lowering=False)
    P = Prog(nc)
    Hd = {}

    def din(name, shape, dtype=F32):
        Hd[name] = nc.dram_tensor(name, list(shape), dtype, kind="ExternalInput")
        return Hd[name]

    din("x", (T, C))
    din("WqkvT", (C, 3 * C)); din("bqkv", (3 * C,))
    din("WoutT", (C, C)); din("bout", (C,))
    din("proj_pair", (NP, 128, M))
    din("w_router", (C,)); din("w_block", (C, C))
    din("Wq", (C, C)); din("Wk", (C, C)); din("Wv", (C, C))
    din("bq", (C,)); din("bk", (C,)); din("bv", (C,))
    din("rel_stack", (H, 128, NF))
    din("router_wT", (C, 16)); din("corr_bias", (16,))
    din("gate_w", (E, C, EH)); din("up_w", (E, C, EH)); din("down_w", (E, EH, C))
    din("sgwT", (C, HS)); din("suwT", (C, HS)); din("sdwT", (HS, C))
    din("ident_bf", (128, 128), BF16); din("ident_f32", (128, 128))
    din("ones_f32", (128, 128)); din("ones_bf", (128, 128), BF16)
    din("trimask_bf", (128, 256), BF16)
    y = nc.dram_tensor("y", [T, C], F32, kind="ExternalOutput")

    A = nc.alloc_sbuf_tensor
    xT = A("xT", [128, NCT, T], F32)
    xnT = A("xnT", [128, NCT, T], BF16)
    big0 = A("big0", [128, NCT, T], BF16)
    ident_bf = A("c_idb", [128, 128], BF16)
    ident_f32 = A("c_idf", [128, 128], F32)
    ones_f32 = A("c_onf", [128, 128], F32)
    ones_bf = A("c_onb", [128, 128], BF16)
    trimask = A("c_tri", [128, 256], BF16)
    consts = {"ident_bf": ident_bf, "ident_f32": ident_f32, "ones_f32": ones_f32,
              "ones_bf": ones_bf, "trimask_bf": trimask}

    SCR_W = 16384
    scratch = A("scratch", [128, SCR_W], F32)
    scr_f32 = scratch.ap()
    scr_bf = scratch.ap().bitcast(BF16)
    PGW = 512

    def skeys(w0, w1):
        return [(id(scratch), pg) for pg in range(w0 // PGW, (w1 - 1) // PGW + 1)]

    class SV:
        def __init__(self, off_w, n_w, dtype):
            self.off_w, self.n_w, self.dtype = off_w, n_w, dtype
            self.keys = skeys(off_w, off_w + n_w)

        def ap(self, part=(0, 128)):
            if self.dtype == F32:
                a = scr_f32[:, self.off_w:self.off_w + self.n_w]
            else:
                a = scr_bf[:, 2 * self.off_w:2 * self.off_w + 2 * self.n_w]
            if part != (0, 128):
                a = scr_f32[part[0]:part[1], self.off_w:self.off_w + self.n_w] if self.dtype == F32 \
                    else scr_bf[part[0]:part[1], 2 * self.off_w:2 * self.off_w + 2 * self.n_w]
            return a

    def sv(off_w, n_w, dtype=F32):
        assert off_w + n_w <= SCR_W, (off_w, n_w)
        return SV(off_w, n_w, dtype)

    APm = nc.alloc_psum_tensor
    psA = APm("psA", [128, T], F32)
    psB = APm("psB", [128, T], F32)

    def lp(fn):
        def g(eng):
            with nc.allow_low_precision(reason="bf16 normalizer for broadcast matmul (blessed bf16 operand path)"):
                return fn(eng)
        return g

    def pk(pst, banks):
        return [(id(pst), b) for b in banks]

    def bank_of(col_lo, col_hi):
        return list(range(col_lo // 512, (col_hi - 1) // 512 + 1))

    rows = sv(0, 2 * T)
    m_b = sv(2 * T, T)
    rstd_b = sv(3 * T, T)
    logitsT = sv(4 * T, T)
    stg_a = sv(5 * T, T)
    stg_b = sv(6 * T, T)
    xbf_a = sv(7 * T, T // 2, BF16)
    xbf_b = sv(7 * T + T // 2, T // 2, BF16)

    class Arena:
        FULL = [(0, 4 * T), (5 * T, 8 * T)]
        LATE = [(5 * T, 8 * T)]

        def __init__(self, ranges="full"):
            self.RANGES = self.FULL if ranges == "full" else self.LATE
            self.ri = 0
            self.cur = self.RANGES[0][0]

        def sv(self, n_w, dtype=F32, align=16):
            self.cur = (self.cur + align - 1) // align * align
            while self.cur + n_w > self.RANGES[self.ri][1]:
                self.ri += 1
                assert self.ri < len(self.RANGES), f"arena overflow ({n_w} words)"
                self.cur = self.RANGES[self.ri][0]
            v = sv(self.cur, n_w, dtype)
            self.cur += n_w
            return v

    for nm, hh in consts.items():
        P.dma("sync", out=hh[:], in_=Hd[nm].ap(), reads=[Hd[nm]], writes=[hh], tag=f"ldc{nm}")

    xr = Hd["x"].ap().rearrange("(ti p) c -> ti p c", p=128)
    for ti in range(NCH):
        stgv = stg_a if ti % 2 == 0 else stg_b
        P.dma("sync", out=stgv.ap()[:, 0:C], in_=xr[ti],
              reads=[Hd["x"]], writes=[stgv.keys], tag=f"ldx{ti}")
        pst = psA if ti % 2 == 0 else psB
        for ci in range(NCT):
            bank = ci % 4
            col = bank * 512 + (ci // 4) * 128
            dst = pst[:, col:col + 128]
            P.op("tensor",
                 lambda eng, dst=dst, s=stgv, ci=ci: eng.transpose(
                     dst, s.ap()[:, ci * 128:(ci + 1) * 128], ident_f32[:]),
                 reads=[stgv.keys, ident_f32], writes=[pk(pst, [bank])], tag=f"tx{ti}_{ci}")
            P.op("vector",
                 lambda eng, dst=dst, ti=ti, ci=ci: eng.tensor_copy(
                     xT[:, ci, ti * 128:(ti + 1) * 128], dst),
                 reads=[pk(pst, [bank])], writes=[xT], tag=f"txe{ti}_{ci}")

    def layernorm(layer):
        m_row = rows.ap((0, 1))[:, 0:T]
        rstd_row = rows.ap((0, 1))[:, T:2 * T]
        s1, s2 = psA, psB
        for ci in range(NCT):
            sqv = stg_a if ci % 2 == 0 else stg_b
            P.op("scalar",
                 lambda eng, ci=ci, sqv=sqv: eng.square(sqv.ap(), xT[:, ci, :]),
                 reads=[xT], writes=[sqv.keys], tag=f"ln{layer}sq{ci}")
            for tb in range(4):
                P.op("tensor",
                     lambda eng, ci=ci, tb=tb: eng.matmul(
                         s1[0:1, tb * 512:(tb + 1) * 512], ones_f32[:, 0:1],
                         xT[:, ci, tb * 512:(tb + 1) * 512],
                         start=(ci == 0), stop=(ci == NCT - 1)),
                     reads=[xT, ones_f32], writes=[pk(s1, [tb])], tag=f"ln{layer}s1_{ci}_{tb}")
                P.op("tensor",
                     lambda eng, ci=ci, tb=tb, sqv=sqv: eng.matmul(
                         s2[0:1, tb * 512:(tb + 1) * 512], ones_f32[:, 0:1],
                         sqv.ap()[:, tb * 512:(tb + 1) * 512],
                         start=(ci == 0), stop=(ci == NCT - 1)),
                     reads=[sqv.keys, ones_f32], writes=[pk(s2, [tb])], tag=f"ln{layer}s2_{ci}_{tb}")
        var_row = stg_a.ap((0, 1))[:, 0:T]
        tmp_row = stg_b.ap((0, 1))[:, 0:T]
        P.op("vector", lambda eng: eng.tensor_scalar_mul(m_row, s1[0:1, :], 1.0 / C),
             reads=[pk(s1, range(4))], writes=[rows.keys], tag=f"ln{layer}m")
        P.op("vector", lambda eng: eng.tensor_scalar_mul(var_row, s2[0:1, :], 1.0 / C),
             reads=[pk(s2, range(4))], writes=[stg_a.keys], tag=f"ln{layer}ms")
        P.op("vector", lambda eng: eng.scalar_tensor_tensor(
            tmp_row, m_row, -1.0, m_row, ALU.mult, ALU.mult),
             reads=[rows.keys], writes=[stg_b.keys], tag=f"ln{layer}negm2")
        P.op("vector", lambda eng: eng.tensor_add(var_row, var_row, tmp_row),
             reads=[stg_a.keys, stg_b.keys], writes=[stg_a.keys], tag=f"ln{layer}var")
        P.op("vector", lambda eng: eng.tensor_scalar_add(var_row, var_row, 1e-5),
             reads=[stg_a.keys], writes=[stg_a.keys], tag=f"ln{layer}eps")
        P.op("scalar", lambda eng: eng.activation(rstd_row, var_row, AF.Ln),
             reads=[stg_a.keys], writes=[rows.keys], tag=f"ln{layer}ln")
        P.op("scalar", lambda eng: eng.activation(rstd_row, rstd_row, AF.Exp, scale=-0.5),
             reads=[rows.keys], writes=[rows.keys], tag=f"ln{layer}ex")
        P.op("vector", lambda eng: eng.tensor_mul(tmp_row, rstd_row, rstd_row),
             reads=[rows.keys], writes=[stg_b.keys], tag=f"ln{layer}n1")
        P.op("vector", lambda eng: eng.tensor_mul(tmp_row, tmp_row, var_row),
             reads=[stg_a.keys, stg_b.keys], writes=[stg_b.keys], tag=f"ln{layer}n2")
        P.op("vector", lambda eng: eng.tensor_scalar(
            tmp_row, tmp_row, -0.5, 1.5, ALU.mult, ALU.add),
             reads=[stg_b.keys], writes=[stg_b.keys], tag=f"ln{layer}n3")
        P.op("vector", lambda eng: eng.tensor_mul(rstd_row, rstd_row, tmp_row),
             reads=[rows.keys, stg_b.keys], writes=[rows.keys], tag=f"ln{layer}n4")
        for tb in range(4):
            P.op("tensor",
                 lambda eng, tb=tb: eng.matmul(
                     s1[:, tb * 512:(tb + 1) * 512], ones_f32[0:1, :],
                     m_row[:, tb * 512:(tb + 1) * 512], start=True, stop=True),
                 reads=[rows.keys, ones_f32], writes=[pk(s1, [tb])], tag=f"ln{layer}bm{tb}")
            P.op("tensor",
                 lambda eng, tb=tb: eng.matmul(
                     s2[:, tb * 512:(tb + 1) * 512], ones_f32[0:1, :],
                     rstd_row[:, tb * 512:(tb + 1) * 512], start=True, stop=True),
                 reads=[rows.keys, ones_f32], writes=[pk(s2, [tb])], tag=f"ln{layer}br{tb}")
        P.op("vector", lambda eng: eng.tensor_copy(m_b.ap(), s1[:, :]),
             reads=[pk(s1, range(4))], writes=[m_b.keys], tag=f"ln{layer}me")
        P.op("vector", lambda eng: eng.tensor_copy(rstd_b.ap(), s2[:, :]),
             reads=[pk(s2, range(4))], writes=[rstd_b.keys], tag=f"ln{layer}re")
        for ci in range(NCT):
            P.op("vector",
                 lambda eng, ci=ci: eng.tensor_sub(stg_b.ap(), xT[:, ci, :], m_b.ap()),
                 reads=[xT, m_b.keys], writes=[stg_b.keys], tag=f"ln{layer}sb{ci}")
            P.op("vector",
                 lambda eng, ci=ci: eng.tensor_mul(xnT[:, ci, :], stg_b.ap(), rstd_b.ap()),
                 reads=[stg_b.keys, rstd_b.keys], writes=[xnT], tag=f"ln{layer}xn{ci}")

    def router_l0(ar):
        wr = ar.sv(8)
        for j in range(NCT):
            P.dma("sync", out=wr.ap()[:, j:j + 1],
                  in_=Hd["w_router"].ap()[j * 128:(j + 1) * 128].unsqueeze(1),
                  reads=[Hd["w_router"]], writes=[wr.keys], tag=f"ldwr{j}")
        for ci in range(NCT):
            for tb in range(4):
                P.op("tensor",
                     lambda eng, ci=ci, tb=tb: eng.matmul(
                         psA[0:1, tb * 512:(tb + 1) * 512], wr.ap()[:, ci:ci + 1],
                         xT[:, ci, tb * 512:(tb + 1) * 512],
                         start=(ci == 0), stop=(ci == NCT - 1)),
                     reads=[wr.keys, xT], writes=[pk(psA, [tb])], tag=f"rs0_{ci}_{tb}")
        for ci in range(NCT):
            P.op("tensor",
                 lambda eng, ci=ci: eng.matmul(
                     psB[0:1, 0:1], wr.ap()[:, ci:ci + 1], ones_f32[:, 0:1],
                     start=(ci == 0), stop=(ci == NCT - 1)),
                 reads=[wr.keys, ones_f32], writes=[pk(psB, [0])], tag=f"rsw{ci}")
        sumw_sb = ar.sv(1)
        P.op("vector", lambda eng: eng.tensor_copy(sumw_sb.ap((0, 1)), psB[0:1, 0:1]),
             reads=[pk(psB, [0])], writes=[sumw_sb.keys], tag="rswe")
        sc = logitsT.ap((0, 1))[:, 0:T]
        m_row = rows.ap((0, 1))[:, 0:T]
        rstd_row = rows.ap((0, 1))[:, T:2 * T]
        P.op("vector", lambda eng: eng.tensor_single_scalar(
            sc, m_row, sumw_sb.ap((0, 1)), ALU.mult),
             reads=[rows.keys, sumw_sb.keys], writes=[logitsT.keys], tag="rt1")
        P.op("vector", lambda eng: eng.tensor_sub(sc, psA[0:1, :], sc),
             reads=[pk(psA, range(4)), logitsT.keys], writes=[logitsT.keys], tag="rt2")
        P.op("vector", lambda eng: eng.tensor_mul(sc, sc, rstd_row),
             reads=[logitsT.keys, rows.keys], writes=[logitsT.keys], tag="rt3")

    def l0_ffn(ar):
        sc_row = logitsT.ap((0, 1))[:, 0:T]
        scr_d = nc.dram_tensor("scr_row_d", [T], F32, kind="Internal")
        plane = ar.sv(16)
        cnt = ar.sv(16)
        maskpl = ar.sv(16)
        S_b = ar.sv(T)
        M_b = ar.sv(T)
        fs1 = ar.sv(T)
        fs2 = ar.sv(T)
        fs3 = ar.sv(T)
        wblk = [ar.sv(512, BF16), ar.sv(512, BF16)]
        P.dma("scalar", out=scr_d.ap().unsqueeze(0), in_=sc_row,
              reads=[logitsT.keys], writes=[scr_d], tag="ffn_scout")
        P.dma("scalar", out=plane.ap()[:, 0:16],
              in_=scr_d.ap().rearrange("(p j) -> p j", j=16),
              reads=[scr_d], writes=[plane.keys], tag="ffn_plane")
        for tb in range(4):
            P.op("tensor", lambda eng, tb=tb: eng.matmul(
                psA[:, tb * 512:(tb + 1) * 512], ones_f32[0:1, :],
                sc_row[:, tb * 512:(tb + 1) * 512], start=True, stop=True),
                 reads=[logitsT.keys, ones_f32], writes=[pk(psA, [tb])], tag=f"ffn_sb{tb}")
        P.op("vector", lambda eng: eng.tensor_copy(S_b.ap(), psA[:, :]),
             reads=[pk(psA, range(4))], writes=[S_b.keys], tag="ffn_sbe")
        for j in range(16):
            P.op("vector", lambda eng, j=j: eng.tensor_scalar(
                fs1.ap(), S_b.ap(), plane.ap()[:, j:j + 1], 0.0, ALU.is_gt, ALU.add,
                accum_out=cnt.ap()[:, j:j + 1]),
                 reads=[S_b.keys, plane.keys], writes=[fs1.keys, cnt.keys], tag=f"ffn_cnt{j}")
        KCAP = int(T * 0.75)
        P.op("vector", lambda eng: eng.tensor_single_scalar(
            maskpl.ap()[:, 0:16], cnt.ap()[:, 0:16], float(KCAP), ALU.is_lt),
             reads=[cnt.keys], writes=[maskpl.keys], tag="ffn_mask")
        P.dma("scalar", out=scr_d.ap().rearrange("(p j) -> p j", j=16),
              in_=maskpl.ap()[:, 0:16], reads=[maskpl.keys], writes=[scr_d], tag="ffn_mkout")
        P.dma("scalar", out=fs2.ap((0, 1))[:, 0:T], in_=scr_d.ap().unsqueeze(0),
              reads=[scr_d], writes=[fs2.keys], tag="ffn_mkin")
        for tb in range(4):
            P.op("tensor", lambda eng, tb=tb: eng.matmul(
                psA[:, tb * 512:(tb + 1) * 512], ones_f32[0:1, :],
                fs2.ap((0, 1))[:, tb * 512:(tb + 1) * 512], start=True, stop=True),
                 reads=[fs2.keys, ones_f32], writes=[pk(psA, [tb])], tag=f"ffn_mb{tb}")
        P.op("vector", lambda eng: eng.tensor_copy(M_b.ap(), psA[:, :]),
             reads=[pk(psA, range(4))], writes=[M_b.keys], tag="ffn_mbe")
        wblk_r = Hd["w_block"].ap().rearrange("(ci p) (eo q) -> eo p ci q", p=128, q=128)
        for eo in range(NCT):
            wb = wblk[eo % 2]
            P.dma("gpsimd", out=wb.ap().rearrange("p (ci q) -> p ci q", q=128),
                  in_=wblk_r[eo], reads=[Hd["w_block"]], writes=[wb.keys], tag=f"ffn_w{eo}")
            ps = psA if eo % 2 == 0 else psB
            for ci in range(NCT):
                for tb in range(4):
                    P.op("tensor", lambda eng, ps=ps, ci=ci, tb=tb, wb=wb: eng.matmul(
                        ps[:, tb * 512:(tb + 1) * 512],
                        wb.ap().rearrange("p (ci q) -> p ci q", q=128)[:, ci, :],
                        xnT[:, ci, tb * 512:(tb + 1) * 512],
                        start=(ci == 0), stop=(ci == NCT - 1)),
                         reads=[wb.keys, xnT], writes=[pk(ps, [tb])], tag=f"ffn_p{eo}_{ci}_{tb}")
            P.op("scalar", lambda eng, ps=ps: eng.square(fs1.ap(), ps[:, :]),
                 reads=[pk(ps, range(4))], writes=[fs1.keys], tag=f"ffn_g1_{eo}")
            P.op("vector", lambda eng, ps=ps: eng.scalar_tensor_tensor(
                fs2.ap(), fs1.ap(), GC1 * GC2, ps[:, :], ALU.mult, ALU.mult),
                 reads=[fs1.keys, pk(ps, range(4))], writes=[fs2.keys], tag=f"ffn_g2_{eo}")
            P.op("vector", lambda eng, ps=ps: eng.scalar_tensor_tensor(
                fs1.ap(), ps[:, :], GC1, fs2.ap(), ALU.mult, ALU.add),
                 reads=[fs2.keys, pk(ps, range(4))], writes=[fs1.keys], tag=f"ffn_g3_{eo}")
            P.op("scalar", lambda eng: eng.activation(fs2.ap(), fs1.ap(), AF.Tanh),
                 reads=[fs1.keys], writes=[fs2.keys], tag=f"ffn_g4_{eo}")
            P.op("vector", lambda eng: eng.tensor_scalar(
                fs1.ap(), fs2.ap(), 0.5, 0.5, ALU.mult, ALU.add),
                 reads=[fs2.keys], writes=[fs1.keys], tag=f"ffn_g5_{eo}")
            P.op("vector", lambda eng, ps=ps: eng.tensor_mul(fs2.ap(), fs1.ap(), ps[:, :]),
                 reads=[fs1.keys, pk(ps, range(4))], writes=[fs2.keys], tag=f"ffn_g6_{eo}")
            P.op("vector", lambda eng, eo=eo: eng.tensor_sub(fs3.ap(), fs2.ap(), xnT[:, eo, :]),
                 reads=[fs2.keys, xnT], writes=[fs3.keys], tag=f"ffn_e1_{eo}")
            P.op("vector", lambda eng: eng.tensor_mul(fs3.ap(), fs3.ap(), M_b.ap()),
                 reads=[fs3.keys, M_b.keys], writes=[fs3.keys], tag=f"ffn_e2_{eo}")
            P.op("vector", lambda eng, eo=eo: eng.tensor_add(xT[:, eo, :], xT[:, eo, :], fs3.ap()),
                 reads=[fs3.keys, xT], writes=[xT], tag=f"ffn_e3_{eo}")
            P.op("vector", lambda eng, eo=eo: eng.tensor_add(xT[:, eo, :], xT[:, eo, :], xnT[:, eo, :]),
                 reads=[xnT, xT], writes=[xT], tag=f"ffn_e4_{eo}")

    def l0_attn(ar):
        qTp = ar.sv(T // 2, BF16)
        kTp = ar.sv(T // 2, BF16)
        vstg = ar.sv(T)
        vnat = ar.sv(T // 2, BF16)
        qp_sb = ar.sv(T, BF16)
        kpn_sb = ar.sv(T // 2, BF16)
        kp_sb = ar.sv(T, BF16)
        _wq = ar.sv(512, BF16); _wk = ar.sv(512, BF16); _wv = ar.sv(512, BF16)
        wq = [_wq, _wq]
        wk = [_wk, _wk]
        wv = [_wv, _wv]
        projs = [ar.sv(32, BF16), ar.sv(32, BF16)]
        bq24 = ar.sv(24)
        aw_sb = ar.sv(128, BF16)
        kv32 = ar.sv(128)
        kvbf = ar.sv(64, BF16)
        kst32 = ar.sv(2)
        kstbf = ar.sv(2, BF16)
        nr_sb = ar.sv(512)
        bc_sb = ar.sv(512)
        kst_sb = ar.sv(32)
        _wo = ar.sv(512, BF16)
        wout = [_wo, _wo]
        bo8 = ar.sv(8)
        tout = vstg

        for j in range(24):
            P.dma("sync", out=bq24.ap()[:, j:j + 1],
                  in_=Hd["bqkv"].ap()[j * 128:(j + 1) * 128].unsqueeze(1),
                  reads=[Hd["bqkv"]], writes=[bq24.keys], tag=f"l0a_bq{j}")
        for j in range(8):
            P.dma("sync", out=bo8.ap()[:, j:j + 1],
                  in_=Hd["bout"].ap()[j * 128:(j + 1) * 128].unsqueeze(1),
                  reads=[Hd["bout"]], writes=[bo8.keys], tag=f"l0a_bo{j}")
        wr_all = Hd["WqkvT"].ap().rearrange("(ci p) e -> p ci e", p=128)

        def wtile(sv_):
            return sv_.ap().rearrange("p (ci q) -> p ci q", q=128)

        for hp in range(NP):
            bi = hp % 2
            for mat, wsv, col0 in (("q", wq[bi], hp * 128), ("k", wk[bi], C + hp * 128),
                                   ("v", wv[bi], 2 * C + hp * 128)):
                P.dma("gpsimd", out=wtile(wsv), in_=wr_all[:, :, col0:col0 + 128],
                      reads=[Hd["WqkvT"]], writes=[wsv.keys], tag=f"l0a_w{mat}{hp}")
            P.dma("gpsimd", out=projs[bi].ap(), in_=Hd["proj_pair"].ap()[hp],
                  reads=[Hd["proj_pair"]], writes=[projs[bi].keys], tag=f"l0a_pj{hp}")
            for mat, wsv, ps, outsv, seg in (("q", wq[bi], psA, qTp, hp),
                                             ("k", wk[bi], psB, kTp, 8 + hp),
                                             ("v", wv[bi], psA, vstg, 16 + hp)):
                for ci in range(NCT):
                    for tb in range(4):
                        P.op("tensor", lambda eng, ps=ps, wsv=wsv, ci=ci, tb=tb: eng.matmul(
                            ps[:, tb * 512:(tb + 1) * 512], wtile(wsv)[:, ci, :],
                            xnT[:, ci, tb * 512:(tb + 1) * 512],
                            start=(ci == 0), stop=(ci == NCT - 1)),
                             reads=[wsv.keys, xnT], writes=[pk(ps, [tb])],
                             tag=f"l0a_{mat}{hp}_{ci}_{tb}")
                P.op("scalar", lambda eng, ps=ps, outsv=outsv, seg=seg: eng.activation(
                    outsv.ap(), ps[:, :], AF.Identity, bias=bq24.ap()[:, seg:seg + 1]),
                     reads=[pk(ps, range(4)), bq24.keys], writes=[outsv.keys],
                     tag=f"l0a_{mat}e{hp}")
            for ti in range(16):
                bank = ti % 4
                col = bank * 512 + ((ti // 4) % 4) * 128
                P.op("tensor", lambda eng, ti=ti, col=col: eng.transpose(
                    psB[:, col:col + 128], vstg.ap()[:, ti * 128:(ti + 1) * 128], ident_f32[:]),
                     reads=[vstg.keys, ident_f32], writes=[pk(psB, [bank])], tag=f"l0a_vt{hp}_{ti}")
                P.op("vector", lambda eng, ti=ti, col=col: eng.tensor_copy(
                    vnat.ap().rearrange("p (a b) -> p a b", b=128)[:, ti, :],
                    psB[:, col:col + 128]),
                     reads=[pk(psB, [ti % 4])], writes=[vnat.keys], tag=f"l0a_vte{hp}_{ti}")
            for nm, src, dst in (("qp", qTp, qp_sb), ("kp", kTp, kp_sb)):
                for h2 in range(2):
                    ps = psA if h2 == 0 else psB
                    r0, r1 = h2 * 64, h2 * 64 + 64
                    for tb in range(4):
                        P.op("tensor", lambda eng, ps=ps, src=src, r0=r0, r1=r1, tb=tb, pj=projs[bi]: eng.matmul(
                            ps[0:64, tb * 512:(tb + 1) * 512], pj.ap()[r0:r1, :],
                            src.ap()[r0:r1, tb * 512:(tb + 1) * 512], start=True, stop=True),
                             reads=[projs[bi].keys, src.keys], writes=[pk(ps, [tb])],
                             tag=f"l0a_{nm}{hp}_{h2}_{tb}")
                for h2 in range(2):
                    ps = psA if h2 == 0 else psB
                    P.op("scalar", lambda eng, ps=ps, dst=dst, h2=h2: eng.activation(
                        dst.ap((0, 64))[:, h2 * 2048:(h2 + 1) * 2048], ps[0:64, :], AF.Relu),
                         reads=[pk(ps, range(4))], writes=[dst.keys], tag=f"l0a_{nm}e{hp}_{h2}")
            for h2 in range(2):
                r0, r1 = h2 * 64, h2 * 64 + 64
                for ch in range(NCH):
                    col = h2 * 1024 + ch * 64
                    bank = col // 512
                    P.op("tensor", lambda eng, r0=r0, r1=r1, ch=ch, col=col, pj=projs[bi]: eng.matmul(
                        psA[:, col:col + 64], kTp.ap()[r0:r1, ch * 128:(ch + 1) * 128],
                        pj.ap()[r0:r1, :], start=True, stop=True),
                         reads=[kTp.keys, projs[bi].keys], writes=[pk(psA, [bank])],
                         tag=f"l0a_kpn{hp}_{h2}_{ch}")
            P.op("scalar", lambda eng: eng.activation(kpn_sb.ap(), psA[:, :], AF.Relu),
                 reads=[pk(psA, range(4))], writes=[kpn_sb.keys], tag=f"l0a_kpne{hp}")
            aw_all = sv(qTp.off_w, 2048, BF16)
            kvch_sb = vstg
            qp_r = qp_sb.ap((0, 64))
            kp_r = kp_sb.ap((0, 64))
            kpn_r = kpn_sb.ap()
            vnat_r = vnat.ap().rearrange("p (a b) -> p a b", b=128)
            kvbf_r = kvbf.ap((0, 64))
            kstbf_r = kstbf.ap((0, 64))
            for ch in range(NCH):
                pb = ch % 2
                for h2 in range(2):
                    P.op("tensor", lambda eng, h2=h2, ch=ch, pb=pb: eng.matmul(
                        psB[:, pb * 512 + h2 * 128: pb * 512 + (h2 + 1) * 128],
                        kp_r[:, h2 * 2048 + ch * 128: h2 * 2048 + (ch + 1) * 128],
                        qp_r[:, h2 * 2048 + ch * 128: h2 * 2048 + (ch + 1) * 128],
                        start=True, stop=True),
                         reads=[kp_sb.keys, qp_sb.keys], writes=[pk(psB, [pb])],
                         tag=f"pf_aw{hp}_{ch}_{h2}")
                P.op("vector", lambda eng, ch=ch, pb=pb: eng.scalar_tensor_tensor(
                    aw_all.ap()[:, ch * 256:(ch + 1) * 256], psB[:, pb * 512:pb * 512 + 256],
                    1.0, trimask[:], ALU.mult, ALU.mult),
                     reads=[pk(psB, [pb]), trimask], writes=[aw_all.keys], tag=f"pf_awm{hp}_{ch}")
            for ch in range(NCH):
                bank = ch // 4
                col = bank * 512 + (ch % 4) * 128
                for h2 in range(2):
                    P.op("tensor", lambda eng, h2=h2, ch=ch, col=col: eng.matmul(
                        psB[0:64, col + h2 * 64: col + (h2 + 1) * 64],
                        kpn_r[:, h2 * 1024 + ch * 64: h2 * 1024 + (ch + 1) * 64],
                        vnat_r[:, ch, h2 * 64:(h2 + 1) * 64], start=True, stop=True),
                         reads=[kpn_sb.keys, vnat.keys], writes=[pk(psB, [bank])],
                         tag=f"pf_kv{hp}_{ch}_{h2}")
            P.op("vector", lambda eng: eng.tensor_copy(kvch_sb.ap((0, 64)), psB[0:64, :]),
                 reads=[pk(psB, range(4))], writes=[kvch_sb.keys], tag=f"pf_kve{hp}")
            for ch in range(NCH):
                for h2 in range(2):
                    P.op("tensor", lambda eng, h2=h2, ch=ch: eng.matmul(
                        psA[0:64, 1536 + ch * 2 + h2: 1537 + ch * 2 + h2],
                        kpn_r[:, h2 * 1024 + ch * 64: h2 * 1024 + (ch + 1) * 64],
                        ones_bf[:, 0:1], start=True, stop=True),
                         reads=[kpn_sb.keys, ones_bf], writes=[pk(psA, [3])],
                         tag=f"pf_ks{hp}_{ch}_{h2}")
            P.op("vector", lambda eng: eng.tensor_copy(kst_sb.ap((0, 64)), psA[0:64, 1536:1568]),
                 reads=[pk(psA, [3])], writes=[kst_sb.keys], tag=f"pf_kse{hp}")
            P.op("vector", lambda eng: eng.memset(kv32.ap((0, 64)), 0.0),
                 reads=[], writes=[kv32.keys], tag=f"l0a_kv0{hp}")
            P.op("vector", lambda eng: eng.memset(kst32.ap((0, 64)), 0.0),
                 reads=[], writes=[kst32.keys], tag=f"l0a_ks0{hp}")
            P.op("scalar", lambda eng: eng.copy(kvbf.ap((0, 64)), kv32.ap((0, 64))),
                 reads=[kv32.keys], writes=[kvbf.keys], tag=f"l0a_kvb0{hp}")
            P.op("scalar", lambda eng: eng.copy(kstbf.ap((0, 64))[:, 0:2], kst32.ap((0, 64))[:, 0:2]),
                 reads=[kst32.keys], writes=[kstbf.keys], tag=f"l0a_ksb0{hp}")
            for ch in range(NCH):
                pb = ch % 2
                cb0 = pb * 512
                nb0 = 1024 + pb * 512
                bb0 = pb * 512
                for h2 in range(2):
                    if ch > 0:
                        P.op("tensor", lambda eng, h2=h2, ch=ch, cb0=cb0: eng.matmul(
                            psA[0:64, cb0 + h2 * 128: cb0 + (h2 + 1) * 128],
                            kvbf_r[:, h2 * 64:(h2 + 1) * 64],
                            qp_r[:, h2 * 2048 + ch * 128: h2 * 2048 + (ch + 1) * 128],
                            start=True, stop=False),
                             reads=[kvbf.keys, qp_sb.keys], writes=[pk(psA, [cb0 // 512])],
                             tag=f"pf_cxh{hp}_{ch}_{h2}")
                    P.op("tensor", lambda eng, h2=h2, ch=ch, cb0=cb0: eng.matmul(
                        psA[0:64, cb0 + h2 * 128: cb0 + (h2 + 1) * 128],
                        vnat_r[:, ch, h2 * 64:(h2 + 1) * 64],
                        aw_all.ap()[:, ch * 256 + h2 * 128: ch * 256 + (h2 + 1) * 128],
                        start=(ch == 0), stop=True),
                         reads=[vnat.keys, aw_all.keys], writes=[pk(psA, [cb0 // 512])],
                         tag=f"pf_cxi{hp}_{ch}_{h2}")
                    if ch > 0:
                        P.op("tensor", lambda eng, h2=h2, ch=ch, nb0=nb0: eng.matmul(
                            psA[0:1, nb0 + h2 * 128: nb0 + (h2 + 1) * 128],
                            kstbf_r[:, h2:h2 + 1],
                            qp_r[:, h2 * 2048 + ch * 128: h2 * 2048 + (ch + 1) * 128],
                            start=True, stop=False),
                             reads=[kstbf.keys, qp_sb.keys], writes=[pk(psA, [nb0 // 512])],
                             tag=f"pf_nh{hp}_{ch}_{h2}")
                    P.op("tensor", lambda eng, h2=h2, ch=ch, nb0=nb0: eng.matmul(
                        psA[0:1, nb0 + h2 * 128: nb0 + (h2 + 1) * 128], ones_bf[:, 0:1],
                        aw_all.ap()[:, ch * 256 + h2 * 128: ch * 256 + (h2 + 1) * 128],
                        start=(ch == 0), stop=True),
                         reads=[ones_bf, aw_all.keys], writes=[pk(psA, [nb0 // 512])],
                         tag=f"pf_ni{hp}_{ch}_{h2}")
                if ch < NCH - 1:
                    P.op("vector", lambda eng, ch=ch: eng.tensor_add(
                        kv32.ap((0, 64)), kv32.ap((0, 64)),
                        kvch_sb.ap((0, 64))[:, ch * 128:(ch + 1) * 128]),
                         reads=[kvch_sb.keys, kv32.keys], writes=[kv32.keys], tag=f"pf_kva{hp}_{ch}")
                    P.op("vector", lambda eng, ch=ch: eng.tensor_add(
                        kst32.ap((0, 64))[:, 0:2], kst32.ap((0, 64))[:, 0:2],
                        kst_sb.ap((0, 64))[:, ch * 2:(ch + 1) * 2]),
                         reads=[kst_sb.keys, kst32.keys], writes=[kst32.keys], tag=f"pf_ksa{hp}_{ch}")
                    P.op("scalar", lambda eng: eng.copy(kvbf.ap((0, 64)), kv32.ap((0, 64))),
                         reads=[kv32.keys], writes=[kvbf.keys], tag=f"pf_kvb{hp}_{ch}")
                    P.op("scalar", lambda eng: eng.copy(kstbf_r[:, 0:2], kst32.ap((0, 64))[:, 0:2]),
                         reads=[kst32.keys], writes=[kstbf.keys], tag=f"pf_ksb{hp}_{ch}")
                nrv = nr_sb.ap((0, 1))[:, pb * 256:pb * 256 + 256]
                nrb = scr_bf[0:1, 2 * nr_sb.off_w + pb * 512: 2 * nr_sb.off_w + pb * 512 + 256]
                P.op("vector", lambda eng, nb0=nb0, nrv=nrv: eng.tensor_scalar_add(
                    nrv, psA[0:1, nb0:nb0 + 256], 1e-6),
                     reads=[pk(psA, [nb0 // 512])], writes=[nr_sb.keys], tag=f"pf_ne{hp}_{ch}")
                P.op("vector", lp(lambda eng, nrv=nrv, nrb=nrb: eng.reciprocal(nrb, nrv)),
                     reads=[nr_sb.keys], writes=[nr_sb.keys], tag=f"pf_nr{hp}_{ch}")
                P.op("tensor", lambda eng, bb0=bb0, nrb=nrb: eng.matmul(
                    psB[0:64, bb0 + 256:bb0 + 512], ones_bf[0:1, 0:64], nrb,
                    start=True, stop=True),
                     reads=[nr_sb.keys, ones_bf], writes=[pk(psB, [bb0 // 512])], tag=f"pf_nb{hp}_{ch}")
                P.op("vector", lambda eng, bb0=bb0, pb=pb: eng.tensor_copy(
                    bc_sb.ap((0, 64))[:, pb * 256:pb * 256 + 256], psB[0:64, bb0 + 256:bb0 + 512]),
                     reads=[pk(psB, [bb0 // 512])], writes=[bc_sb.keys], tag=f"pf_nbe{hp}_{ch}")
                P.op("vector", lambda eng, hp=hp, ch=ch, cb0=cb0, pb=pb: eng.tensor_mul(
                    big0[0:64, hp, ch * 128:(ch + 1) * 128], psA[0:64, cb0:cb0 + 128],
                    bc_sb.ap((0, 64))[:, pb * 256:pb * 256 + 128]),
                     reads=[pk(psA, [cb0 // 512]), bc_sb.keys], writes=[big0], tag=f"pf_o0{hp}_{ch}")
                P.op("vector", lambda eng, hp=hp, ch=ch, cb0=cb0, pb=pb: eng.tensor_mul(
                    big0[64:128, hp, ch * 128:(ch + 1) * 128], psA[0:64, cb0 + 128:cb0 + 256],
                    bc_sb.ap((0, 64))[:, pb * 256 + 128:pb * 256 + 256]),
                     reads=[pk(psA, [cb0 // 512]), bc_sb.keys], writes=[big0], tag=f"pf_o1{hp}_{ch}")
        wout_r = Hd["WoutT"].ap().rearrange("(ci p) (co q) -> co p ci q", p=128, q=128)
        for co in range(NCT):
            wb = wout[co % 2]
            P.dma("gpsimd", out=wtile(wb), in_=wout_r[co],
                  reads=[Hd["WoutT"]], writes=[wb.keys], tag=f"l0a_wo{co}")
            ps = psA if co % 2 == 0 else psB
            for ci in range(NCT):
                for tb in range(4):
                    P.op("tensor", lambda eng, ps=ps, wb=wb, ci=ci, tb=tb, co=co: eng.matmul(
                        ps[:, tb * 512:(tb + 1) * 512], wtile(wb)[:, ci, :],
                        big0[:, ci, tb * 512:(tb + 1) * 512],
                        start=(ci == 0), stop=(ci == NCT - 1)),
                         reads=[wb.keys, big0], writes=[pk(ps, [tb])], tag=f"l0a_o{co}_{ci}_{tb}")
            P.op("scalar", lambda eng, ps=ps, co=co: eng.activation(
                tout.ap(), ps[:, :], AF.Identity, bias=bo8.ap()[:, co:co + 1]),
                 reads=[pk(ps, range(4)), bo8.keys], writes=[tout.keys], tag=f"l0a_oe{co}")
            P.op("vector", lambda eng, co=co: eng.tensor_add(
                xT[:, co, :], xT[:, co, :], tout.ap()),
                 reads=[tout.keys, xT], writes=[xT], tag=f"l0a_oa{co}")

    def router_l1(ar):
        rw = ar.sv(128)
        sumw = ar.sv(16)
        t16 = ar.sv(T)
        P.dma("sync", out=rw.ap().rearrange("p (ci e) -> p ci e", e=16),
              in_=Hd["router_wT"].ap().rearrange("(ci p) e -> p ci e", p=128),
              reads=[Hd["router_wT"]], writes=[rw.keys], tag="r1_w")
        rw_r = rw.ap().rearrange("p (ci e) -> p ci e", e=16)
        for ci in range(NCT):
            for tb in range(4):
                P.op("tensor", lambda eng, ci=ci, tb=tb: eng.matmul(
                    psA[0:16, tb * 512:(tb + 1) * 512], rw_r[:, ci, :],
                    xT[:, ci, tb * 512:(tb + 1) * 512],
                    start=(ci == 0), stop=(ci == NCT - 1)),
                     reads=[rw.keys, xT], writes=[pk(psA, [tb])], tag=f"r1_s0_{ci}_{tb}")
            P.op("tensor", lambda eng, ci=ci: eng.matmul(
                psB[0:16, 0:1], rw_r[:, ci, :], ones_f32[:, 0:1],
                start=(ci == 0), stop=(ci == NCT - 1)),
                 reads=[rw.keys, ones_f32], writes=[pk(psB, [0])], tag=f"r1_sw{ci}")
        P.op("vector", lambda eng: eng.tensor_copy(sumw.ap((0, 16))[:, 0:1], psB[0:16, 0:1]),
             reads=[pk(psB, [0])], writes=[sumw.keys], tag="r1_swe")
        lg = logitsT.ap((0, 16))[:, 0:T]
        P.op("vector", lambda eng: eng.tensor_single_scalar(
            t16.ap((0, 16)), m_b.ap((0, 16)), sumw.ap((0, 16))[:, 0:1], ALU.mult),
             reads=[m_b.keys, sumw.keys], writes=[t16.keys], tag="r1_t1")
        P.op("vector", lambda eng: eng.tensor_sub(lg, psA[0:16, :], t16.ap((0, 16))),
             reads=[pk(psA, range(4)), t16.keys], writes=[logitsT.keys], tag="r1_t2")
        P.op("vector", lambda eng: eng.tensor_mul(lg, lg, rstd_b.ap((0, 16))),
             reads=[logitsT.keys, rstd_b.keys], writes=[logitsT.keys], tag="r1_t3")

    def l1_attn(ar):
        kTp = ar.sv(T // 2, BF16)
        vstg = ar.sv(T)
        vnat = ar.sv(1088, BF16)
        expstk = [ar.sv(1984, BF16), ar.sv(1984, BF16)]
        attn_bf = [ar.sv(512, BF16), ar.sv(512, BF16)]
        t64 = ar.sv(1024)
        bc_sb = ar.sv(1024)
        wtl = [ar.sv(512, BF16), ar.sv(512, BF16), ar.sv(512, BF16)]
        bseg = ar.sv(24)
        bqs = ar.sv(8)
        stk_bf = vstg

        for nm, j0 in (("bq", 0), ("bk", 8), ("bv", 16)):
            for j in range(8):
                P.dma("sync", out=bseg.ap()[:, j0 + j:j0 + j + 1],
                      in_=Hd[nm].ap()[j * 128:(j + 1) * 128].unsqueeze(1),
                      reads=[Hd[nm]], writes=[bseg.keys], tag=f"l1a_{nm}{j}")
        P.op("vector", lambda eng: eng.tensor_scalar_mul(bqs.ap(), bseg.ap()[:, 0:8], 0.125),
             reads=[bseg.keys], writes=[bqs.keys], tag="l1a_bqs")

        def wtile(sv_):
            return sv_.ap().rearrange("p (ci q) -> p ci q", q=128)

        wq_r = Hd["Wq"].ap().rearrange("(ci p) (hp q) -> hp p ci q", p=128, q=128)
        wk_r = Hd["Wk"].ap().rearrange("(ci p) (hp q) -> hp p ci q", p=128, q=128)
        wv_r = Hd["Wv"].ap().rearrange("(ci p) (hp q) -> hp p ci q", p=128, q=128)
        for hp in range(NP):
            P.dma("gpsimd", out=wtile(wtl[0]), in_=wq_r[hp],
                  reads=[Hd["Wq"]], writes=[wtl[0].keys], tag=f"l1a_wq{hp}")
            ps = psA if hp % 2 == 0 else psB
            for ci in range(NCT):
                for tb in range(4):
                    P.op("tensor", lambda eng, ps=ps, ci=ci, tb=tb: eng.matmul(
                        ps[:, tb * 512:(tb + 1) * 512], wtile(wtl[0])[:, ci, :],
                        xnT[:, ci, tb * 512:(tb + 1) * 512],
                        start=(ci == 0), stop=(ci == NCT - 1)),
                         reads=[wtl[0].keys, xnT], writes=[pk(ps, [tb])],
                         tag=f"l1a_q{hp}_{ci}_{tb}")
            P.op("scalar", lambda eng, ps=ps, hp=hp: eng.activation(
                big0[:, hp, :], ps[:, :], AF.Identity,
                bias=bqs.ap()[:, hp:hp + 1], scale=0.125),
                 reads=[pk(ps, range(4)), bqs.keys], writes=[big0], tag=f"l1a_qe{hp}")
        for hp in range(NP):
            P.dma("gpsimd", out=wtile(wtl[1]), in_=wk_r[hp],
                  reads=[Hd["Wk"]], writes=[wtl[1].keys], tag=f"l1a_wk{hp}")
            P.dma("gpsimd", out=wtile(wtl[2]), in_=wv_r[hp],
                  reads=[Hd["Wv"]], writes=[wtl[2].keys], tag=f"l1a_wv{hp}")
            for ci in range(NCT):
                for tb in range(4):
                    P.op("tensor", lambda eng, ci=ci, tb=tb: eng.matmul(
                        psA[:, tb * 512:(tb + 1) * 512], wtile(wtl[1])[:, ci, :],
                        xnT[:, ci, tb * 512:(tb + 1) * 512],
                        start=(ci == 0), stop=(ci == NCT - 1)),
                         reads=[wtl[1].keys, xnT], writes=[pk(psA, [tb])],
                         tag=f"l1a_k{hp}_{ci}_{tb}")
            P.op("scalar", lambda eng, hp=hp: eng.activation(
                kTp.ap(), psA[:, :], AF.Identity, bias=bseg.ap()[:, 8 + hp:9 + hp]),
                 reads=[pk(psA, range(4)), bseg.keys], writes=[kTp.keys], tag=f"l1a_ke{hp}")
            for ci in range(NCT):
                for tb in range(4):
                    P.op("tensor", lambda eng, ci=ci, tb=tb: eng.matmul(
                        psB[:, tb * 512:(tb + 1) * 512], wtile(wtl[2])[:, ci, :],
                        xnT[:, ci, tb * 512:(tb + 1) * 512],
                        start=(ci == 0), stop=(ci == NCT - 1)),
                         reads=[wtl[2].keys, xnT], writes=[pk(psB, [tb])],
                         tag=f"l1a_v{hp}_{ci}_{tb}")
            P.op("scalar", lambda eng, hp=hp: eng.activation(
                vstg.ap(), psB[:, :], AF.Identity, bias=bseg.ap()[:, 16 + hp:17 + hp]),
                 reads=[pk(psB, range(4)), bseg.keys], writes=[vstg.keys], tag=f"l1a_ve{hp}")
            vnat_r = vnat.ap().rearrange("p (a b) -> p a b", b=136)
            for ti in range(16):
                bank = ti % 4
                col = bank * 512 + ((ti // 4) % 4) * 128
                P.op("tensor", lambda eng, ti=ti, col=col: eng.transpose(
                    psA[:, col:col + 128], vstg.ap()[:, ti * 128:(ti + 1) * 128], ident_f32[:]),
                     reads=[vstg.keys, ident_f32], writes=[pk(psA, [col // 512])],
                     tag=f"l1a_vt{hp}_{ti}")
                P.op("vector", lambda eng, ti=ti, col=col: eng.tensor_copy(
                    vnat_r[:, ti, 0:64], psA[:, col:col + 64]),
                     reads=[pk(psA, [col // 512])], writes=[vnat.keys], tag=f"l1a_va{hp}_{ti}")
                P.op("vector", lambda eng, ti=ti, col=col: eng.tensor_copy(
                    vnat_r[:, ti, 68:132], psA[:, col + 64:col + 128]),
                     reads=[pk(psA, [col // 512])], writes=[vnat.keys], tag=f"l1a_vb{hp}_{ti}")
            P.op("vector", lambda eng: eng.memset(vnat_r[:, :, 64:68], 1.0),
                 reads=[], writes=[vnat.keys], tag=f"l1a_on1{hp}")
            P.op("vector", lambda eng: eng.memset(vnat_r[:, :, 132:136], 1.0),
                 reads=[], writes=[vnat.keys], tag=f"l1a_on2{hp}")
            stks = []
            for h2 in range(2):
                h = 2 * hp + h2
                est = expstk[h2]
                stk_view = stk_bf.ap().bitcast(BF16)[:, 0:NF]
                P.dma("gpsimd", out=stk_view, in_=Hd["rel_stack"].ap()[h],
                      reads=[Hd["rel_stack"]], writes=[stk_bf.keys], tag=f"l1a_stk{h}")
                P.op("scalar", lambda eng, est=est, stk_view=stk_view: eng.activation(
                    est.ap(), stk_view, AF.Exp),
                     reads=[stk_bf.keys], writes=[est.keys], tag=f"l1a_estk{h}")
            for qh in range(2):
                q0 = qh * 1024
                pend_av = None
                for kc in range(NCH):
                    for h2 in range(2):
                        h = 2 * hp + h2
                        r0, r1 = h2 * 64, h2 * 64 + 64
                        sp0 = h2 * 1024
                        for tb in range(2):
                            P.op("tensor", lambda eng, kc=kc, sp0=sp0, tb=tb, r0=r0, r1=r1, q0=q0, hp=hp: eng.matmul(
                                psA[:, sp0 + tb * 512: sp0 + (tb + 1) * 512],
                                kTp.ap()[r0:r1, kc * 128:(kc + 1) * 128],
                                big0[r0:r1, hp, q0 + tb * 512: q0 + (tb + 1) * 512],
                                start=True, stop=True),
                                 reads=[kTp.keys, big0], writes=[pk(psA, [sp0 // 512 + tb])],
                                 tag=f"l1a_s{h}_{qh}_{kc}_{tb}")
                    if pend_av is not None:
                        pend_av()
                        pend_av = None
                    for h2 in range(2):
                        h = 2 * hp + h2
                        sp0 = h2 * 1024
                        ab = attn_bf[h2]
                        P.op("scalar", lambda eng, ab=ab, sp0=sp0: eng.activation(
                            ab.ap(), psA[:, sp0:sp0 + 1024], AF.Exp),
                             reads=[pk(psA, [sp0 // 512, sp0 // 512 + 1])],
                             writes=[ab.keys], tag=f"l1a_ex{h}_{qh}_{kc}")
                    for h2 in range(2):
                        h = 2 * hp + h2
                        ab = attn_bf[h2]
                        est = expstk[h2]
                        f0 = 1920 - kc * 128 + q0
                        P.op("vector", lambda eng, ab=ab, est=est, f0=f0: eng.tensor_mul(
                            ab.ap(), ab.ap(), est.ap()[:, f0:f0 + 1024]),
                             reads=[ab.keys, est.keys], writes=[ab.keys],
                             tag=f"l1a_bm{h}_{qh}_{kc}")
                    def mk_av(kc=kc, qh=qh, hp=hp):
                        def emit_av():
                            for h2 in range(2):
                                h = 2 * hp + h2
                                ab = attn_bf[h2]
                                av0 = h2 * 1024
                                for tb in range(2):
                                    P.op("tensor", lambda eng, ab=ab, kc=kc, tb=tb, h2=h2, av0=av0: eng.matmul(
                                        psB[0:68, av0 + tb * 512: av0 + (tb + 1) * 512],
                                        vnat_r[:, kc, h2 * 68:(h2 + 1) * 68],
                                        ab.ap()[:, tb * 512:(tb + 1) * 512],
                                        start=(kc == 0), stop=(kc == NCH - 1)),
                                         reads=[vnat.keys, ab.keys], writes=[pk(psB, [av0 // 512 + tb])],
                                         tag=f"l1a_av{h}_{qh}_{kc}_{tb}")
                        return emit_av
                    pend_av = mk_av()
                pend_av()
                for h2 in range(2):
                    h = 2 * hp + h2
                    r0, r1 = h2 * 64, h2 * 64 + 64
                    av0 = h2 * 1024
                    sp0 = h2 * 1024
                    nrbf = scr_bf[0:1, 2 * t64.off_w:2 * t64.off_w + 1024]
                    P.op("vector", lp(lambda eng, av0=av0, nrbf=nrbf: eng.reciprocal(
                        nrbf, psB[64:65, av0:av0 + 1024])),
                         reads=[pk(psB, [av0 // 512, av0 // 512 + 1])], writes=[t64.keys],
                         tag=f"l1a_nr{h}_{qh}")
                    P.op("tensor", lambda eng, sp0=sp0, nrbf=nrbf: eng.matmul(
                        psA[0:64, sp0:sp0 + 512], ones_bf[0:1, 0:64], nrbf[:, 0:512],
                        start=True, stop=True),
                         reads=[t64.keys, ones_bf], writes=[pk(psA, [sp0 // 512])],
                         tag=f"l1a_b1{h}_{qh}")
                    P.op("tensor", lambda eng, sp0=sp0, nrbf=nrbf: eng.matmul(
                        psA[0:64, sp0 + 512:sp0 + 1024], ones_bf[0:1, 0:64],
                        nrbf[:, 512:1024], start=True, stop=True),
                         reads=[t64.keys, ones_bf], writes=[pk(psA, [sp0 // 512 + 1])],
                         tag=f"l1a_b2{h}_{qh}")
                    P.op("vector", lambda eng, sp0=sp0: eng.tensor_copy(
                        bc_sb.ap((0, 64)), psA[0:64, sp0:sp0 + 1024]),
                         reads=[pk(psA, [sp0 // 512, sp0 // 512 + 1])], writes=[bc_sb.keys],
                         tag=f"l1a_bce{h}_{qh}")
                    P.op("vector", lambda eng, r0=r0, r1=r1, av0=av0: eng.tensor_mul(
                        t64.ap((r0, r1)), psB[0:64, av0:av0 + 1024], bc_sb.ap((0, 64))),
                         reads=[pk(psB, [av0 // 512, av0 // 512 + 1]), bc_sb.keys],
                         writes=[t64.keys], tag=f"l1a_ap{h}_{qh}")
                    P.op("vector", lambda eng, r0=r0, r1=r1, hp=hp, q0=q0: eng.tensor_add(
                        xT[r0:r1, hp, q0:q0 + 1024], xT[r0:r1, hp, q0:q0 + 1024],
                        t64.ap((r0, r1))),
                         reads=[t64.keys, xT], writes=[xT], tag=f"l1a_xa{h}_{qh}")

    def l1_moe(ar):
        lgdram = nc.dram_tensor("moe_lg_d", [16, T], F32, kind="Internal")
        cmbdram = nc.dram_tensor("moe_cmb_d", [16, T], F32, kind="Internal")
        spl = ar.sv(256)
        sbp = ar.sv(256)
        msb = ar.sv(256)
        msbo = ar.sv(256)
        gs = ar.sv(64)
        gwk = ar.sv(64)
        gmx = ar.sv(16)
        gmask = ar.sv(64)
        selp = ar.sv(256)
        wsum = ar.sv(16)
        cb16 = ar.sv(16)
        t256 = ar.sv(256)
        cmb_bf = ar.sv(T // 2, BF16)
        sg_bf = ar.sv(T // 2, BF16)
        su_bf = ar.sv(T // 2, BF16)
        wg = [ar.sv(1024, BF16), ar.sv(1024, BF16)]
        wu = [ar.sv(1024, BF16), ar.sv(1024, BF16)]
        wdg = [ar.sv(1024, BF16) for _ in range(4)]
        wsg, wsu, wsd = wg[0], wu[0], wdg[0]
        cmbrow = sv(spl.off_w, T)

        lg = logitsT.ap((0, 16))[:, 0:T]
        P.dma("scalar", out=lgdram.ap(), in_=lg, reads=[logitsT.keys], writes=[lgdram], tag="moe_lgd")
        for e in range(E):
            P.dma("scalar", out=spl.ap()[:, e * 16:(e + 1) * 16],
                  in_=lgdram.ap()[e].rearrange("(p j) -> p j", j=16),
                  reads=[lgdram], writes=[spl.keys], tag=f"moe_pl{e}")
        P.op("scalar", lambda eng: eng.activation(spl.ap(), spl.ap(), AF.Sigmoid),
             reads=[spl.keys], writes=[spl.keys], tag="moe_sig")
        P.dma("sync", out=cb16.ap((0, 1))[:, 0:16], in_=Hd["corr_bias"].ap().unsqueeze(0),
              reads=[Hd["corr_bias"]], writes=[cb16.keys], tag="moe_cb")
        P.op("tensor", lambda eng: eng.matmul(
            psA[:, 0:16], ones_f32[0:1, :], cb16.ap((0, 1))[:, 0:16], start=True, stop=True),
             reads=[cb16.keys, ones_f32], writes=[pk(psA, [0])], tag="moe_cbb")
        P.op("vector", lambda eng: eng.tensor_copy(cb16.ap(), psA[:, 0:16]),
             reads=[pk(psA, [0])], writes=[cb16.keys], tag="moe_cbe")
        for e in range(E):
            P.op("vector", lambda eng, e=e: eng.tensor_single_scalar(
                sbp.ap()[:, e * 16:(e + 1) * 16], spl.ap()[:, e * 16:(e + 1) * 16],
                cb16.ap()[:, e:e + 1], ALU.add),
                 reads=[spl.keys, cb16.keys], writes=[sbp.keys], tag=f"moe_sb{e}")
        for g in range(4):
            e0 = g * 4
            blk = lambda i, e0=e0: sbp.ap()[:, (e0 + i) * 16:(e0 + i + 1) * 16]
            P.op("vector", lambda eng, blk=blk: eng.tensor_max(t256.ap()[:, 0:16], blk(0), blk(1)),
                 reads=[sbp.keys], writes=[t256.keys], tag=f"moe_g{g}a")
            P.op("vector", lambda eng, blk=blk: eng.tensor_max(t256.ap()[:, 16:32], blk(2), blk(3)),
                 reads=[sbp.keys], writes=[t256.keys], tag=f"moe_g{g}b")
            P.op("vector", lambda eng: eng.tensor_max(
                gmx.ap(), t256.ap()[:, 0:16], t256.ap()[:, 16:32]),
                 reads=[t256.keys], writes=[gmx.keys], tag=f"moe_g{g}m1")
            for i in range(4):
                P.op("vector", lambda eng, blk=blk, i=i: eng.tensor_tensor(
                    t256.ap()[:, 32 + i * 16:48 + i * 16], blk(i), gmx.ap(), ALU.is_lt),
                     reads=[sbp.keys, gmx.keys], writes=[t256.keys], tag=f"moe_g{g}lt{i}")
                P.op("vector", lambda eng, blk=blk, i=i: eng.tensor_mul(
                    t256.ap()[:, 96 + i * 16:112 + i * 16], blk(i),
                    t256.ap()[:, 32 + i * 16:48 + i * 16]),
                     reads=[sbp.keys, t256.keys], writes=[t256.keys], tag=f"moe_g{g}q{i}")
                P.op("vector", lambda eng, i=i: eng.tensor_scalar(
                    t256.ap()[:, 32 + i * 16:48 + i * 16],
                    t256.ap()[:, 32 + i * 16:48 + i * 16], 1e30, -1e30, ALU.mult, ALU.add),
                     reads=[t256.keys], writes=[t256.keys], tag=f"moe_g{g}p{i}")
                P.op("vector", lambda eng, i=i: eng.tensor_add(
                    t256.ap()[:, 96 + i * 16:112 + i * 16],
                    t256.ap()[:, 96 + i * 16:112 + i * 16],
                    t256.ap()[:, 32 + i * 16:48 + i * 16]),
                     reads=[t256.keys], writes=[t256.keys], tag=f"moe_g{g}r{i}")
            P.op("vector", lambda eng: eng.tensor_max(
                t256.ap()[:, 0:16], t256.ap()[:, 96:112], t256.ap()[:, 112:128]),
                 reads=[t256.keys], writes=[t256.keys], tag=f"moe_g{g}c")
            P.op("vector", lambda eng: eng.tensor_max(
                t256.ap()[:, 16:32], t256.ap()[:, 128:144], t256.ap()[:, 144:160]),
                 reads=[t256.keys], writes=[t256.keys], tag=f"moe_g{g}d")
            P.op("vector", lambda eng: eng.tensor_max(
                t256.ap()[:, 0:16], t256.ap()[:, 0:16], t256.ap()[:, 16:32]),
                 reads=[t256.keys], writes=[t256.keys], tag=f"moe_g{g}m2")
            P.op("vector", lambda eng, g=g: eng.tensor_add(
                gs.ap()[:, g * 16:(g + 1) * 16], gmx.ap(), t256.ap()[:, 0:16]),
                 reads=[gmx.keys, t256.keys], writes=[gs.keys], tag=f"moe_g{g}s")
        for g in range(4):
            first = True
            for g2 in range(4):
                if g2 == g:
                    continue
                P.op("vector", lambda eng, g=g, g2=g2: eng.tensor_tensor(
                    t256.ap()[:, 0:16], gs.ap()[:, g2 * 16:(g2 + 1) * 16],
                    gs.ap()[:, g * 16:(g + 1) * 16], ALU.is_gt),
                     reads=[gs.keys], writes=[t256.keys], tag=f"moe_c{g}{g2}")
                if first:
                    P.op("vector", lambda eng, g=g: eng.tensor_copy(
                        gwk.ap()[:, g * 16:(g + 1) * 16], t256.ap()[:, 0:16]),
                         reads=[t256.keys], writes=[gwk.keys], tag=f"moe_ci{g}{g2}")
                    first = False
                else:
                    P.op("vector", lambda eng, g=g: eng.tensor_add(
                        gwk.ap()[:, g * 16:(g + 1) * 16], gwk.ap()[:, g * 16:(g + 1) * 16],
                        t256.ap()[:, 0:16]),
                         reads=[t256.keys, gwk.keys], writes=[gwk.keys], tag=f"moe_ca{g}{g2}")
            P.op("vector", lambda eng, g=g: eng.tensor_single_scalar(
                gmask.ap()[:, g * 16:(g + 1) * 16], gwk.ap()[:, g * 16:(g + 1) * 16],
                2.0, ALU.is_lt),
                 reads=[gwk.keys], writes=[gmask.keys], tag=f"moe_gm{g}")
        P.op("vector", lambda eng: eng.tensor_scalar(
            gwk.ap(), gmask.ap(), 1e30, -1e30, ALU.mult, ALU.add),
             reads=[gmask.keys], writes=[gwk.keys], tag="moe_gterm")
        for e in range(E):
            g = e // 4
            P.op("vector", lambda eng, e=e, g=g: eng.tensor_mul(
                msb.ap()[:, e * 16:(e + 1) * 16], sbp.ap()[:, e * 16:(e + 1) * 16],
                gmask.ap()[:, g * 16:(g + 1) * 16]),
                 reads=[sbp.keys, gmask.keys], writes=[msb.keys], tag=f"moe_m2{e}")
            P.op("vector", lambda eng, e=e, g=g: eng.tensor_add(
                msb.ap()[:, e * 16:(e + 1) * 16], msb.ap()[:, e * 16:(e + 1) * 16],
                gwk.ap()[:, g * 16:(g + 1) * 16]),
                 reads=[msb.keys, gwk.keys], writes=[msb.keys], tag=f"moe_m3{e}")
        P.op("vector", lambda eng: eng.tensor_copy(msbo.ap(), msb.ap()),
             reads=[msb.keys], writes=[msbo.keys], tag="moe_mo")
        def max_over_e(dst_keys, dst_ap, src_ap, src_keys, tagp):
            P.op("vector", lambda eng: eng.tensor_max(t256.ap()[:, 0:128],
                 src_ap[:, 0:128], src_ap[:, 128:256]),
                 reads=[src_keys], writes=[t256.keys], tag=tagp + "a")
            P.op("vector", lambda eng: eng.tensor_max(t256.ap()[:, 0:64],
                 t256.ap()[:, 0:64], t256.ap()[:, 64:128]),
                 reads=[t256.keys], writes=[t256.keys], tag=tagp + "b")
            P.op("vector", lambda eng: eng.tensor_max(t256.ap()[:, 0:32],
                 t256.ap()[:, 0:32], t256.ap()[:, 32:64]),
                 reads=[t256.keys], writes=[t256.keys], tag=tagp + "c")
            P.op("vector", lambda eng: eng.tensor_max(dst_ap,
                 t256.ap()[:, 0:16], t256.ap()[:, 16:32]),
                 reads=[t256.keys], writes=[dst_keys], tag=tagp + "d")
        for it in range(4):
            max_over_e(gmx.keys, gmx.ap(), msb.ap(), msb.keys, f"moe_t{it}")
            if it < 3:
                for e in range(E):
                    P.op("vector", lambda eng, e=e: eng.tensor_tensor(
                        t256.ap()[:, e * 16:(e + 1) * 16], msb.ap()[:, e * 16:(e + 1) * 16],
                        gmx.ap(), ALU.is_ge),
                         reads=[msb.keys, gmx.keys], writes=[t256.keys], tag=f"moe_x{it}_{e}")
                P.op("vector", lambda eng: eng.scalar_tensor_tensor(
                    msb.ap(), t256.ap(), -2e30, msb.ap(), ALU.mult, ALU.add),
                     reads=[t256.keys, msb.keys], writes=[msb.keys], tag=f"moe_y{it}")
        for e in range(E):
            P.op("vector", lambda eng, e=e: eng.tensor_tensor(
                selp.ap()[:, e * 16:(e + 1) * 16], msbo.ap()[:, e * 16:(e + 1) * 16],
                gmx.ap(), ALU.is_ge),
                 reads=[msbo.keys, gmx.keys], writes=[selp.keys], tag=f"moe_s{e}")
        P.op("vector", lambda eng: eng.tensor_mul(selp.ap(), selp.ap(), spl.ap()),
             reads=[selp.keys, spl.keys], writes=[selp.keys], tag="moe_w")
        P.op("vector", lambda eng: eng.tensor_add(t256.ap()[:, 0:128],
             selp.ap()[:, 0:128], selp.ap()[:, 128:256]),
             reads=[selp.keys], writes=[t256.keys], tag="moe_wsa")
        P.op("vector", lambda eng: eng.tensor_add(t256.ap()[:, 0:64],
             t256.ap()[:, 0:64], t256.ap()[:, 64:128]),
             reads=[t256.keys], writes=[t256.keys], tag="moe_wsb")
        P.op("vector", lambda eng: eng.tensor_add(t256.ap()[:, 0:32],
             t256.ap()[:, 0:32], t256.ap()[:, 32:64]),
             reads=[t256.keys], writes=[t256.keys], tag="moe_wsc")
        P.op("vector", lambda eng: eng.tensor_add(wsum.ap(),
             t256.ap()[:, 0:16], t256.ap()[:, 16:32]),
             reads=[t256.keys], writes=[wsum.keys], tag="moe_wsd")
        P.op("vector", lambda eng: eng.tensor_scalar_add(wsum.ap(), wsum.ap(), 1e-20),
             reads=[wsum.keys], writes=[wsum.keys], tag="moe_wse")
        P.op("vector", lambda eng: eng.reciprocal(wsum.ap(), wsum.ap()),
             reads=[wsum.keys], writes=[wsum.keys], tag="moe_wsi")
        for e in range(E):
            P.op("vector", lambda eng, e=e: eng.tensor_mul(
                selp.ap()[:, e * 16:(e + 1) * 16], selp.ap()[:, e * 16:(e + 1) * 16],
                wsum.ap()),
                 reads=[selp.keys, wsum.keys], writes=[selp.keys], tag=f"moe_cw{e}")
            P.dma("scalar", out=cmbdram.ap()[e].rearrange("(p j) -> p j", j=16),
                  in_=selp.ap()[:, e * 16:(e + 1) * 16],
                  reads=[selp.keys], writes=[cmbdram], tag=f"moe_cd{e}")
        if debug:
            for nm, svv, w_ in (("selp", selp, 256), ("gmask", gmask, 64), ("sbp", sbp, 256),
                                ("spl", spl, 256), ("gs", gs, 64), ("msbo", msbo, 256),
                                ("gmx", gmx, 16)):
                hh = nc.dram_tensor("dbg_" + nm, [128, w_], F32, kind="ExternalOutput")
                P.dma("sync", out=hh.ap(), in_=svv.ap()[:, 0:w_], reads=[svv.keys], writes=[], tag="dbg" + nm)
        sgw_r = Hd["sgwT"].ap().rearrange("(ci p) (hs q) -> hs p ci q", p=128, q=128)
        suw_r = Hd["suwT"].ap().rearrange("(ci p) (hs q) -> hs p ci q", p=128, q=128)
        sdw_r = Hd["sdwT"].ap().rearrange("(hs p) (ct q) -> ct p hs q", p=128, q=128)
        for half in range(2):
            for hs8 in range(8):
                hs = half * 8 + hs8
                P.dma("gpsimd", out=wsg.ap()[:, 0:1024].rearrange("p (ci q) -> p ci q", q=128),
                      in_=sgw_r[hs], reads=[Hd["sgwT"]], writes=[wsg.keys], tag=f"moe_sg{hs}")
                P.dma("gpsimd", out=wsu.ap()[:, 0:1024].rearrange("p (ci q) -> p ci q", q=128),
                      in_=suw_r[hs], reads=[Hd["suwT"]], writes=[wsu.keys], tag=f"moe_su{hs}")
                for ci in range(NCT):
                    for tb in range(4):
                        P.op("tensor", lambda eng, ci=ci, tb=tb: eng.matmul(
                            psA[:, tb * 512:(tb + 1) * 512],
                            wsg.ap()[:, 0:1024].rearrange("p (ci q) -> p ci q", q=128)[:, ci, :],
                            xnT[:, ci, tb * 512:(tb + 1) * 512],
                            start=(ci == 0), stop=(ci == NCT - 1)),
                             reads=[wsg.keys, xnT], writes=[pk(psA, [tb])], tag=f"moe_g{hs}_{ci}_{tb}")
                        P.op("tensor", lambda eng, ci=ci, tb=tb: eng.matmul(
                            psB[:, tb * 512:(tb + 1) * 512],
                            wsu.ap()[:, 0:1024].rearrange("p (ci q) -> p ci q", q=128)[:, ci, :],
                            xnT[:, ci, tb * 512:(tb + 1) * 512],
                            start=(ci == 0), stop=(ci == NCT - 1)),
                             reads=[wsu.keys, xnT], writes=[pk(psB, [tb])], tag=f"moe_u{hs}_{ci}_{tb}")
                P.op("scalar", lambda eng: eng.activation(sg_bf.ap(), psA[:, :], AF.Silu),
                     reads=[pk(psA, range(4))], writes=[sg_bf.keys], tag=f"moe_sge{hs}")
                P.op("scalar", lambda eng: eng.copy(su_bf.ap(), psB[:, :]),
                     reads=[pk(psB, range(4))], writes=[su_bf.keys], tag=f"moe_sue{hs}")
                P.op("vector", lambda eng, hs8=hs8: eng.tensor_mul(
                    big0[:, hs8, :], sg_bf.ap(), su_bf.ap()),
                     reads=[sg_bf.keys, su_bf.keys], writes=[big0], tag=f"moe_h{hs}")
            for ct in range(NCT):
                P.dma("gpsimd", out=wsd.ap()[:, 0:1024].rearrange("p (hs q) -> p hs q", q=128),
                      in_=sdw_r[ct][:, half * 8:(half + 1) * 8, :],
                      reads=[Hd["sdwT"]], writes=[wsd.keys], tag=f"moe_sd{half}_{ct}")
                ps = psA if ct % 2 == 0 else psB
                for hs8 in range(8):
                    for tb in range(4):
                        P.op("tensor", lambda eng, ps=ps, hs8=hs8, tb=tb: eng.matmul(
                            ps[:, tb * 512:(tb + 1) * 512],
                            wsd.ap()[:, 0:1024].rearrange("p (hs q) -> p hs q", q=128)[:, hs8, :],
                            big0[:, hs8, tb * 512:(tb + 1) * 512],
                            start=(hs8 == 0), stop=(hs8 == 7)),
                             reads=[wsd.keys, big0], writes=[pk(ps, [tb])],
                             tag=f"moe_d{half}_{ct}_{hs8}_{tb}")
                P.op("vector", lambda eng, ps=ps, ct=ct: eng.tensor_add(
                    xT[:, ct, :], xT[:, ct, :], ps[:, :]),
                     reads=[pk(ps, range(4)), xT], writes=[xT], tag=f"moe_da{half}_{ct}")
        if debug:
            for ci in range(NCT):
                hh = nc.dram_tensor(f"dbg_x17_{ci}", [128, T], F32, kind="ExternalOutput")
                P.dma("sync", out=hh.ap(), in_=xT[:, ci, :], reads=[xT], writes=[], tag=f"dbgx17_{ci}")
        gw_r = Hd["gate_w"].ap().rearrange("e (ci p) hh -> e p ci hh", p=128)
        uw_r = Hd["up_w"].ap().rearrange("e (ci p) hh -> e p ci hh", p=128)
        dw_r = Hd["down_w"].ap().rearrange("e (hc p) (ct q) -> e p hc ct q", p=128, q=128)
        for grp in range(4):
            for ei in range(4):
                e = grp * 4 + ei
                crbf = scr_bf[0:1, 2 * cmbrow.off_w:2 * cmbrow.off_w + T]
                P.dma("gpsimd", out=crbf, in_=cmbdram.ap()[e].unsqueeze(0),
                      reads=[cmbdram], writes=[cmbrow.keys], tag=f"moe_cr{e}")
                for tb in range(4):
                    P.op("tensor", lambda eng, tb=tb, crbf=crbf: eng.matmul(
                        psA[:, tb * 512:(tb + 1) * 512], ones_bf[0:1, :],
                        crbf[:, tb * 512:(tb + 1) * 512], start=True, stop=True),
                         reads=[cmbrow.keys, ones_bf], writes=[pk(psA, [tb])], tag=f"moe_cb{e}_{tb}")
                P.op("scalar", lambda eng: eng.copy(cmb_bf.ap(), psA[:, :]),
                     reads=[pk(psA, range(4))], writes=[cmb_bf.keys], tag=f"moe_cbe{e}")
                wge, wue = wg[e % 2], wu[e % 2]
                P.dma("gpsimd", out=wge.ap().rearrange("p (ci hh) -> p ci hh", hh=256),
                      in_=gw_r[e], reads=[Hd["gate_w"]], writes=[wge.keys], tag=f"moe_wg{e}")
                P.dma("gpsimd", out=wue.ap().rearrange("p (ci hh) -> p ci hh", hh=256),
                      in_=uw_r[e], reads=[Hd["up_w"]], writes=[wue.keys], tag=f"moe_wu{e}")
                for hc in range(2):
                    for ci in range(NCT):
                        for tb in range(4):
                            P.op("tensor", lambda eng, hc=hc, ci=ci, tb=tb, wge=wge: eng.matmul(
                                psA[:, tb * 512:(tb + 1) * 512],
                                wge.ap().rearrange("p (ci hh) -> p ci hh", hh=256)[:, ci, hc * 128:(hc + 1) * 128],
                                xnT[:, ci, tb * 512:(tb + 1) * 512],
                                start=(ci == 0), stop=(ci == NCT - 1)),
                                 reads=[wge.keys, xnT], writes=[pk(psA, [tb])],
                                 tag=f"moe_eg{e}_{hc}_{ci}_{tb}")
                            P.op("tensor", lambda eng, hc=hc, ci=ci, tb=tb, wue=wue: eng.matmul(
                                psB[:, tb * 512:(tb + 1) * 512],
                                wue.ap().rearrange("p (ci hh) -> p ci hh", hh=256)[:, ci, hc * 128:(hc + 1) * 128],
                                xnT[:, ci, tb * 512:(tb + 1) * 512],
                                start=(ci == 0), stop=(ci == NCT - 1)),
                                 reads=[wue.keys, xnT], writes=[pk(psB, [tb])],
                                 tag=f"moe_eu{e}_{hc}_{ci}_{tb}")
                    P.op("scalar", lambda eng: eng.activation(sg_bf.ap(), psA[:, :], AF.Silu),
                         reads=[pk(psA, range(4))], writes=[sg_bf.keys], tag=f"moe_ege{e}_{hc}")
                    P.op("scalar", lambda eng: eng.copy(su_bf.ap(), psB[:, :]),
                         reads=[pk(psB, range(4))], writes=[su_bf.keys], tag=f"moe_eue{e}_{hc}")
                    P.op("vector", lambda eng: eng.tensor_mul(sg_bf.ap(), sg_bf.ap(), su_bf.ap()),
                         reads=[sg_bf.keys, su_bf.keys], writes=[sg_bf.keys], tag=f"moe_eh{e}_{hc}")
                    P.op("vector", lambda eng, ei=ei, hc=hc: eng.tensor_mul(
                        big0[:, ei * 2 + hc, :], sg_bf.ap(), cmb_bf.ap()),
                         reads=[sg_bf.keys, cmb_bf.keys], writes=[big0], tag=f"moe_ehw{e}_{hc}")
            for ei in range(4):
                e = grp * 4 + ei
                P.dma("gpsimd", out=wdg[ei].ap().rearrange("p (hc ct q) -> p hc ct q", hc=2, q=128),
                      in_=dw_r[e], reads=[Hd["down_w"]], writes=[wdg[ei].keys],
                      tag=f"moe_wd{e}")
            for ct in range(NCT):
                ps = psA if ct % 2 == 0 else psB
                for ei in range(4):
                    e = grp * 4 + ei
                    for hc in range(2):
                        for tb in range(4):
                            P.op("tensor", lambda eng, ps=ps, ei=ei, e=e, hc=hc, ct=ct, tb=tb: eng.matmul(
                                ps[:, tb * 512:(tb + 1) * 512],
                                wdg[ei].ap().rearrange("p (hc ct q) -> p hc ct q", hc=2, q=128)[:, hc, ct, :],
                                big0[:, ei * 2 + hc, tb * 512:(tb + 1) * 512],
                                start=(ei == 0 and hc == 0), stop=(ei == 3 and hc == 1)),
                                 reads=[wdg[ei].keys, big0], writes=[pk(ps, [tb])],
                                 tag=f"moe_ed{e}_{ct}_{hc}_{tb}")
                P.op("vector", lambda eng, ps=ps, ct=ct: eng.tensor_add(
                    xT[:, ct, :], xT[:, ct, :], ps[:, :]),
                     reads=[pk(ps, range(4)), xT], writes=[xT], tag=f"moe_eda{grp}_{ct}")

    layernorm(0)
    if stage >= 1:
        ar0 = Arena("late")
        router_l0(ar0)
    if stage >= 2:
        ar_f = Arena()
        l0_ffn(ar_f)
    if stage >= 3:
        ar_a = Arena()
        l0_attn(ar_a)
    if stage >= 4 and debug:
        for ci in range(NCT):
            hh = nc.dram_tensor(f"dbg_x1_{ci}", [128, T], F32, kind="ExternalOutput")
            P.dma("sync", out=hh.ap(), in_=xT[:, ci, :], reads=[xT], writes=[], tag=f"dbgx1_{ci}")
    if stage >= 4:
        layernorm(1)
        ar1 = Arena("late")
        router_l1(ar1)
    if stage >= 5:
        ar_l1 = Arena()
        l1_attn(ar_l1)
    if stage >= 6 and debug:
        for ci in range(NCT):
            hh = nc.dram_tensor(f"dbg_x15_{ci}", [128, T], F32, kind="ExternalOutput")
            P.dma("sync", out=hh.ap(), in_=xT[:, ci, :], reads=[xT], writes=[], tag=f"dbgx15_{ci}")
    if stage >= 6:
        ar_m = Arena()
        l1_moe(ar_m)

    for ti in range(NCH):
        pst = psA if ti % 2 == 0 else psB
        stgv = stg_a if ti % 2 == 0 else stg_b
        for ci in range(NCT):
            bank = ci % 4
            col = bank * 512 + (ci // 4) * 128
            dst = pst[:, col:col + 128]
            P.op("tensor",
                 lambda eng, dst=dst, ti=ti, ci=ci: eng.transpose(
                     dst, xT[:, ci, ti * 128:(ti + 1) * 128], ident_f32[:]),
                 reads=[xT, ident_f32], writes=[pk(pst, [bank])], tag=f"oy{ti}_{ci}")
            P.op("vector",
                 lambda eng, dst=dst, ci=ci, stgv=stgv: eng.tensor_copy(
                     stgv.ap()[:, ci * 128:(ci + 1) * 128], dst),
                 reads=[pk(pst, [bank])], writes=[stgv.keys], tag=f"oye{ti}_{ci}")
        P.dma("sync", out=y.ap().rearrange("(ti p) c -> ti p c", p=128)[ti],
              in_=stgv.ap()[:, 0:C], reads=[stgv.keys], writes=[], tag=f"sty{ti}")

    dbg = {}
    if debug:
        def dump(name, src_ap, shape, dtype, readkeys):
            hh = nc.dram_tensor("dbg_" + name, list(shape), dtype, kind="ExternalOutput")
            P.dma("sync", out=hh.ap(), in_=src_ap, reads=[readkeys], writes=[], tag="dbg" + name)
            dbg[name] = hh
        for ci in range(NCT):
            hh = nc.dram_tensor(f"dbg_xnT{ci}", [128, T], BF16, kind="ExternalOutput")
            P.dma("sync", out=hh.ap(), in_=xnT[:, ci, :], reads=[xnT], writes=[], tag=f"dbgxn{ci}")
        dump("mrow", rows.ap((0, 1))[:, 0:T], (1, T), F32, rows.keys)
        dump("rstd", rows.ap((0, 1))[:, T:2 * T], (1, T), F32, rows.keys)
        if stage >= 1:
            dump("scoreL0", logitsT.ap((0, 1))[:, 0:T], (1, T), F32, logitsT.keys)
        if stage >= 4:
            dump("logits", logitsT.ap((0, 16))[:, 0:T], (16, T), F32, logitsT.keys)

    from contextlib import ExitStack
    stack = ExitStack()
    sems = {e: stack.enter_context(nc.semaphore(f"s_{e}")) for e in Prog.CENG}
    qsems = {q: [stack.enter_context(nc.semaphore(f"q_{q}{l}")) for l in range(Prog.NLANES)]
             for q in Prog.QUEUES}
    with nc.Block() as block:
        P.emit_all(block, sems, qsems)
    stack.close()
    return nc


def make_in_maps(inputs, n_cores=8):
    x = np.asarray(inputs["L0_attn_x"])
    base = prep_weights(inputs)
    base.update(prep_consts())
    in_maps = []
    for b in range(n_cores):
        m = dict(base)
        m["x"] = np.ascontiguousarray(x[b])
        in_maps.append(m)
    return in_maps


def kernel(**inputs):
    nc = build_program()
    in_maps = make_in_maps(inputs)
    res = run_bass_kernel_spmd(nc, in_maps, core_ids=list(range(8)))
    return np.stack([res.results[b]["y"] for b in range(8)], axis=0)
